# version 07
# baseline (speedup 1.0000x reference)
import numpy as np
import concourse.bass as bass
import concourse.mybir as mybir
from concourse.bass_utils import run_bass_kernel_spmd

F32 = mybir.dt.float32
BF16 = mybir.dt.bfloat16
I32 = mybir.dt.int32
ALU = mybir.AluOpType
AF = mybir.ActivationFunctionType
AX = mybir.AxisListType

D = 2048
DI = 4096
H = 64
NS = 64
DP = 8480
DFF = 8192
CBIG = 12582912.0
TWO_PI = float(2 * np.pi)
PI = float(np.pi)
OFF_B = 2 * DI
OFF_DT = 2 * DI + 2 * NS
OFF_TH = OFF_DT + 2 * H


class Emitter:
    ENGS = ("pe", "act", "dve", "pool", "sp")

    def __init__(self):
        self.steps = []
        self.writers = {}
        self.readers = {}
        self.counts = {e: 0 for e in self.ENGS}
        self.dma_counts = {}
        self.known = {e: {} for e in self.ENGS}

    def add(self, eng, fn, reads=(), writes=(), dma=None):
        need = {}
        for k in reads:
            w = self.writers.get(k)
            if w:
                need[w[0]] = max(need.get(w[0], 0), w[1])
        for k in writes:
            w = self.writers.get(k)
            if w:
                need[w[0]] = max(need.get(w[0], 0), w[1])
            for s, c in self.readers.get(k, ()):
                need[s] = max(need.get(s, 0), c)
        kn = self.known[eng]
        waits = []
        for s, c in sorted(need.items(), key=str):
            if kn.get(s, 0) < c:
                waits.append((s, c))
                kn[s] = c
        if dma is None:
            self.counts[eng] += 1
            sem, cnt, inc = "S_" + eng, self.counts[eng], 1
        else:
            key = writes[0] if writes else reads[0]
            dma = "D" + ("s" if eng == "pool" else "h") + repr(key)
            self.dma_counts[dma] = self.dma_counts.get(dma, 0) + 16
            sem, cnt, inc = dma, self.dma_counts[dma], 16
        self.steps.append((eng, fn, waits, sem, inc))
        for k in reads:
            self.readers.setdefault(k, []).append((sem, cnt))
        for k in writes:
            self.writers[k] = (sem, cnt)
            self.readers[k] = []


def build_program(t_loc=1024, ncores=8, dbg=()):
    CH = 128
    nch = t_loc // CH
    ntt = nch
    nk = 16
    TL1 = t_loc + 1
    ntg = t_loc // 512
    DF8 = DFF // 8
    DD8 = DFF // 8
    NQ = 4
    TQ = t_loc // NQ
    NSEG = max(ncores // 2, 1)
    assert DD8 <= D and DF8 <= DFF
    nc = bass.Bass("TRN2", target_bir_lowering=False)
    em = Emitter()
    A = em.add
    ctx = []

    def sbuf(name, shape, dtype):
        t = nc.sbuf_tensor(name, shape, dtype)
        ctx.append(t)
        return t.__enter__()

    def psumt(name, shape, dtype):
        t = nc.psum_tensor(name, shape, dtype)
        ctx.append(t)
        return t.__enter__()

    def din(name, shape, dt=F32, kind="ExternalInput"):
        return nc.dram_tensor(name, shape, dt, kind=kind).ap()

    def dit(name, shape, dt, shared=False):
        if shared:
            return nc.dram_tensor(name, shape, dt, addr_space="Shared").ap()
        return nc.dram_tensor(name, shape, dt).ap()

    x_sl = din("x_sl", [t_loc, D])
    x_pre = din("x_pre", [max(NSEG - 1, 1) * t_loc, D])
    x_ex = din("x_ex", [1, D])
    w_in = din("w_in", [D, DP])
    w_out = din("w_out", [DI, D])
    wg_hbm = din("wg_hbm", [DFF, D])
    wu_hbm = din("wu_hbm", [DFF, D])
    wd_hbm = din("wd_hbm", [D, DFF])
    a_log = din("a_log", [H, 1])
    dt_bias = din("dt_bias", [H, 1])
    bnb_in = din("bnb_in", [1, NS])
    cnb_in = din("cnb_in", [1, NS])
    identf_in = din("identf_in", [128, 128])
    triu_in = din("triu_in", [128, 128])
    maskr_in = din("maskr_in", [128, 128])
    bmask_in = din("bmask_in", [H, 1])
    out_hbm = din("out", [t_loc, D], F32, "ExternalOutput")

    xbf_dram = dit("xbf_dram", [t_loc, D], BF16)
    ahl_dram = dit("ahl_dram", [2 * nch, H * CH], BF16)
    z_dram = dit("z_dram", [t_loc, DI], BF16)
    xs_dram = dit("xs_dram", [t_loc, DI], BF16)
    prev_dram = dit("prev_dram", [nch * NS, DI], BF16)
    y_dram = dit("y_dram", [t_loc, DI], BF16)
    ln1_dram = dit("ln1_dram", [t_loc, D], F32)
    gu_dram = dit("gu_dram", [t_loc, DFF], F32)
    guq_dram = dit("guq_dram", [t_loc, DFF], BF16)
    wqt_dram = dit("wqt_dram", [max(DF8, D), max(D, DD8)], BF16)
    row_dram = dit("row_dram", [4, 256], F32)
    wqg_all = dit("wqg_all", [DFF // DF8, DF8 * D], BF16)
    wqu_all = dit("wqu_all", [DFF // DF8, DF8 * D], BF16)
    wqd_all = dit("wqd_all", [DFF // DD8, DD8 * D], BF16)

    dbg_out = {}
    for name, shape in dbg:
        dbg_out[name] = din("dbg_" + name, list(shape), F32, "ExternalOutput")

    identf = sbuf("identf", [64, 64], F32)
    identb = sbuf("identb", [128, 128], BF16)
    triu = sbuf("triu", [128, 128], BF16)
    maskr = sbuf("maskr", [128, 128], BF16)
    ones_rb = sbuf("ones_rb", [1, 128], BF16)
    ones_rf = sbuf("ones_rf", [1, 128], F32)
    ones_cf = sbuf("ones_cf", [128, 1], F32)
    aneg = sbuf("aneg", [H, 1], F32)
    alog_s = sbuf("alog_s", [H, 1], F32)
    dtb_s = sbuf("dtb_s", [H, 1], F32)
    bnb_m = sbuf("bnb_m", [128, NS], F32)
    cnb_m = sbuf("cnb_m", [128, NS], F32)
    bmask_s = sbuf("bmask_s", [H, 1], F32)
    AR = sbuf("AR", [128, 49152], BF16)
    ARf = AR[:, :].bitcast(F32)
    ARi = AR[:, :].bitcast(I32)
    _p2b = 33 * 512
    rr_f = ARf[0:32, _p2b:_p2b + t_loc]
    rr_i = ARi[0:32, _p2b + t_loc:_p2b + 2 * t_loc]
    rr_g = rr_f
    rr_m = ARf[0:32, _p2b + t_loc:_p2b + 2 * t_loc]
    cum_l = ARf[0:32, _p2b + 2 * t_loc:_p2b + 3 * t_loc]
    cs_bf = AR[0:32, 2 * (_p2b + 3 * t_loc):2 * (_p2b + 3 * t_loc) + 2 * t_loc]
    th_sg = ARf[0:32, _p2b + 4 * t_loc:_p2b + 4 * t_loc + 512]
    P2B_KEYS = [("AR", b) for b in range(33, 33 + (4 * t_loc + 512 + 511) // 512)]
    dtl_fm = ARf[:, _p2b + 4 * t_loc + 512:_p2b + 5 * t_loc + 513]
    wsm = AR[:, 46336:46336 + nk * 160]
    ev_tm = ARf[:, 41 * 512:41 * 512 + nch * H]
    qd_tm = AR[:, 43008:43008 + nch * H]
    ws_tm = AR[:, 43008 + 512:43008 + 512 + nch * H]
    c0_tm = AR[:, 44032:44032 + nch * H]
    aflat_h = AR[0:1, 0:H * CH]
    K_AFH = [("AR", b) for b in range((H * CH + 1023) // 1024)]
    aflat_l = AR[0:1, 8 * 1024:8 * 1024 + H * CH]
    K_AFL = [("AR", b) for b in range(8, 8 + (H * CH + 1023) // 1024)]
    x_st = sbuf("x_st", [128, 2 * D], F32)
    z_st = sbuf("z_st", [128, D], BF16)
    xs_st = sbuf("xs_st", [128, DI], BF16)
    y_st = sbuf("y_st", [128, DI], BF16)
    xs_db = sbuf("xs_db", [128, 2 * DI], BF16)
    xe_fm = sbuf("xe_fm", [128, 16], BF16)
    dt_loc = sbuf("dt_loc", [H, TL1], F32)
    lamS = sbuf("lamS", [H, TL1], F32)
    f1 = sbuf("f1", [H, TL1], F32)
    f2 = sbuf("f2", [H, TL1], F32)
    sc128 = f2[:, 0:128]
    wg_fm = sbuf("wg_fm", [H, t_loc], F32)
    u_fm = sbuf("u_fm", [H, t_loc], F32)
    aloc_fm = sbuf("aloc_fm", [H, t_loc], F32)
    ev_fm = sbuf("ev_fm", [H, t_loc], F32)
    ah_fm = sbuf("ah_fm", [H, t_loc], BF16)
    al_fm = sbuf("al_fm", [H, t_loc], BF16)
    vrow = sbuf("vrow", [1, 512], F32)
    carry = sbuf("carry", [32, 1], F32)
    cos_tm = sbuf("cos_tm", [128, nch * 32], BF16)
    sin_tm = sbuf("sin_tm", [128, nch * 32], BF16)
    bc_t = sbuf("bc_t", [128, nch * 128], F32)
    bcf = sbuf("bcf", [128, 2 * NS], F32)
    b_tmbA = sbuf("b_tmbA", [128, nch * NS], BF16)
    b_fmb = sbuf("b_fmb", [NS, t_loc], BF16)
    c_fmb = sbuf("c_fmb", [NS, t_loc], BF16)
    rms_sc = sbuf("rms_sc", [128, NS], F32)
    wqacc = rms_sc[:, 0:24]
    qsc = rms_sc[:, 24:28]
    runmax = rms_sc[:, 28:36]
    qrow = rms_sc[0:1, 40:48]
    wsraw = rms_sc[:, 36:39]
    rw1 = sbuf("rw1", [128, 1], F32)
    rw2 = sbuf("rw2", [128, 1], F32)
    sdd_tm = sbuf("sdd_tm", [128, nch], F32)
    xscal = sbuf("xscal", [128, 4 * 8], F32)
    pwf = sbuf("pwf", [128, 1024], F32)
    pwacc = sbuf("pwacc", [128, 25], F32)

    cd_col = sbuf("cd_col", [H, 1], F32)
    cd_row = sbuf("cd_row", [1, H], F32)
    e_bufs = sbuf("e_bufs", [128, 2 * CH], BF16)
    g_bufs = sbuf("g_bufs", [128, 2 * CH], BF16)
    s_bf = sbuf("s_bf", [128, CH], BF16)
    sc128b = s_bf
    u2_bf = sbuf("u2_bf", [128, 512], BF16)
    sc_a = u2_bf[:, :].bitcast(F32)[:, 0:64]
    sc_b = u2_bf[:, :].bitcast(F32)[:, 64:128]
    coef_s = sbuf("coef_s", [4, H], F32)

    def ark(lo, n):
        return [("AR", b) for b in range(lo // 1024, (lo + n + 1023) // 1024)]

    x_fm = AR[:, 0:nk * t_loc]
    K_XFM = [ark(j * t_loc, t_loc) for j in range(nk)]
    w_buf = AR[:, nk * t_loc:nk * t_loc + 2 * 512]
    K_WB = [ark(nk * t_loc + p * 512, 512) for p in range(2)]
    st_run = ARf[0:NS, 8704:12800]
    K_STR = ark(17408, 2 * DI)
    wx_bf = AR[:, 25600:29696]
    K_WX = ark(25600, DI)
    prev_bf = AR[0:NS, 29696:33792]
    K_PRV = ark(29696, DI)
    y_fm = AR[:, 0:32 * t_loc]
    K_YFM = [ark(j * t_loc, t_loc) for j in range(32)]
    wo_buf = AR[:, 32 * t_loc:32 * t_loc + 2 * 512]
    K_WO = [ark(32 * t_loc + p * 512, 512) for p in range(2)]
    xq_fm = AR[:, 0:nk * t_loc]
    K_XQ = K_XFM
    wff_buf = AR[:, nk * t_loc:nk * t_loc + 8 * 1024]
    K_WF = [ark(nk * t_loc + p * 4096, 4096) for p in range(2)]
    guq_fm = AR[:, 24 * 1024:24 * 1024 + (DFF // 128) * TQ]
    K_GQ = [ark(24 * 1024 + j * TQ, TQ) for j in range(DFF // 128)]

    PB = psumt("PB", [128, 2048], F32)
    PBb = PB[:, :].bitcast(BF16)
    PS4 = psumt("PS4", [128, 512], F32)
    PS5 = psumt("PS5", [128, 512], F32)
    PS6 = psumt("PS6", [128, 512], F32)
    PS7 = psumt("PS7", [128, 512], F32)
    PBK = [("PB", 0), ("PB", 1), ("PB", 2), ("PB", 3)]

    A("pool", lambda e: e.dma_start(out=identb[:], in_=identf_in), writes=["p0ch", "identb"], dma="Q_x")
    A("pool", lambda e: e.dma_start(out=triu[:], in_=triu_in), writes=["p0ch", "triu"], dma="Q_x")
    A("pool", lambda e: e.dma_start(out=maskr[:], in_=maskr_in), writes=["p0ch", "maskr"], dma="Q_x")
    A("sp", lambda e: e.dma_start(out=identf[:], in_=identf_in[0:64, 0:64]), writes=["p0chs", "identf"], dma="Q_s")
    for t, s, k in ((alog_s, a_log, "alog"), (dtb_s, dt_bias, "dtb"), (bmask_s, bmask_in, "bmask")):
        A("sp", lambda e, t=t, s=s: e.dma_start(out=t[:], in_=s), writes=["p0chs", k], dma="Q_s")
    A("dve", lambda e: e.memset(ones_rb[:], 1.0), writes=["ones_rb"])
    A("dve", lambda e: e.memset(ones_rf[:], 1.0), writes=["ones_rf"])
    A("dve", lambda e: e.memset(ones_cf[:], 1.0), writes=["ones_cf"])
    A("act", lambda e: e.activation(aneg[:], alog_s[:], AF.Exp), reads=["alog"], writes=["aneg"])
    A("dve", lambda e: e.tensor_scalar(aneg[:], aneg[:], -1.0, None, ALU.mult),
      reads=["aneg"], writes=["aneg"])
    for src, dstm, k in ((bnb_in, bnb_m, "bnb_m"), (cnb_in, cnb_m, "cnb_m")):
        A("sp", lambda e, s=src: e.dma_start(out=vrow[0:1, 0:NS], in_=s), reads=["vrow"],
          writes=["vrow"], dma="Q_s")
        A("pe", lambda e: e.matmul(PS4[:, 0:NS], ones_rf[:], vrow[0:1, 0:NS], start=True, stop=True),
          reads=["ones_rf", "vrow", "PS4"], writes=["PS4"])
        A("dve", lambda e, d=dstm: e.tensor_copy(d[:], PS4[:, 0:NS]), reads=["PS4"], writes=["PS4", k])

    for j in range(nk):
        A("pool", lambda e, j=j: e.dma_start(out=wsm[:, j * 160:j * 160 + 128],
                                             in_=w_in[j * 128:(j + 1) * 128, OFF_DT:OFF_DT + 128]),
          writes=[("wsmch", j % 2), ("wsm", j)], dma="Q_x")
        A("pool", lambda e, j=j: e.dma_start(out=wsm[:, j * 160 + 128:(j + 1) * 160],
                                             in_=w_in[j * 128:(j + 1) * 128, OFF_TH:OFF_TH + 32]),
          writes=[("wsmch", j % 2), ("wsm2", j)], dma="Q_x")

    def front(seg):
        own = seg == NSEG - 1
        if own:
            xsrc, xex_src = x_sl, x_ex
        elif seg == NSEG - 2:
            xsrc = x_pre[seg * t_loc:(seg + 1) * t_loc, :]
            xex_src = x_sl[0:1, :]
        else:
            xsrc = x_pre[seg * t_loc:(seg + 1) * t_loc, :]
            xex_src = x_pre[(seg + 1) * t_loc:(seg + 1) * t_loc + 1, :]
        A("pool", lambda e: e.dma_start(out=xbf_dram[:, :], in_=xsrc[:, :]), writes=["xbfd"], dma="Q_x")
        for j in range(nk):
            A("act", lambda e, j=j: e.dma_start_transpose(x_fm[:, j * t_loc:(j + 1) * t_loc],
                                                          xbf_dram[:, j * 128:(j + 1) * 128]),
              reads=["xbfd"] + K_XFM[j], writes=K_XFM[j], dma="Q_t")
        A("sp", lambda e: e.dma_start(out=x_st[0:1, 0:D], in_=xex_src), reads=[("x_st", 0)],
          writes=[("x_st", 0)], dma="Q_s")
        for j in range(16):
            A("pe", lambda e, j=j: e.matmul(PS4[:, j:j + 1], x_st[0:1, j * 128:(j + 1) * 128],
                                            ones_rf[0:1, 0:1], start=True, stop=True),
              reads=[("x_st", 0), "ones_rf"] + (["PS4"] if j == 0 else []),
              writes=["PS4"] if j == 15 else [])
        A("dve", lambda e: e.tensor_copy(xe_fm[:], PS4[:, 0:16]), reads=["PS4"], writes=["PS4", "xe_fm"])

        for g in range(ntg):
            for j in range(nk):
                A("pe", lambda e, g=g, j=j: e.matmul(PS7[0:128, 0:512], wsm[:, j * 160:j * 160 + 128],
                                                     x_fm[:, j * t_loc + g * 512:j * t_loc + (g + 1) * 512],
                                                     start=(j == 0), stop=(j == nk - 1)),
                  reads=K_XFM[j] + [("wsm", j)] + (["PS7"] if j == 0 else []),
                  writes=["PS7"] if j == nk - 1 else [])
            A("dve", lambda e, g=g: e.tensor_copy(dtl_fm[:, g * 512:(g + 1) * 512], PS7[0:128, 0:512]),
              reads=["PS7", "dtl"], writes=["PS7", "dtl"] + [("wst", cc) for cc in range(nch)])
        for j in range(nk):
            A("pe", lambda e, j=j: e.matmul(PS5[0:128, 0:1], wsm[:, j * 160:j * 160 + 128], xe_fm[:, j:j + 1],
                                            start=(j == 0), stop=(j == nk - 1)),
              reads=["xe_fm", ("wsm", j)] + (["PS5"] if j == 0 else []),
              writes=["PS5"] if j == nk - 1 else [])
        A("dve", lambda e: e.tensor_copy(dtl_fm[:, t_loc:TL1], PS5[0:128, 0:1]),
          reads=["PS5", "dtl"], writes=["PS5", "dtl"] + [("wst", cc) for cc in range(nch)])

        def psg(t):
            return PB[:, (t % 4) * 512:(t % 4 + 1) * 512] if t < 4 else (PS4, PS5, PS6, PS7)[t - 4][:, :]

        def psk(t):
            return ("PB", t) if t < 4 else ("PS4", "PS5", "PS6", "PS7")[t - 4]

        for g in (([16] + list(range(8, 16)) + list(range(8))) if own
                  else ([16] + list(range(8, 16)))):
            gw = 512 if g < 16 else 128
            goff = g * 512 if g < 16 else OFF_B
            for j in range(nk):
                A("pool", lambda e, g=g, j=j, gw=gw, goff=goff: e.dma_start(
                    out=w_buf[:, (j % 2) * 512:(j % 2) * 512 + gw],
                    in_=w_in[j * 128:(j + 1) * 128, goff:goff + gw]),
                  reads=K_WB[j % 2], writes=K_WB[j % 2] + [("w_l", j % 2)], dma="Q_x")
                for t in range(ntt):
                    A("pe", lambda e, g=g, j=j, t=t, gw=gw: e.matmul(
                        psg(t)[:, 0:gw], x_fm[:, j * t_loc + t * 128:j * t_loc + (t + 1) * 128],
                        w_buf[:, (j % 2) * 512:(j % 2) * 512 + gw],
                        start=(j == 0), stop=(j == nk - 1)),
                      reads=K_XFM[j] + K_WB[j % 2] + [("w_l", j % 2)] + ([psk(t)] if j == 0 else []),
                      writes=[psk(t)] if j == nk - 1 else [])
            for t in range(ntt):
                if g < 8:
                    A("dve", lambda e, g=g, t=t: e.tensor_copy(z_st[:, (t % 4) * 512:(t % 4 + 1) * 512],
                                                               psg(t)),
                      reads=[psk(t), ("zs", t % 4)], writes=[psk(t), ("zs", t % 4)])
                    A("act", lambda e, g=g, t=t: e.dma_start(
                        out=z_dram[t * 128:(t + 1) * 128, g * 512:(g + 1) * 512],
                        in_=z_st[:, (t % 4) * 512:(t % 4 + 1) * 512]),
                      reads=[("zs", t % 4)], writes=[("zch", t % 2), ("zs", t % 4), ("z_d", t)], dma="Q_t")
                elif g < 16:
                    A("act", lambda e, g=g, t=t: e.activation(xs_st[:, (t % 4) * 512:(t % 4 + 1) * 512],
                                                              psg(t), AF.Silu),
                      reads=[psk(t), ("xss", t % 4)], writes=[psk(t), ("xss", t % 4)])
                    A("act", lambda e, g=g, t=t: e.dma_start(
                        out=xs_dram[t * 128:(t + 1) * 128, (g - 8) * 512:(g - 7) * 512],
                        in_=xs_st[:, (t % 4) * 512:(t % 4 + 1) * 512]),
                      reads=[("xss", t % 4)], writes=[("xsch", t % 2), ("xss", t % 4), ("xs_d", t)], dma="Q_t")
                else:
                    A("dve", lambda e, t=t: e.tensor_copy(bc_t[:, t * 128:(t + 1) * 128],
                                                          psg(t)[:, 0:128]),
                      reads=[psk(t), "bc_t"], writes=[psk(t), "bc_t", ("bc", t)])

        A("dve", lambda e: e.tensor_scalar(f1[:], dtl_fm[0:64, :], dtb_s[:], None, ALU.add),
          reads=["dtl", "dtb", "f1"], writes=["f1"])
        A("act", lambda e: e.activation(f2[:], f1[:], AF.Abs), reads=["f1", "f2"], writes=["f2"])
        A("act", lambda e: e.activation(f2[:], f2[:], AF.Exp, scale=-1.0), reads=["f2"], writes=["f2"])
        A("act", lambda e: e.activation(f2[:], f2[:], AF.Ln, bias=1.0), reads=["f2"], writes=["f2"])
        A("dve", lambda e: e.tensor_scalar(f1[:], f1[:], 0.0, None, ALU.max), reads=["f1"], writes=["f1"])
        A("dve", lambda e: e.tensor_tensor(dt_loc[:], f1[:], f2[:], ALU.add),
          reads=["f1", "f2", "dt"], writes=["dt"])
        A("dve", lambda e: e.tensor_scalar(dt_loc[:], dt_loc[:], 0.5, 1e-4, ALU.min, ALU.max),
          reads=["dt"], writes=["dt"])
        A("act", lambda e: e.activation(lamS[:], dtl_fm[64:128, :], AF.Sigmoid),
          reads=["dtl", "lamS"], writes=["lamS"])
        A("dve", lambda e: e.tensor_tensor(wg_fm[:], dt_loc[:, 0:t_loc], lamS[:, 0:t_loc], ALU.mult),
          reads=["dt", "lamS", "wg"], writes=["wg"])
        A("dve", lambda e: e.tensor_tensor(lamS[:], dt_loc[:], lamS[:], ALU.mult),
          reads=["dt", "lamS"], writes=["lamS"])
        A("dve", lambda e: e.tensor_tensor(lamS[:], dt_loc[:], lamS[:], ALU.subtract),
          reads=["dt", "lamS"], writes=["lamS"])
        if own:
            A("dve", lambda e: e.tensor_scalar(lamS[:, t_loc:TL1], lamS[:, t_loc:TL1],
                                               bmask_s[:], None, ALU.mult),
              reads=["lamS", "bmask"], writes=["lamS"])
        A("dve", lambda e: e.tensor_tensor(u_fm[:], wg_fm[:], lamS[:, 1:TL1], ALU.add),
          reads=["wg", "lamS", "u"], writes=["u"])
        A("dve", lambda e: e.tensor_scalar(f1[:, 0:t_loc], dt_loc[:, 0:t_loc], aneg[:], None, ALU.mult),
          reads=["dt", "aneg", "f1"], writes=["f1"])
        for c in range(nch):
            A("dve", lambda e, c=c: e.tensor_tensor_scan(
                aloc_fm[:, c * CH:(c + 1) * CH], ones_cf[0:H, 0:1].broadcast_to((H, CH)),
                f1[:, c * CH:(c + 1) * CH], 0.0, ALU.mult, ALU.add),
              reads=["f1", "ones_cf", "aloc"], writes=["aloc"])
        A("act", lambda e: e.activation(ev_fm[:], u_fm[:], AF.Ln), reads=["u", "ev"], writes=["ev"])
        A("dve", lambda e: e.tensor_tensor(ev_fm[:], ev_fm[:], aloc_fm[:], ALU.subtract),
          reads=["ev", "aloc"], writes=["ev"])
        A("dve", lambda e: e.tensor_copy(ah_fm[:], aloc_fm[:]), reads=["aloc", "ah"], writes=["ah"])
        A("dve", lambda e: e.tensor_copy(f1[:, 0:t_loc], ah_fm[:]), reads=["ah", "f1"], writes=["f1"])
        A("dve", lambda e: e.tensor_tensor(f1[:, 0:t_loc], aloc_fm[:], f1[:, 0:t_loc], ALU.subtract),
          reads=["f1", "aloc"], writes=["f1"])
        A("dve", lambda e: e.tensor_copy(al_fm[:], f1[:, 0:t_loc]), reads=["f1", "al"], writes=["al"])
        if own:
            for c in range(nch):
                A("act", lambda e, c=c: e.dma_start(
                    out=ahl_dram[c:c + 1, :].rearrange("o (h t) -> (o h) t", h=H),
                    in_=ah_fm[:, c * CH:(c + 1) * CH]),
                  reads=["ah"], writes=["ahl_d"], dma="Q_t")
                A("act", lambda e, c=c: e.dma_start(
                    out=ahl_dram[nch + c:nch + c + 1, :].rearrange("o (h t) -> (o h) t", h=H),
                    in_=al_fm[:, c * CH:(c + 1) * CH]),
                  reads=["al"], writes=["ahl_d2"], dma="Q_t")

        for g in range(ntg):
            for j in range(nk):
                A("pe", lambda e, g=g, j=j: e.matmul(PS6[0:32, 0:512], wsm[:, j * 160 + 128:(j + 1) * 160],
                                                     x_fm[:, j * t_loc + g * 512:j * t_loc + (g + 1) * 512],
                                                     start=(j == 0), stop=(j == nk - 1)),
                  reads=K_XFM[j] + [("wsm2", j)] + (["PS6"] if j == 0 else []),
                  writes=["PS6"] if j == nk - 1 else [])
            A("dve", lambda e: e.tensor_copy(th_sg[:], PS6[0:32, 0:512]),
              reads=["PS6", "th_sg"], writes=["th_sg", "PS6"])
            A("pe", lambda e, g=g: e.matmul(PS5[0:1, 0:512], ones_cf[0:H, 0:1],
                                            dt_loc[:, g * 512:(g + 1) * 512], start=True, stop=True),
              reads=["dt", "ones_cf", "PS5"], writes=["PS5"])
            A("dve", lambda e: e.tensor_scalar(vrow[:], PS5[0:1, 0:512], 1.0 / H, None, ALU.mult),
              reads=["PS5", "vrow"], writes=["vrow", "PS5"])
            A("pe", lambda e: e.matmul(PS5[0:32, 0:512], ones_rf[0:1, 0:32], vrow[:], start=True, stop=True),
              reads=["vrow", "ones_rf", "PS5"], writes=["PS5"])
            A("dve", lambda e: e.scalar_tensor_tensor(th_sg[:], th_sg[:], -1.0, PS5[0:32, 0:512],
                                                      ALU.mult, ALU.mult),
              reads=["th_sg", "PS5"], writes=["th_sg", "PS5"])
            A("dve", lambda e, g=g: e.tensor_copy(cum_l[:, g * 512:(g + 1) * 512], th_sg[:]),
              reads=["th_sg", "cum"], writes=["cum"])
        if seg == 0:
            A("dve", lambda e: e.memset(carry[:], 0.0), reads=["carry"], writes=["carry"])
        for g in range(ntg):
            A("dve", lambda e, g=g: e.tensor_tensor_scan(
                cum_l[:, g * 512:(g + 1) * 512], ones_cf[0:32, 0:1].broadcast_to((32, 512)),
                cum_l[:, g * 512:(g + 1) * 512], carry[0:32, 0:1], ALU.mult, ALU.add),
              reads=["cum", "carry", "ones_cf"], writes=["cum"])
            A("dve", lambda e, g=g: e.tensor_copy(carry[:], cum_l[:, g * 512 + 511:g * 512 + 512]),
              reads=["cum", "carry"], writes=["carry"])
        for phase, key, off in ((0.0, "sin", t_loc), (0.25, "cos", 0)):
            A("dve", lambda e, ph=phase: e.tensor_scalar(rr_f[:], cum_l[:], 1.0 / TWO_PI, ph,
                                                         ALU.mult, ALU.add),
              reads=["cum", "rr_f"], writes=["rr_f"])
            A("dve", lambda e: e.tensor_copy(rr_i[:], rr_f[:]), reads=["rr_f", "rr_i"], writes=["rr_i"])
            A("dve", lambda e: e.tensor_copy(rr_g[:], rr_i[:]), reads=["rr_i", "rr_g"], writes=["rr_g"])
            A("dve", lambda e: e.scalar_tensor_tensor(rr_f[:], rr_g[:], -TWO_PI, cum_l[:],
                                                      ALU.mult, ALU.add),
              reads=["rr_g", "cum", "rr_f"], writes=["rr_f"])
            if phase == 0.25:
                A("dve", lambda e: e.tensor_scalar(rr_f[:], rr_f[:], PI / 2, None, ALU.add),
                  reads=["rr_f"], writes=["rr_f"])
            A("dve", lambda e: e.tensor_scalar(rr_m[:], rr_f[:], PI, None, ALU.is_gt),
              reads=["rr_f", "rr_i"], writes=["rr_i"])
            A("dve", lambda e: e.scalar_tensor_tensor(rr_f[:], rr_m[:], -TWO_PI, rr_f[:],
                                                      ALU.mult, ALU.add),
              reads=["rr_f", "rr_i"], writes=["rr_f"])
            A("dve", lambda e: e.tensor_scalar(rr_m[:], rr_f[:], -PI, None, ALU.is_lt),
              reads=["rr_f", "rr_i"], writes=["rr_i"])
            A("dve", lambda e: e.scalar_tensor_tensor(rr_f[:], rr_m[:], TWO_PI, rr_f[:],
                                                      ALU.mult, ALU.add),
              reads=["rr_f", "rr_i"], writes=["rr_f"])
            A("act", lambda e, off=off: e.activation(cs_bf[:, off:off + t_loc], rr_f[:], AF.Sin),
              reads=["rr_f"], writes=[key])
        for c in range(nch):
            A("pe", lambda e, c=c: e.transpose(PBb[:, 0:32], cs_bf[:, c * CH:(c + 1) * CH],
                                               identb[0:32, 0:32]),
              reads=["cos", "identb", ("PB", 0)], writes=[("PB", 0)])
            A("dve", lambda e, c=c: e.tensor_copy(cos_tm[:, c * 32:(c + 1) * 32], PBb[:, 0:32]),
              reads=[("PB", 0), "cos_tm"], writes=[("PB", 0), "cos_tm"])
            A("pe", lambda e, c=c: e.transpose(PBb[:, 1024:1056],
                                               cs_bf[:, t_loc + c * CH:t_loc + (c + 1) * CH],
                                               identb[0:32, 0:32]),
              reads=["sin", "identb", ("PB", 1)], writes=[("PB", 1)])
            A("dve", lambda e, c=c: e.tensor_copy(sin_tm[:, c * 32:(c + 1) * 32], PBb[:, 1024:1056]),
              reads=[("PB", 1), "sin_tm"], writes=[("PB", 1), "sin_tm"])

        A("dve", lambda e: e.engine_nop(),
          reads=["cum", "rr_f", "rr_i", "rr_g", "cos", "sin", "th_sg", "carry"],
          writes=P2B_KEYS + ["ev_tm", "qd_tm", "ws_tm"])

        for c in range(nch):
            for nm, off, nbm, nbk, dof in ((("B", 0, bnb_m, "bnb_m", 0),) +
                                           ((("C", NS, cnb_m, "cnb_m", NS),) if own else ())):
                dst = bcf[:, dof:dof + NS]
                sl = bc_t[:, c * 128 + off:c * 128 + off + NS]
                A("dve", lambda e, sl=sl: e.tensor_tensor(rms_sc[:], sl, sl, ALU.mult),
                  reads=[("bc", c), "rms_sc"], writes=["rms_sc"])
                A("dve", lambda e: e.tensor_reduce(rw1[:], rms_sc[:], AX.X, ALU.add),
                  reads=["rms_sc", "rw1"], writes=["rw1"])
                A("dve", lambda e: e.tensor_scalar(rw1[:], rw1[:], 1.0 / NS, 1e-6, ALU.mult, ALU.add),
                  reads=["rw1"], writes=["rw1"])
                A("dve", lambda e: e.reciprocal(rw2[:], rw1[:]), reads=["rw1", "rw2"], writes=["rw2"])
                A("act", lambda e: e.activation(rw2[:], rw2[:], AF.Sqrt), reads=["rw2"], writes=["rw2"])
                A("dve", lambda e, sl=sl: e.tensor_scalar(rms_sc[:], sl, rw2[:], None, ALU.mult),
                  reads=[("bc", c), "rw2", "rms_sc"], writes=["rms_sc"])
                A("dve", lambda e, nbm=nbm: e.tensor_tensor(rms_sc[:], rms_sc[:], nbm[:], ALU.mult),
                  reads=["rms_sc", nbk], writes=["rms_sc"])
                cslice = cos_tm[:, c * 32:(c + 1) * 32]
                sslice = sin_tm[:, c * 32:(c + 1) * 32]
                A("dve", lambda e, cs=cslice: e.tensor_tensor(sc_a[:, 0:32], rms_sc[:, 0:32], cs, ALU.mult),
                  reads=["rms_sc", "cos_tm", "scz"], writes=["scz"])
                A("dve", lambda e, ss=sslice: e.tensor_tensor(sc_a[:, 32:64], rms_sc[:, 32:64], ss, ALU.mult),
                  reads=["rms_sc", "sin_tm", "scz"], writes=["scz"])
                A("dve", lambda e, dst=dst: e.tensor_tensor(
                    dst[:, 0:32], sc_a[:, 0:32], sc_a[:, 32:64], ALU.subtract),
                  reads=["scz", "bcf"], writes=["bcf"])
                A("dve", lambda e, ss=sslice: e.tensor_tensor(sc_b[:, 0:32], rms_sc[:, 0:32], ss, ALU.mult),
                  reads=["rms_sc", "sin_tm", "scz"], writes=["scz"])
                A("dve", lambda e, cs=cslice: e.tensor_tensor(sc_b[:, 32:64], rms_sc[:, 32:64], cs, ALU.mult),
                  reads=["rms_sc", "cos_tm", "scz"], writes=["scz"])
                A("dve", lambda e, dst=dst: e.tensor_tensor(
                    dst[:, 32:NS], sc_b[:, 0:32], sc_b[:, 32:64], ALU.add),
                  reads=["scz", "bcf"], writes=["bcf"])
            A("dve", lambda e, c=c: e.tensor_copy(b_tmbA[:, c * NS:(c + 1) * NS], bcf[:, 0:NS]),
              reads=["bcf", "b_tmbA"], writes=["b_tmbA", ("Bb", c)])
            if not own:
                continue
            A("dve", lambda e, c=c: e.tensor_tensor(rms_sc[:], bcf[:, 0:NS], bcf[:, NS:2 * NS], ALU.mult),
              reads=["bcf", "rms_sc"], writes=["rms_sc"])
            A("dve", lambda e, c=c: e.tensor_reduce(sdd_tm[:, c:c + 1], rms_sc[:], AX.X, ALU.add),
              reads=["rms_sc", "sdd"], writes=["sdd"])
            A("pe", lambda e, c=c: e.transpose(PBb[0:64, 2048:2176], b_tmbA[:, c * NS:(c + 1) * NS],
                                               identb[:]),
              reads=[("Bb", c), "identb", ("PB", 2)], writes=[("PB", 2)])
            A("dve", lambda e, c=c: e.tensor_copy(b_fmb[:, c * CH:(c + 1) * CH], PBb[0:64, 2048:2176]),
              reads=[("PB", 2), "b_fmb"], writes=[("PB", 2), "b_fmb", ("bf", c)])
            A("dve", lambda e, c=c: e.tensor_copy(sc128b[:, 0:NS], bcf[:, NS:2 * NS]),
              reads=["bcf", "s_bf"], writes=["s_bf"])
            A("pe", lambda e, c=c: e.transpose(PBb[0:64, 3072:3200], sc128b[:, 0:NS], identb[:]),
              reads=["s_bf", "identb", ("PB", 3)], writes=[("PB", 3)])
            A("dve", lambda e, c=c: e.tensor_copy(c_fmb[:, c * CH:(c + 1) * CH], PBb[0:64, 3072:3200]),
              reads=[("PB", 3), "c_fmb"], writes=[("PB", 3), "c_fmb", ("cf", c)])

        for c in range(nch):
            A("act", lambda e, c=c: e.activation(sc128[:], ev_fm[:, c * CH:(c + 1) * CH], AF.Exp,
                                                 bias=aloc_fm[:, c * CH + CH - 1:c * CH + CH]),
              reads=["ev", "aloc", "f2"], writes=["f2"])
            A("dve", lambda e: e.tensor_copy(sc128b[0:H, 0:CH], sc128[:]),
              reads=["f2", "s_bf"], writes=["s_bf"])
            A("pe", lambda e, c=c: e.transpose(PBb[:, 3072:3136], sc128b[0:H, 0:CH], identb[0:H, 0:H]),
              reads=["s_bf", "identb", ("PB", 3)], writes=[("PB", 3)])
            A("dve", lambda e, c=c: e.tensor_copy(ws_tm[:, c * H:(c + 1) * H], PBb[:, 3072:3136]),
              reads=[("PB", 3), "ws_tm", "dtl"], writes=[("PB", 3), "ws_tm", ("wst", c)])
            if not own:
                continue
            A("pe", lambda e, c=c: e.transpose(PB[:, 512:576], ev_fm[:, c * CH:(c + 1) * CH],
                                               identf[0:64, 0:64]),
              reads=["ev", "identf", ("PB", 1)], writes=[("PB", 1)])
            A("dve", lambda e, c=c: e.tensor_copy(ev_tm[:, c * H:(c + 1) * H], PB[:, 512:576]),
              reads=[("PB", 1), "ev_tm"], writes=[("PB", 1), "ev_tm", ("evt", c)])
            A("act", lambda e, c=c: e.activation(sc128[:], aloc_fm[:, c * CH:(c + 1) * CH], AF.Exp),
              reads=["aloc", "f2"], writes=["f2"])
            A("dve", lambda e: e.tensor_copy(sc128b[0:H, 0:CH], sc128[:]),
              reads=["f2", "s_bf"], writes=["s_bf"])
            A("pe", lambda e, c=c: e.transpose(PBb[:, 2048:2112], sc128b[0:H, 0:CH], identb[0:H, 0:H]),
              reads=["s_bf", "identb", ("PB", 2)], writes=[("PB", 2)])
            A("dve", lambda e, c=c: e.tensor_copy(qd_tm[:, c * H:(c + 1) * H], PBb[:, 2048:2112]),
              reads=[("PB", 2), "qd_tm", "dtl"], writes=[("PB", 2), "qd_tm", ("qdt", c)])
            A("pe", lambda e, c=c: e.transpose(PB[:, 576:640], wg_fm[:, c * CH:(c + 1) * CH],
                                               identf[0:64, 0:64]),
              reads=["wg", "identf", ("PB", 1)], writes=[("PB", 1)])
            A("dve", lambda e, c=c: e.tensor_scalar(c0_tm[:, c * H:(c + 1) * H], PB[:, 576:640],
                                                    sdd_tm[:, c:c + 1], None, ALU.mult),
              reads=[("PB", 1), "sdd", "c0_tm", "dtl"], writes=[("PB", 1), "c0_tm", ("c0t", c)])

        if seg == 0:
            A("dve", lambda e: e.memset(st_run[:, :], 0.0), reads=K_STR, writes=K_STR)
        for c in range(nch):
            p = c % 2
            A("sp", lambda e, c=c, p=p: e.dma_start(out=xs_db[:, p * DI:(p + 1) * DI],
                                                    in_=xs_dram[c * 128:(c + 1) * 128, :]),
              reads=[("xs_d", c), ("xs_db", p)], writes=[("xs_db", p)], dma="Q_s")
            A("dve", lambda e, c=c, p=p: e.tensor_tensor(
                wx_bf[:, :].rearrange("p (h q) -> p h q", h=H),
                xs_db[:, p * DI:(p + 1) * DI].rearrange("p (h q) -> p h q", h=H),
                ws_tm[:, c * H:(c + 1) * H].unsqueeze(2).broadcast_to((128, H, NS)), ALU.mult),
              reads=[("xs_db", p), ("wst", c)] + K_WX, writes=K_WX)
            if own:
                A("dve", lambda e: e.tensor_copy(prev_bf[:, :], st_run[:, :]),
                  reads=K_STR + K_PRV, writes=K_PRV)
                A("act", lambda e, c=c: e.dma_start(out=prev_dram[c * NS:(c + 1) * NS, :],
                                                    in_=prev_bf[:, :]),
                  reads=K_PRV, writes=["prch", ("prevd", c)], dma="Q_t")
            A("act", lambda e, c=c: e.activation(cd_col[:], aloc_fm[:, c * CH + CH - 1:c * CH + CH], AF.Exp),
              reads=["aloc", "cd_col"], writes=["cd_col"])
            A("act", lambda e: e.dma_start(out=row_dram[1:2, 0:64]
                                           .rearrange("o (h u) -> (o h) u", h=64), in_=cd_col[:]),
              reads=["cd_col"], writes=["rowch", "row_d2"], dma="Q_t")
            A("act", lambda e: e.dma_start(out=cd_row[:], in_=row_dram[1:2, 0:64]),
              reads=["row_d2", "cd_row"], writes=["cd_row"], dma="Q_t")
            A("pe", lambda e: e.matmul(PS7[0:NS, 0:NS], ones_rf[0:1, 0:NS], cd_row[:], start=True, stop=True),
              reads=["cd_row", "ones_rf", "PS7"], writes=["PS7"])
            A("dve", lambda e: e.tensor_tensor(
                st_run[:, :].rearrange("n (h q) -> n h q", h=H),
                st_run[:, :].rearrange("n (h q) -> n h q", h=H),
                PS7[0:NS, 0:NS].unsqueeze(2).broadcast_to((NS, H, NS)), ALU.mult),
              reads=K_STR + ["PS7"] + (K_PRV if own else []), writes=K_STR + ["PS7"])
            for half in range(2):
                for i in range(4):
                    A("pe", lambda e, c=c, half=half, i=i: e.matmul(
                        PB[0:NS, i * 512:(i + 1) * 512], b_tmbA[:, c * NS:(c + 1) * NS],
                        wx_bf[:, half * 2048 + i * 512:half * 2048 + (i + 1) * 512],
                        start=True, stop=True),
                      reads=K_WX + [("Bb", c), ("PB", i)], writes=[("PB", i)])
                A("dve", lambda e, c=c, half=half: e.tensor_tensor(
                    st_run[:, half * 2048:(half + 1) * 2048], st_run[:, half * 2048:(half + 1) * 2048],
                    PB[0:NS, 0:2048], ALU.add),
                  reads=K_STR + PBK, writes=K_STR + PBK)


    def pw_stats(tiles, col0):
        for i, ap in enumerate(tiles):
            h = i % 2
            A("sp", lambda e, ap=ap, h=h: e.dma_start(out=pwf[:, h * 512:(h + 1) * 512], in_=ap),
              reads=[("pwf", h)], writes=[("pwf", h)], dma="Q_s")
            col = col0 + i % 8
            if i < 8:
                A("dve", lambda e, h=h, col=col: e.tensor_reduce(
                    pwacc[:, col:col + 1], pwf[:, h * 512:(h + 1) * 512],
                    AX.X, ALU.add, apply_absolute_value=True),
                  reads=[("pwf", h), "pwacc"], writes=["pwacc"])
            else:
                A("dve", lambda e, h=h: e.tensor_reduce(
                    pwacc[:, 24:25], pwf[:, h * 512:(h + 1) * 512], AX.X, ALU.add,
                    apply_absolute_value=True),
                  reads=[("pwf", h), "pwacc"], writes=["pwacc"])
                A("dve", lambda e, col=col: e.tensor_tensor(
                    pwacc[:, col:col + 1], pwacc[:, col:col + 1], pwacc[:, 24:25], ALU.add),
                  reads=["pwacc"], writes=["pwacc"])

    A("dve", lambda e: e.memset(pwacc[:], 0.0), reads=["pwacc"], writes=["pwacc"])

    def gu_tiles(hbm):
        return [hbm[t * 128:(t + 1) * 128, h * 512:(h + 1) * 512]
                for t in range(DFF // 128) for h in range(4)]

    wd_tiles = [wd_hbm[t * 128:(t + 1) * 128, q * 512:(q + 1) * 512]
                for t in range(D // 128) for q in range(DFF // 512)]
    stats_jobs = [lambda: pw_stats(gu_tiles(wg_hbm), 0),
                  lambda: pw_stats(gu_tiles(wu_hbm), 8),
                  lambda: pw_stats(wd_tiles, 16)]
    npre_seg = max(NSEG - 1, 1)
    done = 0
    for seg in range(NSEG):
        front(seg)
        if seg < NSEG - 1:
            upto = (seg + 1) * 3 // npre_seg
            for job in stats_jobs[done:upto]:
                job()
            done = upto
    for job in stats_jobs[done:]:
        job()

    for col, lo, n in ((0, 0, 8), (1, 8, 8), (2, 16, 8)):
        A("dve", lambda e, col=col, lo=lo, n=n: e.tensor_reduce(qsc[:, col:col + 1],
                                                                pwacc[:, lo:lo + n], AX.X, ALU.add),
          reads=["pwacc", "rms_sc"], writes=["rms_sc"])
    A("pe", lambda e: e.matmul(PS7[0:3, 0:1], qsc[:, 0:3], ones_cf[:, 0:1], start=True, stop=True),
      reads=["rms_sc", "ones_cf", "PS7"], writes=["PS7"])
    A("dve", lambda e: e.tensor_copy(coef_s[0:3, 0:1], PS7[0:3, 0:1]),
      reads=["PS7", "coef_s"], writes=["coef_s", "PS7"])
    A("act", lambda e: e.dma_start(out=row_dram[3:4, 0:3].rearrange("o (a u) -> (o a) u", a=3),
                                   in_=coef_s[0:3, 0:1]),
      reads=["coef_s"], writes=["rowch", "row_d4"], dma="Q_t")
    A("sp", lambda e: e.dma_start(out=qrow[0:1, 0:3], in_=row_dram[3:4, 0:3]),
      reads=["row_d4", "rms_sc"], writes=["rms_sc"], dma="Q_s")
    A("dve", lambda e: e.tensor_scalar(qrow[0:1, 0:4], qrow[0:1, 0:4], 1.0 / (DFF * D), 1e-5,
                                       ALU.mult, ALU.add),
      reads=["rms_sc"], writes=["rms_sc"])
    A("dve", lambda e: e.reciprocal(qrow[0:1, 4:8], qrow[0:1, 0:4]), reads=["rms_sc"], writes=["rms_sc"])
    A("pe", lambda e: e.matmul(PS7[0:128, 0:4], ones_rf[0:1, :], qrow[0:1, 4:8], start=True, stop=True),
      reads=["rms_sc", "ones_rf", "PS7"], writes=["PS7"])
    A("dve", lambda e: e.tensor_copy(qsc[:, 0:4], PS7[0:128, 0:4]),
      reads=["PS7", "rms_sc"], writes=["rms_sc", "PS7"])
    A("pe", lambda e: e.matmul(PS7[0:128, 4:7], ones_rf[0:1, :], qrow[0:1, 0:3], start=True, stop=True),
      reads=["rms_sc", "ones_rf", "PS7"], writes=["PS7"])
    A("dve", lambda e: e.tensor_copy(wsraw[:, 0:3], PS7[0:128, 4:7]),
      reads=["PS7", "rms_sc"], writes=["rms_sc", "PS7"])
    def q_gu(m, r):
        hbm = (wg_hbm, wu_hbm)[m]
        alln = (wqg_all, wqu_all)[m]
        kk = ("qg", "qu")[m]
        for t in range(DF8 // 128):
            row0 = (r * (DF8 // 128) + t) * 128
            for h in range(2):
                q = (t * 2 + h) % 4
                xq = x_st[:, q * 1024:(q + 1) * 1024]
                A("sp", lambda e, xq=xq, row0=row0, h=h, hbm=hbm: e.dma_start(
                    out=xq, in_=hbm[row0:row0 + 128, h * 1024:(h + 1) * 1024]),
                  reads=[("x_stq", q), ("x_st", q // 2)],
                  writes=[("x_stq", q), ("x_st", q // 2)], dma="Q_s")
                A("dve", lambda e, xq=xq, m=m: e.tensor_scalar(xq, xq, qsc[:, m:m + 1], CBIG,
                                                               ALU.mult, ALU.add),
                  reads=[("x_stq", q), "rms_sc"], writes=[("x_stq", q)])
                A("dve", lambda e, xq=xq: e.tensor_scalar(xq, xq, CBIG, None, ALU.subtract),
                  reads=[("x_stq", q)], writes=[("x_stq", q)])
                A("dve", lambda e, xq=xq, h=h: e.tensor_scalar(
                    z_st[:, h * 1024:(h + 1) * 1024], xq, 1.0, -1.0, ALU.min, ALU.max),
                  reads=[("x_stq", q), "z_st", ("zs", 2 * h), ("zs", 2 * h + 1)],
                  writes=["z_st", ("zs", 2 * h), ("zs", 2 * h + 1)])
            A("act", lambda e, t=t: e.dma_start(out=wqt_dram[t * 128:(t + 1) * 128, 0:D],
                                                in_=z_st[:, 0:D]),
              reads=["z_st", ("zs", 0), ("zs", 1), ("zs", 2), ("zs", 3)],
              writes=["wqtch", ("wqt", t)], dma="Q_t")
        for j in range(nk):
            A("act", lambda e, j=j: e.dma_start_transpose(z_st[:, 0:DF8],
                                                          wqt_dram[0:DF8, j * 128:(j + 1) * 128]),
              reads=[("wqt", t) for t in range(DF8 // 128)] + ["z_st", ("zs", 0), ("zs", 1)],
              writes=["z_st", ("zs", 0), ("zs", 1)], dma="Q_t")
            A("act", lambda e, j=j, r=r, alln=alln: e.dma_start(
                out=alln[r:r + 1, j * 128 * DF8:(j + 1) * 128 * DF8]
                .rearrange("o (p q) -> (o p) q", p=128), in_=z_st[:, 0:DF8]),
              reads=["z_st", ("zs", 0), ("zs", 1)], writes=[kk + "ch", (kk, r, j)], dma="Q_t")

    def q_d(r):
        for t in range(D // 128):
            q = t % 4
            h = t % 2
            xq = x_st[:, q * 1024:(q + 1) * 1024]
            A("sp", lambda e, xq=xq, r=r, t=t: e.dma_start(
                out=xq, in_=wd_hbm[t * 128:(t + 1) * 128, r * DD8:(r + 1) * DD8]),
              reads=[("x_stq", q), ("x_st", q // 2)],
              writes=[("x_stq", q), ("x_st", q // 2)], dma="Q_s")
            A("dve", lambda e, xq=xq: e.tensor_scalar(xq, xq, qsc[:, 2:3], CBIG,
                                                      ALU.mult, ALU.add),
              reads=[("x_stq", q), "rms_sc"], writes=[("x_stq", q)])
            A("dve", lambda e, xq=xq: e.tensor_scalar(xq, xq, CBIG, None, ALU.subtract),
              reads=[("x_stq", q)], writes=[("x_stq", q)])
            A("dve", lambda e, xq=xq, h=h: e.tensor_scalar(
                z_st[:, h * 1024:(h + 1) * 1024], xq, 1.0, -1.0, ALU.min, ALU.max),
              reads=[("x_stq", q), "z_st", ("zs", 2 * h), ("zs", 2 * h + 1)],
              writes=["z_st", ("zs", 2 * h), ("zs", 2 * h + 1)])
            A("act", lambda e, t=t, h=h: e.dma_start(
                out=wqt_dram[t * 128:(t + 1) * 128, 0:DD8],
                in_=z_st[:, h * 1024:(h + 1) * 1024]),
              reads=["z_st", ("zs", 2 * h), ("zs", 2 * h + 1)],
              writes=["wqtch", ("wqt2", t)], dma="Q_t")
        for j in range(DD8 // 128):
            A("act", lambda e, j=j: e.dma_start_transpose(z_st[:, 0:D],
                                                          wqt_dram[0:D, j * 128:(j + 1) * 128]),
              reads=[("wqt2", t) for t in range(D // 128)] + ["z_st",
                     ("zs", 0), ("zs", 1), ("zs", 2), ("zs", 3)],
              writes=["z_st", ("zs", 0), ("zs", 1), ("zs", 2), ("zs", 3)], dma="Q_t")
            A("act", lambda e, j=j, r=r: e.dma_start(
                out=wqd_all[r:r + 1, j * 128 * D:(j + 1) * 128 * D]
                .rearrange("o (p q) -> (o p) q", p=128), in_=z_st[:, 0:D]),
              reads=["z_st", ("zs", 0), ("zs", 1), ("zs", 2), ("zs", 3)],
              writes=["qdch", ("qd", r, j)], dma="Q_t")

    pw_units = ([lambda m=m, r=r: q_gu(m, r) for m in range(2) for r in range(DFF // DF8)]
                + [lambda r=r: q_d(r) for r in range(DFF // DD8)])
    PWL = (len(pw_units) + nch - 1) // nch

    for c in range(nch):
        p = c % 2
        A("sp", lambda e, c=c, p=p: e.dma_start(out=xs_db[:, p * DI:(p + 1) * DI],
                                                in_=xs_dram[c * 128:(c + 1) * 128, :]),
          reads=[("xs_d", c), ("xs_db", p)], writes=[("xs_db", p)], dma="Q_s")
        A("act", lambda e, c=c: e.dma_start(out=aflat_h[:], in_=ahl_dram[c:c + 1, :]),
          reads=["ahl_d"] + K_AFH, writes=K_AFH, dma="Q_t")
        A("act", lambda e, c=c: e.dma_start(out=aflat_l[:], in_=ahl_dram[nch + c:nch + c + 1, :]),
          reads=["ahl_d2"] + K_AFL, writes=K_AFL, dma="Q_t")
        A("pe", lambda e, c=c: e.matmul(PS4[:, 0:128], b_fmb[0:NS, c * CH:(c + 1) * CH],
                                        c_fmb[0:NS, c * CH:(c + 1) * CH], start=True, stop=True),
          reads=[("bf", c), ("cf", c), "PS4"], writes=["PS4"])
        A("dve", lambda e: e.tensor_copy(s_bf[:], PS4[:, 0:128]),
          reads=["PS4", "s_bf"], writes=["PS4", "s_bf"])
        for q in range(4):
            for i in range(4):
                A("pe", lambda e, q=q, i=i: e.matmul(
                    PB[:, i * 512:(i + 1) * 512], ones_rb[:],
                    aflat_h[0:1, q * 2048 + i * 512:q * 2048 + (i + 1) * 512],
                    start=True, stop=False),
                  reads=K_AFH + ["ones_rb"], writes=[("PB", i)])
                A("pe", lambda e, q=q, i=i: e.matmul(
                    PB[:, i * 512:(i + 1) * 512], ones_rb[:],
                    aflat_l[0:1, q * 2048 + i * 512:q * 2048 + (i + 1) * 512],
                    start=False, stop=False),
                  reads=K_AFL + ["ones_rb"], writes=[])
                A("pe", lambda e, q=q, i=i: e.matmul(
                    PB[:, i * 512:(i + 1) * 512].rearrange("p (r n) -> p r n", r=4),
                    triu[:], maskr[:, :].unsqueeze(1).broadcast_to((128, 4, 128)),
                    start=False, stop=True),
                  reads=["triu", "maskr"], writes=[("PB", i)])
            for hl in range(16):
                h = q * 16 + hl
                eb = e_bufs[:, (h % 2) * CH:(h % 2 + 1) * CH]
                gb = g_bufs[:, (h % 2) * CH:(h % 2 + 1) * CH]
                A("act", lambda e, c=c, hl=hl, h=h, eb=eb: e.activation(
                    eb, PB[:, hl * 128:(hl + 1) * 128], AF.Exp,
                    bias=ev_tm[:, c * H + h:c * H + h + 1]),
                  reads=[("PB", hl // 4), ("evt", c), ("ebuf", h % 2)], writes=[("ebuf", h % 2)])
                A("dve", lambda e, eb=eb, gb=gb: e.tensor_tensor(gb, s_bf[:], eb, ALU.mult),
                  reads=["s_bf", ("ebuf", h % 2), ("gbuf", h % 2)], writes=[("gbuf", h % 2)])
                pg, pgk = (PS5, "PS5") if (h // 8) % 2 == 0 else (PS6, "PS6")
                A("pe", lambda e, h=h, gb=gb, pg=pg, p=p: e.matmul(
                    pg[:, (h % 8) * 64:(h % 8 + 1) * 64], gb,
                    xs_db[:, p * DI + h * 64:p * DI + (h + 1) * 64], start=True, stop=True),
                  reads=[("gbuf", h % 2), ("xs_db", p), pgk],
                  writes=[pgk] if h % 8 == 7 else [])
                if h % 8 == 7:
                    gg = h // 8
                    A("dve", lambda e, c=c, gg=gg, p=p: e.tensor_tensor(
                        u2_bf[:, :].rearrange("p (r n) -> p r n", r=8),
                        xs_db[:, p * DI + gg * 512:p * DI + (gg + 1) * 512]
                        .rearrange("p (r n) -> p r n", r=8),
                        c0_tm[:, c * H + gg * 8:c * H + (gg + 1) * 8].unsqueeze(2)
                        .broadcast_to((128, 8, 64)), ALU.mult),
                      reads=[("xs_db", p), ("c0t", c), "scz"], writes=["scz"])
                    A("dve", lambda e, gg=gg, pg=pg, pgk=pgk: e.tensor_tensor(
                        y_st[:, gg * 512:(gg + 1) * 512], pg[:, :], u2_bf[:], ALU.add),
                      reads=["scz", pgk, "y_st"], writes=["y_st", pgk])
        A("act", lambda e, c=c: e.dma_start(out=y_dram[c * 128:(c + 1) * 128, :], in_=y_st[:, :]),
          reads=["y_st"], writes=["ych", ("y_d", c)], dma="Q_t")
        for u in pw_units[c * PWL:(c + 1) * PWL]:
            u()

    for c in range(nch):
        p = c % 2
        A("sp", lambda e, c=c, p=p: e.dma_start(out=xs_db[:, p * DI:(p + 1) * DI],
                                                in_=z_dram[c * 128:(c + 1) * 128, :]),
          reads=[("z_d", c), ("xs_db", p)], writes=[("xs_db", p)], dma="Q_s")
        A("sp", lambda e, c=c: e.dma_start(out=y_st[:, :], in_=y_dram[c * 128:(c + 1) * 128, :]),
          reads=[("y_d", c), "y_st"], writes=["y_st"], dma="Q_s")
        A("sp", lambda e, c=c: e.dma_start(out=prev_bf[:, :], in_=prev_dram[c * NS:(c + 1) * NS, :]),
          reads=[("prevd", c)] + K_PRV, writes=K_PRV, dma="Q_s")
        for gg in range(8):
            pg, pgk = (PS5, "PS5") if gg % 2 == 0 else (PS6, "PS6")
            A("pe", lambda e, c=c, gg=gg, pg=pg: e.matmul(
                pg[:, :], c_fmb[0:NS, c * CH:(c + 1) * CH],
                prev_bf[:, gg * 512:(gg + 1) * 512], start=True, stop=True),
              reads=K_PRV + [("cf", c), pgk], writes=[pgk])
            A("dve", lambda e, c=c, gg=gg, pg=pg, pgk=pgk: e.tensor_tensor(
                u2_bf[:, :].rearrange("p (r n) -> p r n", r=8),
                pg[:, :].rearrange("p (r n) -> p r n", r=8),
                qd_tm[:, c * H + gg * 8:c * H + (gg + 1) * 8].unsqueeze(2).broadcast_to((128, 8, 64)),
                ALU.mult),
              reads=[pgk, ("qdt", c), "scz"], writes=["scz", pgk])
            A("dve", lambda e, gg=gg: e.tensor_tensor(y_st[:, gg * 512:(gg + 1) * 512],
                                                      y_st[:, gg * 512:(gg + 1) * 512],
                                                      u2_bf[:], ALU.add),
              reads=["scz", "y_st"], writes=["y_st"])
        for gg in range(8):
            A("dve", lambda e, gg=gg: e.tensor_tensor(
                x_st[:, 0:512], y_st[:, gg * 512:(gg + 1) * 512], y_st[:, gg * 512:(gg + 1) * 512],
                ALU.mult),
              reads=["y_st", ("x_st", 0)], writes=[("x_st", 0)])
            A("dve", lambda e, gg=gg: e.tensor_reduce(
                (rw1[:, 0:1] if gg == 0 else rw2[:, 0:1]), x_st[:, 0:512], AX.X, ALU.add),
              reads=[("x_st", 0), "rw1", "rw2"], writes=[("rw1" if gg == 0 else "rw2")])
            if gg > 0:
                A("dve", lambda e: e.tensor_tensor(rw1[:], rw1[:], rw2[:], ALU.add),
                  reads=["rw1", "rw2"], writes=["rw1"])
        A("dve", lambda e: e.tensor_scalar(rw1[:], rw1[:], 1.0 / DI, 1e-6, ALU.mult, ALU.add),
          reads=["rw1"], writes=["rw1"])
        A("dve", lambda e: e.reciprocal(rw2[:], rw1[:]), reads=["rw1", "rw2"], writes=["rw2"])
        A("act", lambda e: e.activation(rw2[:], rw2[:], AF.Sqrt), reads=["rw2"], writes=["rw2"])
        A("act", lambda e, p=p: e.activation(xs_st[:, :], xs_db[:, p * DI:(p + 1) * DI], AF.Silu),
          reads=[("xs_db", p), "xs_st"], writes=["xs_st"])
        A("dve", lambda e: e.tensor_scalar(y_st[:, :], y_st[:, :], rw2[:], None, ALU.mult),
          reads=["y_st", "rw2"], writes=["y_st"])
        A("dve", lambda e: e.tensor_tensor(y_st[:, :], y_st[:, :], xs_st[:, :], ALU.mult),
          reads=["y_st", "xs_st"], writes=["y_st"])
        A("act", lambda e, c=c: e.dma_start(out=y_dram[c * 128:(c + 1) * 128, :], in_=y_st[:, :]),
          reads=["y_st"], writes=["ych", ("y_d", c)], dma="Q_t")

    def psg(t):
        return PB[:, (t % 4) * 512:(t % 4 + 1) * 512] if t < 4 else (PS4, PS5, PS6, PS7)[t - 4][:, :]

    def psk(t):
        return ("PB", t) if t < 4 else ("PS4", "PS5", "PS6", "PS7")[t - 4]

    for j in range(32):
        A("act", lambda e, j=j: e.dma_start_transpose(y_fm[:, j * t_loc:(j + 1) * t_loc],
                                                      y_dram[:, j * 128:(j + 1) * 128]),
          reads=[("y_d", c) for c in range(nch)] + K_YFM[j], writes=K_YFM[j], dma="Q_t")
    for dgr in range(4):
        for j in range(32):
            A("pool", lambda e, dgr=dgr, j=j: e.dma_start(
                out=wo_buf[:, (j % 2) * 512:(j % 2) * 512 + 512],
                in_=w_out[j * 128:(j + 1) * 128, dgr * 512:(dgr + 1) * 512]),
              reads=K_WO[j % 2], writes=K_WO[j % 2] + [("wo_l", j % 2)], dma="Q_x")
            for t in range(ntt):
                A("pe", lambda e, j=j, t=t: e.matmul(
                    psg(t), y_fm[:, j * t_loc + t * 128:j * t_loc + (t + 1) * 128],
                    wo_buf[:, (j % 2) * 512:(j % 2) * 512 + 512],
                    start=(j == 0), stop=(j == 31)),
                  reads=K_YFM[j] + K_WO[j % 2] + [("wo_l", j % 2)] + ([psk(t)] if j == 0 else []),
                  writes=[psk(t)] if j == 31 else [])
        for t in range(ntt):
            p = t % 2
            A("sp", lambda e, dgr=dgr, t=t, p=p: e.dma_start(
                out=x_st[:, p * D + dgr * 512:p * D + (dgr + 1) * 512],
                in_=x_sl[t * 128:(t + 1) * 128, dgr * 512:(dgr + 1) * 512]),
              reads=[("x_st", p)], writes=[("x_st", p)], dma="Q_s")
            A("dve", lambda e, dgr=dgr, t=t, p=p: e.tensor_tensor(
                x_st[:, p * D + dgr * 512:p * D + (dgr + 1) * 512],
                x_st[:, p * D + dgr * 512:p * D + (dgr + 1) * 512], psg(t), ALU.add),
              reads=[("x_st", p), psk(t)], writes=[("x_st", p), psk(t)])
            A("act", lambda e, dgr=dgr, t=t, p=p: e.dma_start(
                out=gu_dram[t * 128:(t + 1) * 128, dgr * 512:(dgr + 1) * 512],
                in_=x_st[:, p * D + dgr * 512:p * D + (dgr + 1) * 512]),
              reads=[("x_st", p)], writes=[("r1ch", t % 2), ("r1_d", t, dgr)], dma="Q_t")
    for c in range(ntt):
        p = c % 2
        A("sp", lambda e, c=c, p=p: e.dma_start(out=x_st[:, p * D:(p + 1) * D],
                                                in_=gu_dram[c * 128:(c + 1) * 128, 0:D]),
          reads=[("r1_d", c, dgr) for dgr in range(4)] + [("x_st", p)],
          writes=[("x_st", p)], dma="Q_s")
        for bg in range(4):
            A("dve", lambda e, p=p, bg=bg: e.bn_stats(
                sc_a[:, bg * 6:(bg + 1) * 6], x_st[:, p * D + bg * 512:p * D + (bg + 1) * 512]),
              reads=[("x_st", p), "scz"], writes=["scz"])
        A("dve", lambda e: e.bn_aggr(sc_b[:, 0:2], sc_a[:, 0:24]), reads=["scz"], writes=["scz"])
        A("dve", lambda e: e.tensor_scalar(rw1[:], sc_b[:, 1:2], 1.0, 1e-5, ALU.mult, ALU.add),
          reads=["scz", "rw1"], writes=["rw1"])
        A("dve", lambda e: e.reciprocal(rw2[:], rw1[:]), reads=["rw1", "rw2"], writes=["rw2"])
        A("act", lambda e: e.activation(rw2[:], rw2[:], AF.Sqrt), reads=["rw2"], writes=["rw2"])
        A("dve", lambda e, p=p: e.tensor_scalar(x_st[:, p * D:(p + 1) * D], x_st[:, p * D:(p + 1) * D],
                                                sc_b[:, 0:1], rw2[:], ALU.subtract, ALU.mult),
          reads=[("x_st", p), "scz", "rw2"], writes=[("x_st", p)])
        A("act", lambda e, c=c, p=p: e.dma_start(out=ln1_dram[c * 128:(c + 1) * 128, :],
                                                 in_=x_st[:, p * D:(p + 1) * D]),
          reads=[("x_st", p)], writes=["lnch", ("ln1_d", c)], dma="Q_t")
        A("dve", lambda e, p=p: e.tensor_reduce(rw1[:], x_st[:, p * D:(p + 1) * D], AX.X, ALU.max,
                                                apply_absolute_value=True),
          reads=[("x_st", p), "rw1"], writes=["rw1"])
        A("dve", lambda e: e.tensor_scalar(rw1[:], rw1[:], 1.0 / 127.0, 1e-5, ALU.mult, ALU.add),
          reads=["rw1"], writes=["rw1"])
        A("dve", lambda e, c=c: e.tensor_copy(xscal[:, c:c + 1], rw1[:]),
          reads=["rw1", "xscal"], writes=["xscal"])
        A("dve", lambda e: e.reciprocal(rw2[:], rw1[:]), reads=["rw1", "rw2"], writes=["rw2"])
        A("dve", lambda e, p=p: e.tensor_scalar(x_st[:, p * D:(p + 1) * D], x_st[:, p * D:(p + 1) * D],
                                                rw2[:], CBIG, ALU.mult, ALU.add),
          reads=[("x_st", p), "rw2"], writes=[("x_st", p)])
        A("dve", lambda e, p=p: e.tensor_scalar(z_st[:, 0:D], x_st[:, p * D:(p + 1) * D],
                                                CBIG, None, ALU.subtract),
          reads=[("x_st", p), "z_st", ("zs", 0), ("zs", 1), ("zs", 2), ("zs", 3)],
          writes=["z_st", ("zs", 0), ("zs", 1), ("zs", 2), ("zs", 3)])
        A("act", lambda e, c=c: e.dma_start(out=xbf_dram[c * 128:(c + 1) * 128, :], in_=z_st[:, 0:D]),
          reads=["z_st"], writes=["xqch", ("xq_d", c)], dma="Q_t")
    for j in range(nk):
        A("act", lambda e, j=j: e.dma_start_transpose(xq_fm[:, j * t_loc:(j + 1) * t_loc],
                                                      xbf_dram[:, j * 128:(j + 1) * 128]),
          reads=[("xq_d", c) for c in range(ntt)] + K_XQ[j], writes=K_XQ[j], dma="Q_t")

    def psg(t):
        return PB[:, (t % 4) * 512:(t % 4 + 1) * 512] if t < 4 else (PS4, PS5, PS6, PS7)[t - 4][:, :]

    def psk(t):
        return ("PB", t) if t < 4 else ("PS4", "PS5", "PS6", "PS7")[t - 4]

    def wq_tile(alln, j, ig):
        i0 = ig * 512
        r = i0 // DF8
        i_loc = i0 % DF8
        v = alln[r:r + 1, :].rearrange("o (d q) -> (o d) q", d=D)
        return v[j * 128:(j + 1) * 128, i_loc:i_loc + 512]

    def wqd_tile(j):
        i0 = j * 128
        r = i0 // DD8
        i_loc = i0 % DD8
        v = wqd_all[r:r + 1, :].rearrange("o (i q) -> (o i) q", i=DD8)
        return v[i_loc:i_loc + 128, 0:D]

    A("dve", lambda e: e.memset(runmax[:], 0.0), reads=["rms_sc"], writes=["rms_sc"])
    for t in range(ntt):
        A("dve", lambda e, t=t: e.tensor_tensor(xscal[:, 8 + t:9 + t], xscal[:, t:t + 1],
                                                wsraw[:, 0:1], ALU.mult),
          reads=["xscal", "rms_sc"], writes=["xscal"])
        A("dve", lambda e, t=t: e.tensor_tensor(xscal[:, 16 + t:17 + t], xscal[:, t:t + 1],
                                                wsraw[:, 1:2], ALU.mult),
          reads=["xscal", "rms_sc"], writes=["xscal"])
    for ig in range(16):
        for mm, kk in ((0, "qg"), (1, "qu")):
            alln = wqg_all if mm == 0 else wqu_all
            for j in range(nk):
                A("pool", lambda e, alln=alln, j=j, ig=ig: e.dma_start(
                    out=wff_buf[:, (j % 2) * 4096:(j % 2) * 4096 + 512],
                    in_=wq_tile(alln, j, ig)),
                  reads=[(kk, (ig * 512) // DF8, j)] + K_WF[j % 2],
                  writes=K_WF[j % 2] + [("wff_l", j)], dma="Q_x")
                for t in range(ntt):
                    A("pe", lambda e, j=j, t=t: e.matmul(
                        psg(t), xq_fm[:, j * t_loc + t * 128:j * t_loc + (t + 1) * 128],
                        wff_buf[:, (j % 2) * 4096:(j % 2) * 4096 + 512],
                        start=(j == 0), stop=(j == nk - 1)),
                      reads=K_XQ[j] + K_WF[j % 2] + [("wff_l", j)] + ([psk(t)] if j == 0 else []),
                      writes=[psk(t)] if j == nk - 1 else [])
            if mm == 0:
                for t in range(ntt):
                    A("act", lambda e, t=t: e.activation(
                        xs_db[:, :].bitcast(F32)[:, t * 512:(t + 1) * 512], psg(t), AF.Silu,
                        scale=xscal[:, 8 + t:9 + t]),
                      reads=[psk(t), "xs_db_f", "xscal", ("xs_db", 0), ("xs_db", 1)],
                      writes=[psk(t), "xs_db_f", ("xs_db", 0), ("xs_db", 1)])
            else:
                for t in range(ntt):
                    A("dve", lambda e, t=t: e.scalar_tensor_tensor(
                        x_st[:, (t % 2) * D:(t % 2) * D + 512], psg(t), xscal[:, 16 + t:17 + t],
                        xs_db[:, :].bitcast(F32)[:, t * 512:(t + 1) * 512], ALU.mult, ALU.mult),
                      reads=[psk(t), "xs_db_f", "xscal", ("x_st", t % 2)],
                      writes=[("x_st", t % 2), psk(t)])
                    A("dve", lambda e, t=t: e.tensor_reduce(
                        rw1[:], x_st[:, (t % 2) * D:(t % 2) * D + 512], AX.X, ALU.max,
                        apply_absolute_value=True),
                      reads=[("x_st", t % 2), "rw1"], writes=["rw1"])
                    A("dve", lambda e, t=t: e.tensor_tensor(runmax[:, t:t + 1], runmax[:, t:t + 1],
                                                            rw1[:], ALU.max),
                      reads=["rw1", "rms_sc"], writes=["rms_sc"])
                    A("act", lambda e, t=t, ig=ig: e.dma_start(
                        out=gu_dram[t * 128:(t + 1) * 128, ig * 512:(ig + 1) * 512],
                        in_=x_st[:, (t % 2) * D:(t % 2) * D + 512]),
                      reads=[("x_st", t % 2), ("r1_d2",)], writes=[("guch", t % 4), ("gu_d", ig, t)], dma="Q_t")

    A("dve", lambda e: e.tensor_scalar(runmax[:], runmax[:], 1.0 / 127.0, 1e-5, ALU.mult, ALU.add),
      reads=["rms_sc"], writes=["rms_sc"])
    for t in range(ntt):
        A("dve", lambda e, t=t: e.tensor_tensor(xscal[:, 24 + t:25 + t], runmax[:, t:t + 1],
                                                wsraw[:, 2:3], ALU.mult),
          reads=["xscal", "rms_sc"], writes=["xscal"])
    A("dve", lambda e: e.reciprocal(wqacc[:, 0:8], runmax[:]), reads=["rms_sc", "rms_sc"],
      writes=["rms_sc"])
    for t in range(ntt):
        for ig in range(4):
            p = ig % 2
            A("sp", lambda e, t=t, ig=ig, p=p: e.dma_start(
                out=x_st[:, p * D:(p + 1) * D],
                in_=gu_dram[t * 128:(t + 1) * 128, ig * D:(ig + 1) * D]),
              reads=[("gu_d", gg, t) for gg in range(ig * 4, ig * 4 + 4)] + [("x_st", p)],
              writes=[("x_st", p)], dma="Q_s")
            A("dve", lambda e, t=t, p=p: e.tensor_scalar(x_st[:, p * D:(p + 1) * D],
                                                         x_st[:, p * D:(p + 1) * D],
                                                         wqacc[:, t:t + 1], CBIG, ALU.mult, ALU.add),
              reads=[("x_st", p), "rms_sc"], writes=[("x_st", p)])
            A("dve", lambda e, p=p: e.tensor_scalar(z_st[:, 0:D], x_st[:, p * D:(p + 1) * D],
                                                    CBIG, None, ALU.subtract),
              reads=[("x_st", p), "z_st"], writes=["z_st"])
            A("act", lambda e, t=t, ig=ig: e.dma_start(
                out=guq_dram[t * 128:(t + 1) * 128, ig * D:(ig + 1) * D], in_=z_st[:, 0:D]),
              reads=["z_st"], writes=["guqch", ("guq_d", t, ig)], dma="Q_t")

    GUQD = [("guq_d", t, ig) for t in range(8) for ig in range(4)]
    for qq in range(NQ):
        for j in range(DFF // 128):
            A("act", lambda e, j=j, qq=qq: e.dma_start_transpose(
                guq_fm[:, j * TQ:(j + 1) * TQ],
                guq_dram[qq * TQ:(qq + 1) * TQ, j * 128:(j + 1) * 128]),
              reads=GUQD + K_GQ[j], writes=K_GQ[j], dma="Q_t")
        for j in range(DFF // 128):
            A("pool", lambda e, j=j: e.dma_start(
                out=wff_buf[:, (j % 2) * 4096 + 2048:(j % 2) * 4096 + 4096], in_=wqd_tile(j)),
              reads=[("qd", (j * 128) // DD8, (j * 128 % DD8) // 128)] + K_WF[j % 2],
              writes=K_WF[j % 2] + [("wfd_l", j)], dma="Q_x")
            for t in range(TQ // 128):
                for dgr in range(4):
                    bk = (t * 4 + dgr)
                    A("pe", lambda e, j=j, t=t, dgr=dgr, qq=qq, bk=bk: e.matmul(
                        psg(bk), guq_fm[:, j * TQ + t * 128:j * TQ + (t + 1) * 128],
                        wff_buf[:, (j % 2) * 4096 + 2048 + dgr * 512:(j % 2) * 4096 + 2048 + (dgr + 1) * 512],
                        start=(j == 0), stop=(j == DFF // 128 - 1)),
                      reads=K_GQ[j] + K_WF[j % 2] + [("wfd_l", j)] + ([psk(bk)] if j == 0 else []),
                      writes=[psk(bk)] if j == DFF // 128 - 1 else [])
        for t in range(TQ // 128):
            c = qq * (TQ // 128) + t
            p = t % 2
            A("sp", lambda e, c=c, p=p: e.dma_start(out=x_st[:, p * D:(p + 1) * D],
                                                    in_=ln1_dram[c * 128:(c + 1) * 128, :]),
              reads=[("ln1_d", c), ("x_st", p)], writes=[("x_st", p)], dma="Q_s")
            for dgr in range(4):
                bk = t * 4 + dgr
                A("dve", lambda e, t=t, dgr=dgr, p=p, bk=bk, c=c: e.scalar_tensor_tensor(
                    x_st[:, p * D + dgr * 512:p * D + (dgr + 1) * 512],
                    psg(bk), xscal[:, 24 + c:25 + c],
                    x_st[:, p * D + dgr * 512:p * D + (dgr + 1) * 512], ALU.mult, ALU.add),
                  reads=[("x_st", p), psk(bk), "xscal"], writes=[("x_st", p), psk(bk)])
            for bg in range(4):
                A("dve", lambda e, p=p, bg=bg: e.bn_stats(
                    sc_a[:, bg * 6:(bg + 1) * 6], x_st[:, p * D + bg * 512:p * D + (bg + 1) * 512]),
                  reads=[("x_st", p), "scz"], writes=["scz"])
            A("dve", lambda e: e.bn_aggr(sc_b[:, 0:2], sc_a[:, 0:24]),
              reads=["scz", "scz"], writes=["scz"])
            A("dve", lambda e: e.tensor_scalar(rw1[:], sc_b[:, 1:2], 1.0, 1e-5, ALU.mult, ALU.add),
              reads=["scz", "rw1"], writes=["rw1"])
            A("dve", lambda e: e.reciprocal(rw2[:], rw1[:]), reads=["rw1", "rw2"], writes=["rw2"])
            A("act", lambda e: e.activation(rw2[:], rw2[:], AF.Sqrt), reads=["rw2"], writes=["rw2"])
            A("dve", lambda e, p=p: e.tensor_scalar(x_st[:, p * D:(p + 1) * D],
                                                    x_st[:, p * D:(p + 1) * D],
                                                    sc_b[:, 0:1], rw2[:], ALU.subtract, ALU.mult),
              reads=[("x_st", p), "scz", "rw2"], writes=[("x_st", p)])
            A("pool", lambda e, c=c, p=p: e.dma_start(out=out_hbm[c * 128:(c + 1) * 128, :],
                                                      in_=x_st[:, p * D:(p + 1) * D]),
              reads=[("x_st", p)], writes=["outch", ("out_d", c)], dma="Q_o")

    def dmp(name, src, rkeys):
        if name in dbg_out:
            A("pool", lambda e, s=src: e.dma_start(out=dbg_out[name], in_=s), reads=rkeys, writes=["dbgch"], dma="Q_o")

    dmp("dtl", dtl_fm[:], ["dtl"])
    if "xfm0" in dbg_out:
        A("pool", lambda e: e.dma_start(out=dbg_out["xfm0"], in_=x_fm[:, 0:t_loc]),
          reads=K_XFM[0], writes=["dbgch"], dma="Q_o")
    dmp("dt", dt_loc[:], ["dt"])
    dmp("u", u_fm[:], ["u"])
    dmp("aloc", aloc_fm[:], ["aloc"])
    dmp("cum", cum_l[:], ["cum"])
    dmp("ev", ev_fm[:], ["ev"])
    if "xsilu" in dbg_out:
        for c in range(nch):
            A("pool", lambda e, c=c: e.dma_start(out=dbg_out["xsilu"][c * 128:(c + 1) * 128, :],
                                                 in_=xs_dram[c * 128:(c + 1) * 128, :].bitcast(BF16))
              if False else e.dma_start(out=dbg_out["xsilu"][c * 128:(c + 1) * 128, :],
                                        in_=xs_dram[c * 128:(c + 1) * 128, :]),
              reads=[("xs_d", c)], writes=["dbgch"], dma="Q_o")
    if "zdmp" in dbg_out:
        for c in range(nch):
            A("pool", lambda e, c=c: e.dma_start(out=dbg_out["zdmp"][c * 128:(c + 1) * 128, :],
                                                 in_=z_dram[c * 128:(c + 1) * 128, :]),
              reads=[("z_d", c)], writes=["dbgch"], dma="Q_o")
    if "guf" in dbg_out:
        for t in range(ntt):
            A("pool", lambda e, t=t: e.dma_start(out=dbg_out["guf"][t * 128:(t + 1) * 128, :],
                                                 in_=gu_dram[t * 128:(t + 1) * 128, :]),
              reads=[("gu_d", ig, t) for ig in range(16)], writes=["dbgch"], dma="Q_o")
    if "xqd" in dbg_out:
        for c in range(ntt):
            A("pool", lambda e, c=c: e.dma_start(out=dbg_out["xqd"][c * 128:(c + 1) * 128, :],
                                                 in_=xbf_dram[c * 128:(c + 1) * 128, :]),
              reads=[("xq_d", c)], writes=["dbgch"], dma="Q_o")
    if "qrowd" in dbg_out:
        A("pool", lambda e: e.dma_start(out=dbg_out["qrowd"], in_=qrow[0:1, 0:8]),
          reads=["rms_sc"], writes=["dbgch"], dma="Q_o")
    if "ymix" in dbg_out:
        for c in range(nch):
            A("pool", lambda e, c=c: e.dma_start(out=dbg_out["ymix"][c * 128:(c + 1) * 128, :],
                                                 in_=y_dram[c * 128:(c + 1) * 128, :]),
              reads=[("y_d", c)], writes=["dbgch"], dma="Q_o")
    if "ln1" in dbg_out:
        for c in range(ntt):
            A("pool", lambda e, c=c: e.dma_start(out=dbg_out["ln1"][c * 128:(c + 1) * 128, :],
                                                 in_=ln1_dram[c * 128:(c + 1) * 128, :]),
              reads=[("ln1_d", c)], writes=["dbgch"], dma="Q_o")

    sems = {}
    sem_ctx = []
    names = ["S_" + e for e in Emitter.ENGS] + sorted(em.dma_counts.keys())
    for nm in names:
        s = nc.semaphore(nm)
        sem_ctx.append(s)
        sems[nm] = s.__enter__()
    finals = {("S_" + e): c for e, c in em.counts.items() if c}
    finals.update({k: v for k, v in em.dma_counts.items() if v})
    per_eng = {e: [] for e in Emitter.ENGS}
    for (eng, fn, waits, sem, inc) in em.steps:
        per_eng[eng].append((fn, waits, sem, inc))
    blk = nc.Block()
    block = blk.__enter__()

    def runner(e):
        def run(obj):
            for (fn, waits, sem, inc) in per_eng[e]:
                for (s, cc) in waits:
                    obj.wait_ge(sems[s], cc)
                fn(obj).then_inc(sems[sem], inc)
            if e == "pool":
                for s, cc in sorted(finals.items()):
                    obj.wait_ge(sems[s], cc)
        return run

    block.tensor(runner("pe"))
    block.scalar(runner("act"))
    block.vector(runner("dve"))
    block.gpsimd(runner("pool"))
    block.sync(runner("sp"))
    blk.__exit__(None, None, None)
    for s in reversed(sem_ctx):
        s.__exit__(None, None, None)
    for t in reversed(ctx):
        t.__exit__(None, None, None)
    return nc


def make_core_inputs(core, full, t_loc, ncores):
    cps = ncores // 2
    b, slot = core // cps, core % cps
    x = np.asarray(full["L0_attn_x"])
    t0 = slot * t_loc
    d = {}
    d["x_sl"] = np.ascontiguousarray(x[b, t0:t0 + t_loc, :])
    npre = max(cps - 1, 1)
    xp = np.zeros((npre * t_loc, D), np.float32)
    if t0 > 0:
        xp[npre * t_loc - t0:] = x[b, 0:t0, :]
    d["x_pre"] = xp
    if t0 + t_loc < x.shape[1]:
        d["x_ex"] = np.ascontiguousarray(x[b, t0 + t_loc:t0 + t_loc + 1, :])
        bm = 1.0
    else:
        d["x_ex"] = np.zeros((1, D), np.float32)
        bm = 0.0
    d["w_in"] = np.asarray(full["L0_attn_W_in"])
    d["w_out"] = np.asarray(full["L0_attn_W_out"])
    d["wg_hbm"] = np.ascontiguousarray(np.asarray(full["L0_ffn_w_gate"]))
    d["wu_hbm"] = np.ascontiguousarray(np.asarray(full["L0_ffn_w_up"]))
    d["wd_hbm"] = np.ascontiguousarray(np.asarray(full["L0_ffn_w_down"]))
    d["a_log"] = np.asarray(full["L0_attn_A_log"]).reshape(H, 1).astype(np.float32)
    d["dt_bias"] = np.asarray(full["L0_attn_dt_bias"]).reshape(H, 1).astype(np.float32)
    d["bnb_in"] = np.asarray(full["L0_attn_B_norm_bias"]).reshape(1, NS).astype(np.float32)
    d["cnb_in"] = np.asarray(full["L0_attn_C_norm_bias"]).reshape(1, NS).astype(np.float32)
    d["identf_in"] = np.eye(128, dtype=np.float32)
    d["triu_in"] = np.triu(np.ones((128, 128), np.float32))
    d["maskr_in"] = (-1e4 * np.eye(128)).astype(np.float32)
    d["bmask_in"] = np.full((H, 1), bm, np.float32)
    return d


def kernel(**inputs):
    t_loc, ncores = 1024, NC
    nc = build_program(t_loc=t_loc, ncores=ncores)
    in_maps = [make_core_inputs(c, inputs, t_loc, ncores) for c in range(ncores)]
    res = run_bass_kernel_spmd(nc, in_maps, core_ids=list(range(ncores)))
    out = np.zeros((2, 4096, D), np.float32)
    cps = ncores // 2
    for c in range(ncores):
        b, slot = c // cps, c % cps
        out[b, slot * t_loc:(slot + 1) * t_loc, :] = res.results[c]["out"]
    return out


NC = 8


# version 08
# speedup vs baseline: 1.2584x; 1.2584x over previous
import numpy as np
import concourse.bass as bass
import concourse.mybir as mybir
from concourse.bass_utils import run_bass_kernel_spmd

F32 = mybir.dt.float32
BF16 = mybir.dt.bfloat16
I32 = mybir.dt.int32
ALU = mybir.AluOpType
AF = mybir.ActivationFunctionType
AX = mybir.AxisListType

D = 2048
DI = 4096
H = 64
NS = 64
DP = 8480
DFF = 8192
CBIG = 12582912.0
TWO_PI = float(2 * np.pi)
PI = float(np.pi)
OFF_B = 2 * DI
OFF_DT = 2 * DI + 2 * NS
OFF_TH = OFF_DT + 2 * H


class Emitter:
    ENGS = ("pe", "act", "dve", "pool", "sp")

    def __init__(self):
        self.steps = []
        self.writers = {}
        self.readers = {}
        self.counts = {e: 0 for e in self.ENGS}
        self.dma_counts = {}
        self.known = {e: {} for e in self.ENGS}

    def add(self, eng, fn, reads=(), writes=(), dma=None):
        need = {}
        for k in reads:
            w = self.writers.get(k)
            if w:
                need[w[0]] = max(need.get(w[0], 0), w[1])
        for k in writes:
            w = self.writers.get(k)
            if w:
                need[w[0]] = max(need.get(w[0], 0), w[1])
            for s, c in self.readers.get(k, ()):
                need[s] = max(need.get(s, 0), c)
        kn = self.known[eng]
        waits = []
        for s, c in sorted(need.items(), key=str):
            if kn.get(s, 0) < c:
                waits.append((s, c))
                kn[s] = c
        if dma is None:
            self.counts[eng] += 1
            sem, cnt, inc = "S_" + eng, self.counts[eng], 1
        else:
            key = writes[0] if writes else reads[0]
            dma = "D" + ("s" if eng == "pool" else "h") + repr(key)
            self.dma_counts[dma] = self.dma_counts.get(dma, 0) + 16
            sem, cnt, inc = dma, self.dma_counts[dma], 16
        self.steps.append((eng, fn, waits, sem, inc))
        for k in reads:
            self.readers.setdefault(k, []).append((sem, cnt))
        for k in writes:
            self.writers[k] = (sem, cnt)
            self.readers[k] = []


def build_program(t_loc=1024, ncores=8, dbg=()):
    CH = 128
    nch = t_loc // CH
    ntt = nch
    nk = 16
    TL1 = t_loc + 1
    ntg = t_loc // 512
    DF8 = DFF // 8
    DD8 = DFF // 8
    NQ = 4
    TQ = t_loc // NQ
    NSEG = max(ncores // 2, 1)
    assert DD8 <= D and DF8 <= DFF
    nc = bass.Bass("TRN2", target_bir_lowering=False)
    em = Emitter()
    A = em.add
    ctx = []

    def sbuf(name, shape, dtype):
        t = nc.sbuf_tensor(name, shape, dtype)
        ctx.append(t)
        return t.__enter__()

    def psumt(name, shape, dtype):
        t = nc.psum_tensor(name, shape, dtype)
        ctx.append(t)
        return t.__enter__()

    def din(name, shape, dt=F32, kind="ExternalInput"):
        return nc.dram_tensor(name, shape, dt, kind=kind).ap()

    def dit(name, shape, dt, shared=False):
        if shared:
            return nc.dram_tensor(name, shape, dt, addr_space="Shared").ap()
        return nc.dram_tensor(name, shape, dt).ap()

    x_sl = din("x_sl", [t_loc, D])
    x_pre = din("x_pre", [max(NSEG - 1, 1) * t_loc, D])
    x_ex = din("x_ex", [1, D])
    w_in = din("w_in", [D, DP])
    w_out = din("w_out", [DI, D])
    wg_hbm = din("wg_hbm", [DFF, D])
    wu_hbm = din("wu_hbm", [DFF, D])
    wd_hbm = din("wd_hbm", [D, DFF])
    a_log = din("a_log", [H, 1])
    dt_bias = din("dt_bias", [H, 1])
    bnb_in = din("bnb_in", [1, NS])
    cnb_in = din("cnb_in", [1, NS])
    identf_in = din("identf_in", [128, 128])
    triu_in = din("triu_in", [128, 128])
    maskr_in = din("maskr_in", [128, 128])
    bmask_in = din("bmask_in", [H, 1])
    out_hbm = din("out", [t_loc, D], F32, "ExternalOutput")

    xbf_dram = dit("xbf_dram", [t_loc, D], BF16)
    ahl_dram = dit("ahl_dram", [2 * nch, H * CH], BF16)
    z_dram = dit("z_dram", [t_loc, DI], BF16)
    xs_dram = dit("xs_dram", [t_loc, DI], BF16)
    prev_dram = dit("prev_dram", [nch * NS, DI], BF16)
    y_dram = dit("y_dram", [t_loc, DI], BF16)
    ln1_dram = dit("ln1_dram", [t_loc, D], F32)
    gu_dram = dit("gu_dram", [t_loc, DFF], F32)
    guq_dram = dit("guq_dram", [t_loc, DFF], BF16)
    wqt_dram = dit("wqt_dram", [max(DF8, D), max(D, DD8)], BF16)
    row_dram = dit("row_dram", [4, 256], F32)
    wqg_all = dit("wqg_all", [DFF, D], BF16)
    wqu_all = dit("wqu_all", [DFF, D], BF16)
    wqd_all = dit("wqd_all", [D, DFF], BF16)

    dbg_out = {}
    for name, shape in dbg:
        dbg_out[name] = din("dbg_" + name, list(shape), F32, "ExternalOutput")

    identf = sbuf("identf", [64, 64], F32)
    identb = sbuf("identb", [128, 128], BF16)
    triu = sbuf("triu", [128, 128], BF16)
    maskr = sbuf("maskr", [128, 128], BF16)
    ones_rb = sbuf("ones_rb", [1, 128], BF16)
    ones_rf = sbuf("ones_rf", [1, 128], F32)
    ones_cf = sbuf("ones_cf", [128, 1], F32)
    aneg = sbuf("aneg", [H, 1], F32)
    alog_s = sbuf("alog_s", [H, 1], F32)
    dtb_s = sbuf("dtb_s", [H, 1], F32)
    bnb_m = sbuf("bnb_m", [128, NS], F32)
    cnb_m = sbuf("cnb_m", [128, NS], F32)
    bmask_s = sbuf("bmask_s", [H, 1], F32)
    AR = sbuf("AR", [128, 49152], BF16)
    ARf = AR[:, :].bitcast(F32)
    ARi = AR[:, :].bitcast(I32)
    _p2b = 33 * 512
    rr_f = ARf[0:32, _p2b:_p2b + t_loc]
    rr_i = ARi[0:32, _p2b + t_loc:_p2b + 2 * t_loc]
    rr_g = rr_f
    rr_m = ARf[0:32, _p2b + t_loc:_p2b + 2 * t_loc]
    cum_l = ARf[0:32, _p2b + 2 * t_loc:_p2b + 3 * t_loc]
    cs_bf = AR[0:32, 2 * (_p2b + 3 * t_loc):2 * (_p2b + 3 * t_loc) + 2 * t_loc]
    th_sg = ARf[0:32, _p2b + 4 * t_loc:_p2b + 4 * t_loc + 512]
    P2B_KEYS = [("AR", b) for b in range(33, 33 + (4 * t_loc + 512 + 511) // 512)]
    dtl_fm = ARf[:, _p2b + 4 * t_loc + 512:_p2b + 5 * t_loc + 513]
    wsm = AR[:, 46336:46336 + nk * 160]
    ev_tm = ARf[:, 41 * 512:41 * 512 + nch * H]
    qd_tm = AR[:, 43008:43008 + nch * H]
    ws_tm = AR[:, 43008 + 512:43008 + 512 + nch * H]
    c0_tm = AR[:, 44032:44032 + nch * H]
    aflat_h = AR[0:1, 0:H * CH]
    K_AFH = [("AR", b) for b in range((H * CH + 1023) // 1024)]
    aflat_l = AR[0:1, 8 * 1024:8 * 1024 + H * CH]
    K_AFL = [("AR", b) for b in range(8, 8 + (H * CH + 1023) // 1024)]
    x_st = sbuf("x_st", [128, 2 * D], F32)
    z_st = sbuf("z_st", [128, D], BF16)
    xs_st = sbuf("xs_st", [128, DI], BF16)
    y_st = sbuf("y_st", [128, DI], BF16)
    xs_db = sbuf("xs_db", [128, 2 * DI], BF16)
    xe_fm = sbuf("xe_fm", [128, 16], BF16)
    dt_loc = sbuf("dt_loc", [H, TL1], F32)
    lamS = sbuf("lamS", [H, TL1], F32)
    f1 = sbuf("f1", [H, TL1], F32)
    f2 = sbuf("f2", [H, TL1], F32)
    sc128 = f2[:, 0:128]
    wg_fm = sbuf("wg_fm", [H, t_loc], F32)
    u_fm = sbuf("u_fm", [H, t_loc], F32)
    aloc_fm = sbuf("aloc_fm", [H, t_loc], F32)
    ev_fm = sbuf("ev_fm", [H, t_loc], F32)
    ah_fm = sbuf("ah_fm", [H, t_loc], BF16)
    al_fm = sbuf("al_fm", [H, t_loc], BF16)
    vrow = sbuf("vrow", [1, 512], F32)
    carry = sbuf("carry", [32, 1], F32)
    cos_tm = sbuf("cos_tm", [128, nch * 32], BF16)
    sin_tm = sbuf("sin_tm", [128, nch * 32], BF16)
    bc_t = sbuf("bc_t", [128, nch * 128], F32)
    bcf = sbuf("bcf", [128, 2 * NS], F32)
    b_tmbA = sbuf("b_tmbA", [128, nch * NS], BF16)
    b_fmb = sbuf("b_fmb", [NS, t_loc], BF16)
    c_fmb = sbuf("c_fmb", [NS, t_loc], BF16)
    rms_sc = sbuf("rms_sc", [128, NS], F32)
    wqacc = rms_sc[:, 0:24]
    qsc = rms_sc[:, 24:28]
    runmax = rms_sc[:, 28:36]
    qrow = rms_sc[0:1, 40:48]
    wsraw = rms_sc[:, 36:39]
    rw1 = sbuf("rw1", [128, 1], F32)
    rw2 = sbuf("rw2", [128, 1], F32)
    sdd_tm = sbuf("sdd_tm", [128, nch], F32)
    xscal = sbuf("xscal", [128, 4 * 8], F32)
    pwf = sbuf("pwf", [128, 1024], F32)
    pwacc = sbuf("pwacc", [128, 25], F32)

    cd_col = sbuf("cd_col", [H, 1], F32)
    cd_row = sbuf("cd_row", [1, H], F32)
    e_bufs = sbuf("e_bufs", [128, 2 * CH], BF16)
    g_bufs = sbuf("g_bufs", [128, 2 * CH], BF16)
    s_bf = sbuf("s_bf", [128, CH], BF16)
    sc128b = s_bf
    u2_bf = sbuf("u2_bf", [128, 512], BF16)
    sc_a = u2_bf[:, :].bitcast(F32)[:, 0:64]
    sc_b = u2_bf[:, :].bitcast(F32)[:, 64:128]
    coef_s = sbuf("coef_s", [4, H], F32)

    def ark(lo, n):
        return [("AR", b) for b in range(lo // 1024, (lo + n + 1023) // 1024)]

    x_fm = AR[:, 0:nk * t_loc]
    K_XFM = [ark(j * t_loc, t_loc) for j in range(nk)]
    w_buf = AR[:, nk * t_loc:nk * t_loc + 2 * 512]
    K_WB = [ark(nk * t_loc + p * 512, 512) for p in range(2)]
    st_run = ARf[0:NS, 8704:12800]
    K_STR = ark(17408, 2 * DI)
    wx_bf = AR[:, 25600:29696]
    K_WX = ark(25600, DI)
    prev_bf = AR[0:NS, 29696:33792]
    K_PRV = ark(29696, DI)
    y_fm = AR[:, 0:32 * t_loc]
    K_YFM = [ark(j * t_loc, t_loc) for j in range(32)]
    wo_buf = AR[:, 32 * t_loc:32 * t_loc + 2 * 512]
    K_WO = [ark(32 * t_loc + p * 512, 512) for p in range(2)]
    xq_fm = AR[:, 0:nk * t_loc]
    K_XQ = K_XFM
    wff_buf = AR[:, nk * t_loc:nk * t_loc + 8 * 1024]
    K_WF = [ark(nk * t_loc + p * 4096, 4096) for p in range(2)]
    guq_fm = AR[:, 24 * 1024:24 * 1024 + (DFF // 128) * TQ]
    K_GQ = [ark(24 * 1024 + j * TQ, TQ) for j in range(DFF // 128)]

    PB = psumt("PB", [128, 2048], F32)
    PBb = PB[:, :].bitcast(BF16)
    PS4 = psumt("PS4", [128, 512], F32)
    PS5 = psumt("PS5", [128, 512], F32)
    PS6 = psumt("PS6", [128, 512], F32)
    PS7 = psumt("PS7", [128, 512], F32)
    PBK = [("PB", 0), ("PB", 1), ("PB", 2), ("PB", 3)]

    A("pool", lambda e: e.dma_start(out=identb[:], in_=identf_in), writes=["p0ch", "identb"], dma="Q_x")
    A("pool", lambda e: e.dma_start(out=triu[:], in_=triu_in), writes=["p0ch", "triu"], dma="Q_x")
    A("pool", lambda e: e.dma_start(out=maskr[:], in_=maskr_in), writes=["p0ch", "maskr"], dma="Q_x")
    A("sp", lambda e: e.dma_start(out=identf[:], in_=identf_in[0:64, 0:64]), writes=["p0chs", "identf"], dma="Q_s")
    for t, s, k in ((alog_s, a_log, "alog"), (dtb_s, dt_bias, "dtb"), (bmask_s, bmask_in, "bmask")):
        A("sp", lambda e, t=t, s=s: e.dma_start(out=t[:], in_=s), writes=["p0chs", k], dma="Q_s")
    A("dve", lambda e: e.memset(ones_rb[:], 1.0), writes=["ones_rb"])
    A("dve", lambda e: e.memset(ones_rf[:], 1.0), writes=["ones_rf"])
    A("dve", lambda e: e.memset(ones_cf[:], 1.0), writes=["ones_cf"])
    A("act", lambda e: e.activation(aneg[:], alog_s[:], AF.Exp), reads=["alog"], writes=["aneg"])
    A("dve", lambda e: e.tensor_scalar(aneg[:], aneg[:], -1.0, None, ALU.mult),
      reads=["aneg"], writes=["aneg"])
    for src, dstm, k in ((bnb_in, bnb_m, "bnb_m"), (cnb_in, cnb_m, "cnb_m")):
        A("sp", lambda e, s=src: e.dma_start(out=vrow[0:1, 0:NS], in_=s), reads=["vrow"],
          writes=["vrow"], dma="Q_s")
        A("pe", lambda e: e.matmul(PS4[:, 0:NS], ones_rf[:], vrow[0:1, 0:NS], start=True, stop=True),
          reads=["ones_rf", "vrow", "PS4"], writes=["PS4"])
        A("dve", lambda e, d=dstm: e.tensor_copy(d[:], PS4[:, 0:NS]), reads=["PS4"], writes=["PS4", k])

    for j in range(nk):
        A("pool", lambda e, j=j: e.dma_start(out=wsm[:, j * 160:j * 160 + 128],
                                             in_=w_in[j * 128:(j + 1) * 128, OFF_DT:OFF_DT + 128]),
          writes=[("wsmch", j % 2), ("wsm", j)], dma="Q_x")
        A("pool", lambda e, j=j: e.dma_start(out=wsm[:, j * 160 + 128:(j + 1) * 160],
                                             in_=w_in[j * 128:(j + 1) * 128, OFF_TH:OFF_TH + 32]),
          writes=[("wsmch", j % 2), ("wsm2", j)], dma="Q_x")

    def front(seg):
        own = seg == NSEG - 1
        if own:
            xsrc, xex_src = x_sl, x_ex
        elif seg == NSEG - 2:
            xsrc = x_pre[seg * t_loc:(seg + 1) * t_loc, :]
            xex_src = x_sl[0:1, :]
        else:
            xsrc = x_pre[seg * t_loc:(seg + 1) * t_loc, :]
            xex_src = x_pre[(seg + 1) * t_loc:(seg + 1) * t_loc + 1, :]
        A("pool", lambda e: e.dma_start(out=xbf_dram[:, :], in_=xsrc[:, :]), writes=["xbfd"], dma="Q_x")
        for j in range(nk):
            A("act", lambda e, j=j: e.dma_start_transpose(x_fm[:, j * t_loc:(j + 1) * t_loc],
                                                          xbf_dram[:, j * 128:(j + 1) * 128]),
              reads=["xbfd"] + K_XFM[j], writes=K_XFM[j], dma="Q_t")
        A("sp", lambda e: e.dma_start(out=x_st[0:1, 0:D], in_=xex_src), reads=[("x_st", 0)],
          writes=[("x_st", 0)], dma="Q_s")
        for j in range(16):
            A("pe", lambda e, j=j: e.matmul(PS4[:, j:j + 1], x_st[0:1, j * 128:(j + 1) * 128],
                                            ones_rf[0:1, 0:1], start=True, stop=True),
              reads=[("x_st", 0), "ones_rf"] + (["PS4"] if j == 0 else []),
              writes=["PS4"] if j == 15 else [])
        A("dve", lambda e: e.tensor_copy(xe_fm[:], PS4[:, 0:16]), reads=["PS4"], writes=["PS4", "xe_fm"])

        for g in range(ntg):
            for j in range(nk):
                A("pe", lambda e, g=g, j=j: e.matmul(PS7[0:128, 0:512], wsm[:, j * 160:j * 160 + 128],
                                                     x_fm[:, j * t_loc + g * 512:j * t_loc + (g + 1) * 512],
                                                     start=(j == 0), stop=(j == nk - 1)),
                  reads=K_XFM[j] + [("wsm", j)] + (["PS7"] if j == 0 else []),
                  writes=["PS7"] if j == nk - 1 else [])
            A("dve", lambda e, g=g: e.tensor_copy(dtl_fm[:, g * 512:(g + 1) * 512], PS7[0:128, 0:512]),
              reads=["PS7", "dtl"], writes=["PS7", "dtl"] + [("wst", cc) for cc in range(nch)])
        for j in range(nk):
            A("pe", lambda e, j=j: e.matmul(PS5[0:128, 0:1], wsm[:, j * 160:j * 160 + 128], xe_fm[:, j:j + 1],
                                            start=(j == 0), stop=(j == nk - 1)),
              reads=["xe_fm", ("wsm", j)] + (["PS5"] if j == 0 else []),
              writes=["PS5"] if j == nk - 1 else [])
        A("dve", lambda e: e.tensor_copy(dtl_fm[:, t_loc:TL1], PS5[0:128, 0:1]),
          reads=["PS5", "dtl"], writes=["PS5", "dtl"] + [("wst", cc) for cc in range(nch)])

        def psg(t):
            return PB[:, (t % 4) * 512:(t % 4 + 1) * 512] if t < 4 else (PS4, PS5, PS6, PS7)[t - 4][:, :]

        def psk(t):
            return ("PB", t) if t < 4 else ("PS4", "PS5", "PS6", "PS7")[t - 4]

        for g in (([16] + list(range(8, 16)) + list(range(8))) if own
                  else ([16] + list(range(8, 16)))):
            gw = 512 if g < 16 else 128
            goff = g * 512 if g < 16 else OFF_B
            for j in range(nk):
                A("pool", lambda e, g=g, j=j, gw=gw, goff=goff: e.dma_start(
                    out=w_buf[:, (j % 2) * 512:(j % 2) * 512 + gw],
                    in_=w_in[j * 128:(j + 1) * 128, goff:goff + gw]),
                  reads=K_WB[j % 2], writes=K_WB[j % 2] + [("w_l", j % 2)], dma="Q_x")
                for t in range(ntt):
                    A("pe", lambda e, g=g, j=j, t=t, gw=gw: e.matmul(
                        psg(t)[:, 0:gw], x_fm[:, j * t_loc + t * 128:j * t_loc + (t + 1) * 128],
                        w_buf[:, (j % 2) * 512:(j % 2) * 512 + gw],
                        start=(j == 0), stop=(j == nk - 1)),
                      reads=K_XFM[j] + K_WB[j % 2] + [("w_l", j % 2)] + ([psk(t)] if j == 0 else []),
                      writes=[psk(t)] if j == nk - 1 else [])
            for t in range(ntt):
                if g < 8:
                    A("dve", lambda e, g=g, t=t: e.tensor_copy(z_st[:, (t % 4) * 512:(t % 4 + 1) * 512],
                                                               psg(t)),
                      reads=[psk(t), ("zs", t % 4)], writes=[psk(t), ("zs", t % 4)])
                    A("act", lambda e, g=g, t=t: e.dma_start(
                        out=z_dram[t * 128:(t + 1) * 128, g * 512:(g + 1) * 512],
                        in_=z_st[:, (t % 4) * 512:(t % 4 + 1) * 512]),
                      reads=[("zs", t % 4)], writes=[("zch", t % 2), ("zs", t % 4), ("z_d", t)], dma="Q_t")
                elif g < 16:
                    A("act", lambda e, g=g, t=t: e.activation(xs_st[:, (t % 4) * 512:(t % 4 + 1) * 512],
                                                              psg(t), AF.Silu),
                      reads=[psk(t), ("xss", t % 4)], writes=[psk(t), ("xss", t % 4)])
                    A("act", lambda e, g=g, t=t: e.dma_start(
                        out=xs_dram[t * 128:(t + 1) * 128, (g - 8) * 512:(g - 7) * 512],
                        in_=xs_st[:, (t % 4) * 512:(t % 4 + 1) * 512]),
                      reads=[("xss", t % 4)], writes=[("xsch", t % 2), ("xss", t % 4), ("xs_d", t)], dma="Q_t")
                else:
                    A("dve", lambda e, t=t: e.tensor_copy(bc_t[:, t * 128:(t + 1) * 128],
                                                          psg(t)[:, 0:128]),
                      reads=[psk(t), "bc_t"], writes=[psk(t), "bc_t", ("bc", t)])

        A("dve", lambda e: e.tensor_scalar(f1[:], dtl_fm[0:64, :], dtb_s[:], None, ALU.add),
          reads=["dtl", "dtb", "f1"], writes=["f1"])
        A("act", lambda e: e.activation(f2[:], f1[:], AF.Abs), reads=["f1", "f2"], writes=["f2"])
        A("act", lambda e: e.activation(f2[:], f2[:], AF.Exp, scale=-1.0), reads=["f2"], writes=["f2"])
        A("act", lambda e: e.activation(f2[:], f2[:], AF.Ln, bias=1.0), reads=["f2"], writes=["f2"])
        A("dve", lambda e: e.tensor_scalar(f1[:], f1[:], 0.0, None, ALU.max), reads=["f1"], writes=["f1"])
        A("dve", lambda e: e.tensor_tensor(dt_loc[:], f1[:], f2[:], ALU.add),
          reads=["f1", "f2", "dt"], writes=["dt"])
        A("dve", lambda e: e.tensor_scalar(dt_loc[:], dt_loc[:], 0.5, 1e-4, ALU.min, ALU.max),
          reads=["dt"], writes=["dt"])
        A("act", lambda e: e.activation(lamS[:], dtl_fm[64:128, :], AF.Sigmoid),
          reads=["dtl", "lamS"], writes=["lamS"])
        A("dve", lambda e: e.tensor_tensor(wg_fm[:], dt_loc[:, 0:t_loc], lamS[:, 0:t_loc], ALU.mult),
          reads=["dt", "lamS", "wg"], writes=["wg"])
        A("dve", lambda e: e.tensor_tensor(lamS[:], dt_loc[:], lamS[:], ALU.mult),
          reads=["dt", "lamS"], writes=["lamS"])
        A("dve", lambda e: e.tensor_tensor(lamS[:], dt_loc[:], lamS[:], ALU.subtract),
          reads=["dt", "lamS"], writes=["lamS"])
        if own:
            A("dve", lambda e: e.tensor_scalar(lamS[:, t_loc:TL1], lamS[:, t_loc:TL1],
                                               bmask_s[:], None, ALU.mult),
              reads=["lamS", "bmask"], writes=["lamS"])
        A("dve", lambda e: e.tensor_tensor(u_fm[:], wg_fm[:], lamS[:, 1:TL1], ALU.add),
          reads=["wg", "lamS", "u"], writes=["u"])
        A("dve", lambda e: e.tensor_scalar(f1[:, 0:t_loc], dt_loc[:, 0:t_loc], aneg[:], None, ALU.mult),
          reads=["dt", "aneg", "f1"], writes=["f1"])
        for c in range(nch):
            A("dve", lambda e, c=c: e.tensor_tensor_scan(
                aloc_fm[:, c * CH:(c + 1) * CH], ones_cf[0:H, 0:1].broadcast_to((H, CH)),
                f1[:, c * CH:(c + 1) * CH], 0.0, ALU.mult, ALU.add),
              reads=["f1", "ones_cf", "aloc"], writes=["aloc"])
        A("act", lambda e: e.activation(ev_fm[:], u_fm[:], AF.Ln), reads=["u", "ev"], writes=["ev"])
        A("dve", lambda e: e.tensor_tensor(ev_fm[:], ev_fm[:], aloc_fm[:], ALU.subtract),
          reads=["ev", "aloc"], writes=["ev"])
        A("dve", lambda e: e.tensor_copy(ah_fm[:], aloc_fm[:]), reads=["aloc", "ah"], writes=["ah"])
        A("dve", lambda e: e.tensor_copy(f1[:, 0:t_loc], ah_fm[:]), reads=["ah", "f1"], writes=["f1"])
        A("dve", lambda e: e.tensor_tensor(f1[:, 0:t_loc], aloc_fm[:], f1[:, 0:t_loc], ALU.subtract),
          reads=["f1", "aloc"], writes=["f1"])
        A("dve", lambda e: e.tensor_copy(al_fm[:], f1[:, 0:t_loc]), reads=["f1", "al"], writes=["al"])
        if own:
            for c in range(nch):
                A("act", lambda e, c=c: e.dma_start(
                    out=ahl_dram[c:c + 1, :].rearrange("o (h t) -> (o h) t", h=H),
                    in_=ah_fm[:, c * CH:(c + 1) * CH]),
                  reads=["ah"], writes=["ahl_d"], dma="Q_t")
                A("act", lambda e, c=c: e.dma_start(
                    out=ahl_dram[nch + c:nch + c + 1, :].rearrange("o (h t) -> (o h) t", h=H),
                    in_=al_fm[:, c * CH:(c + 1) * CH]),
                  reads=["al"], writes=["ahl_d2"], dma="Q_t")

        for g in range(ntg):
            for j in range(nk):
                A("pe", lambda e, g=g, j=j: e.matmul(PS6[0:32, 0:512], wsm[:, j * 160 + 128:(j + 1) * 160],
                                                     x_fm[:, j * t_loc + g * 512:j * t_loc + (g + 1) * 512],
                                                     start=(j == 0), stop=(j == nk - 1)),
                  reads=K_XFM[j] + [("wsm2", j)] + (["PS6"] if j == 0 else []),
                  writes=["PS6"] if j == nk - 1 else [])
            A("dve", lambda e: e.tensor_copy(th_sg[:], PS6[0:32, 0:512]),
              reads=["PS6", "th_sg"], writes=["th_sg", "PS6"])
            A("pe", lambda e, g=g: e.matmul(PS5[0:1, 0:512], ones_cf[0:H, 0:1],
                                            dt_loc[:, g * 512:(g + 1) * 512], start=True, stop=True),
              reads=["dt", "ones_cf", "PS5"], writes=["PS5"])
            A("dve", lambda e: e.tensor_scalar(vrow[:], PS5[0:1, 0:512], 1.0 / H, None, ALU.mult),
              reads=["PS5", "vrow"], writes=["vrow", "PS5"])
            A("pe", lambda e: e.matmul(PS5[0:32, 0:512], ones_rf[0:1, 0:32], vrow[:], start=True, stop=True),
              reads=["vrow", "ones_rf", "PS5"], writes=["PS5"])
            A("dve", lambda e: e.scalar_tensor_tensor(th_sg[:], th_sg[:], -1.0, PS5[0:32, 0:512],
                                                      ALU.mult, ALU.mult),
              reads=["th_sg", "PS5"], writes=["th_sg", "PS5"])
            A("dve", lambda e, g=g: e.tensor_copy(cum_l[:, g * 512:(g + 1) * 512], th_sg[:]),
              reads=["th_sg", "cum"], writes=["cum"])
        if seg == 0:
            A("dve", lambda e: e.memset(carry[:], 0.0), reads=["carry"], writes=["carry"])
        for g in range(ntg):
            A("dve", lambda e, g=g: e.tensor_tensor_scan(
                cum_l[:, g * 512:(g + 1) * 512], ones_cf[0:32, 0:1].broadcast_to((32, 512)),
                cum_l[:, g * 512:(g + 1) * 512], carry[0:32, 0:1], ALU.mult, ALU.add),
              reads=["cum", "carry", "ones_cf"], writes=["cum"])
            A("dve", lambda e, g=g: e.tensor_copy(carry[:], cum_l[:, g * 512 + 511:g * 512 + 512]),
              reads=["cum", "carry"], writes=["carry"])
        for phase, key, off in ((0.0, "sin", t_loc), (0.25, "cos", 0)):
            A("dve", lambda e, ph=phase: e.tensor_scalar(rr_f[:], cum_l[:], 1.0 / TWO_PI, ph,
                                                         ALU.mult, ALU.add),
              reads=["cum", "rr_f"], writes=["rr_f"])
            A("dve", lambda e: e.tensor_copy(rr_i[:], rr_f[:]), reads=["rr_f", "rr_i"], writes=["rr_i"])
            A("dve", lambda e: e.tensor_copy(rr_g[:], rr_i[:]), reads=["rr_i", "rr_g"], writes=["rr_g"])
            A("dve", lambda e: e.scalar_tensor_tensor(rr_f[:], rr_g[:], -TWO_PI, cum_l[:],
                                                      ALU.mult, ALU.add),
              reads=["rr_g", "cum", "rr_f"], writes=["rr_f"])
            if phase == 0.25:
                A("dve", lambda e: e.tensor_scalar(rr_f[:], rr_f[:], PI / 2, None, ALU.add),
                  reads=["rr_f"], writes=["rr_f"])
            A("dve", lambda e: e.tensor_scalar(rr_m[:], rr_f[:], PI, None, ALU.is_gt),
              reads=["rr_f", "rr_i"], writes=["rr_i"])
            A("dve", lambda e: e.scalar_tensor_tensor(rr_f[:], rr_m[:], -TWO_PI, rr_f[:],
                                                      ALU.mult, ALU.add),
              reads=["rr_f", "rr_i"], writes=["rr_f"])
            A("dve", lambda e: e.tensor_scalar(rr_m[:], rr_f[:], -PI, None, ALU.is_lt),
              reads=["rr_f", "rr_i"], writes=["rr_i"])
            A("dve", lambda e: e.scalar_tensor_tensor(rr_f[:], rr_m[:], TWO_PI, rr_f[:],
                                                      ALU.mult, ALU.add),
              reads=["rr_f", "rr_i"], writes=["rr_f"])
            A("act", lambda e, off=off: e.activation(cs_bf[:, off:off + t_loc], rr_f[:], AF.Sin),
              reads=["rr_f"], writes=[key])
        for c in range(nch):
            A("pe", lambda e, c=c: e.transpose(PBb[:, 0:32], cs_bf[:, c * CH:(c + 1) * CH],
                                               identb[0:32, 0:32]),
              reads=["cos", "identb", ("PB", 0)], writes=[("PB", 0)])
            A("dve", lambda e, c=c: e.tensor_copy(cos_tm[:, c * 32:(c + 1) * 32], PBb[:, 0:32]),
              reads=[("PB", 0), "cos_tm"], writes=[("PB", 0), "cos_tm"])
            A("pe", lambda e, c=c: e.transpose(PBb[:, 1024:1056],
                                               cs_bf[:, t_loc + c * CH:t_loc + (c + 1) * CH],
                                               identb[0:32, 0:32]),
              reads=["sin", "identb", ("PB", 1)], writes=[("PB", 1)])
            A("dve", lambda e, c=c: e.tensor_copy(sin_tm[:, c * 32:(c + 1) * 32], PBb[:, 1024:1056]),
              reads=[("PB", 1), "sin_tm"], writes=[("PB", 1), "sin_tm"])

        A("dve", lambda e: e.engine_nop(),
          reads=["cum", "rr_f", "rr_i", "rr_g", "cos", "sin", "th_sg", "carry"],
          writes=P2B_KEYS + ["ev_tm", "qd_tm", "ws_tm"])

        for c in range(nch):
            for nm, off, nbm, nbk, dof in ((("B", 0, bnb_m, "bnb_m", 0),) +
                                           ((("C", NS, cnb_m, "cnb_m", NS),) if own else ())):
                dst = bcf[:, dof:dof + NS]
                sl = bc_t[:, c * 128 + off:c * 128 + off + NS]
                A("dve", lambda e, sl=sl: e.tensor_tensor(rms_sc[:], sl, sl, ALU.mult),
                  reads=[("bc", c), "rms_sc"], writes=["rms_sc"])
                A("dve", lambda e: e.tensor_reduce(rw1[:], rms_sc[:], AX.X, ALU.add),
                  reads=["rms_sc", "rw1"], writes=["rw1"])
                A("dve", lambda e: e.tensor_scalar(rw1[:], rw1[:], 1.0 / NS, 1e-6, ALU.mult, ALU.add),
                  reads=["rw1"], writes=["rw1"])
                A("dve", lambda e: e.reciprocal(rw2[:], rw1[:]), reads=["rw1", "rw2"], writes=["rw2"])
                A("act", lambda e: e.activation(rw2[:], rw2[:], AF.Sqrt), reads=["rw2"], writes=["rw2"])
                A("dve", lambda e, sl=sl: e.tensor_scalar(rms_sc[:], sl, rw2[:], None, ALU.mult),
                  reads=[("bc", c), "rw2", "rms_sc"], writes=["rms_sc"])
                A("dve", lambda e, nbm=nbm: e.tensor_tensor(rms_sc[:], rms_sc[:], nbm[:], ALU.mult),
                  reads=["rms_sc", nbk], writes=["rms_sc"])
                cslice = cos_tm[:, c * 32:(c + 1) * 32]
                sslice = sin_tm[:, c * 32:(c + 1) * 32]
                A("dve", lambda e, cs=cslice: e.tensor_tensor(sc_a[:, 0:32], rms_sc[:, 0:32], cs, ALU.mult),
                  reads=["rms_sc", "cos_tm", "scz"], writes=["scz"])
                A("dve", lambda e, ss=sslice: e.tensor_tensor(sc_a[:, 32:64], rms_sc[:, 32:64], ss, ALU.mult),
                  reads=["rms_sc", "sin_tm", "scz"], writes=["scz"])
                A("dve", lambda e, dst=dst: e.tensor_tensor(
                    dst[:, 0:32], sc_a[:, 0:32], sc_a[:, 32:64], ALU.subtract),
                  reads=["scz", "bcf"], writes=["bcf"])
                A("dve", lambda e, ss=sslice: e.tensor_tensor(sc_b[:, 0:32], rms_sc[:, 0:32], ss, ALU.mult),
                  reads=["rms_sc", "sin_tm", "scz"], writes=["scz"])
                A("dve", lambda e, cs=cslice: e.tensor_tensor(sc_b[:, 32:64], rms_sc[:, 32:64], cs, ALU.mult),
                  reads=["rms_sc", "cos_tm", "scz"], writes=["scz"])
                A("dve", lambda e, dst=dst: e.tensor_tensor(
                    dst[:, 32:NS], sc_b[:, 0:32], sc_b[:, 32:64], ALU.add),
                  reads=["scz", "bcf"], writes=["bcf"])
            A("dve", lambda e, c=c: e.tensor_copy(b_tmbA[:, c * NS:(c + 1) * NS], bcf[:, 0:NS]),
              reads=["bcf", "b_tmbA"], writes=["b_tmbA", ("Bb", c)])
            if not own:
                continue
            A("dve", lambda e, c=c: e.tensor_tensor(rms_sc[:], bcf[:, 0:NS], bcf[:, NS:2 * NS], ALU.mult),
              reads=["bcf", "rms_sc"], writes=["rms_sc"])
            A("dve", lambda e, c=c: e.tensor_reduce(sdd_tm[:, c:c + 1], rms_sc[:], AX.X, ALU.add),
              reads=["rms_sc", "sdd"], writes=["sdd"])
            A("pe", lambda e, c=c: e.transpose(PBb[0:64, 2048:2176], b_tmbA[:, c * NS:(c + 1) * NS],
                                               identb[:]),
              reads=[("Bb", c), "identb", ("PB", 2)], writes=[("PB", 2)])
            A("dve", lambda e, c=c: e.tensor_copy(b_fmb[:, c * CH:(c + 1) * CH], PBb[0:64, 2048:2176]),
              reads=[("PB", 2), "b_fmb"], writes=[("PB", 2), "b_fmb", ("bf", c)])
            A("dve", lambda e, c=c: e.tensor_copy(sc128b[:, 0:NS], bcf[:, NS:2 * NS]),
              reads=["bcf", "s_bf"], writes=["s_bf"])
            A("pe", lambda e, c=c: e.transpose(PBb[0:64, 3072:3200], sc128b[:, 0:NS], identb[:]),
              reads=["s_bf", "identb", ("PB", 3)], writes=[("PB", 3)])
            A("dve", lambda e, c=c: e.tensor_copy(c_fmb[:, c * CH:(c + 1) * CH], PBb[0:64, 3072:3200]),
              reads=[("PB", 3), "c_fmb"], writes=[("PB", 3), "c_fmb", ("cf", c)])

        for c in range(nch):
            A("act", lambda e, c=c: e.activation(sc128[:], ev_fm[:, c * CH:(c + 1) * CH], AF.Exp,
                                                 bias=aloc_fm[:, c * CH + CH - 1:c * CH + CH]),
              reads=["ev", "aloc", "f2"], writes=["f2"])
            A("dve", lambda e: e.tensor_copy(sc128b[0:H, 0:CH], sc128[:]),
              reads=["f2", "s_bf"], writes=["s_bf"])
            A("pe", lambda e, c=c: e.transpose(PBb[:, 3072:3136], sc128b[0:H, 0:CH], identb[0:H, 0:H]),
              reads=["s_bf", "identb", ("PB", 3)], writes=[("PB", 3)])
            A("dve", lambda e, c=c: e.tensor_copy(ws_tm[:, c * H:(c + 1) * H], PBb[:, 3072:3136]),
              reads=[("PB", 3), "ws_tm", "dtl"], writes=[("PB", 3), "ws_tm", ("wst", c)])
            if not own:
                continue
            A("pe", lambda e, c=c: e.transpose(PB[:, 512:576], ev_fm[:, c * CH:(c + 1) * CH],
                                               identf[0:64, 0:64]),
              reads=["ev", "identf", ("PB", 1)], writes=[("PB", 1)])
            A("dve", lambda e, c=c: e.tensor_copy(ev_tm[:, c * H:(c + 1) * H], PB[:, 512:576]),
              reads=[("PB", 1), "ev_tm"], writes=[("PB", 1), "ev_tm", ("evt", c)])
            A("act", lambda e, c=c: e.activation(sc128[:], aloc_fm[:, c * CH:(c + 1) * CH], AF.Exp),
              reads=["aloc", "f2"], writes=["f2"])
            A("dve", lambda e: e.tensor_copy(sc128b[0:H, 0:CH], sc128[:]),
              reads=["f2", "s_bf"], writes=["s_bf"])
            A("pe", lambda e, c=c: e.transpose(PBb[:, 2048:2112], sc128b[0:H, 0:CH], identb[0:H, 0:H]),
              reads=["s_bf", "identb", ("PB", 2)], writes=[("PB", 2)])
            A("dve", lambda e, c=c: e.tensor_copy(qd_tm[:, c * H:(c + 1) * H], PBb[:, 2048:2112]),
              reads=[("PB", 2), "qd_tm", "dtl"], writes=[("PB", 2), "qd_tm", ("qdt", c)])
            A("pe", lambda e, c=c: e.transpose(PB[:, 576:640], wg_fm[:, c * CH:(c + 1) * CH],
                                               identf[0:64, 0:64]),
              reads=["wg", "identf", ("PB", 1)], writes=[("PB", 1)])
            A("dve", lambda e, c=c: e.tensor_scalar(c0_tm[:, c * H:(c + 1) * H], PB[:, 576:640],
                                                    sdd_tm[:, c:c + 1], None, ALU.mult),
              reads=[("PB", 1), "sdd", "c0_tm", "dtl"], writes=[("PB", 1), "c0_tm", ("c0t", c)])

        if seg == 0:
            A("dve", lambda e: e.memset(st_run[:, :], 0.0), reads=K_STR, writes=K_STR)
        for c in range(nch):
            p = c % 2
            A("sp", lambda e, c=c, p=p: e.dma_start(out=xs_db[:, p * DI:(p + 1) * DI],
                                                    in_=xs_dram[c * 128:(c + 1) * 128, :]),
              reads=[("xs_d", c), ("xs_db", p)], writes=[("xs_db", p)], dma="Q_s")
            A("dve", lambda e, c=c, p=p: e.tensor_tensor(
                wx_bf[:, :].rearrange("p (h q) -> p h q", h=H),
                xs_db[:, p * DI:(p + 1) * DI].rearrange("p (h q) -> p h q", h=H),
                ws_tm[:, c * H:(c + 1) * H].unsqueeze(2).broadcast_to((128, H, NS)), ALU.mult),
              reads=[("xs_db", p), ("wst", c)] + K_WX, writes=K_WX)
            if own:
                A("dve", lambda e: e.tensor_copy(prev_bf[:, :], st_run[:, :]),
                  reads=K_STR + K_PRV, writes=K_PRV)
                A("act", lambda e, c=c: e.dma_start(out=prev_dram[c * NS:(c + 1) * NS, :],
                                                    in_=prev_bf[:, :]),
                  reads=K_PRV, writes=["prch", ("prevd", c)], dma="Q_t")
            A("act", lambda e, c=c: e.activation(cd_col[:], aloc_fm[:, c * CH + CH - 1:c * CH + CH], AF.Exp),
              reads=["aloc", "cd_col"], writes=["cd_col"])
            A("pe", lambda e: e.transpose(PS4[0:1, 0:64], cd_col[:], identf[:]),
              reads=["cd_col", "identf", "PS4"], writes=["PS4"])
            A("dve", lambda e: e.tensor_copy(cd_row[:], PS4[0:1, 0:64]),
              reads=["PS4", "cd_row"], writes=["cd_row", "PS4"])
            A("pe", lambda e: e.matmul(PS7[0:NS, 0:NS], ones_rf[0:1, 0:NS], cd_row[:], start=True, stop=True),
              reads=["cd_row", "ones_rf", "PS7"], writes=["PS7"])
            A("dve", lambda e: e.tensor_tensor(
                st_run[:, :].rearrange("n (h q) -> n h q", h=H),
                st_run[:, :].rearrange("n (h q) -> n h q", h=H),
                PS7[0:NS, 0:NS].unsqueeze(2).broadcast_to((NS, H, NS)), ALU.mult),
              reads=K_STR + ["PS7"] + (K_PRV if own else []), writes=K_STR + ["PS7"])
            for half in range(2):
                for i in range(4):
                    A("pe", lambda e, c=c, half=half, i=i: e.matmul(
                        PB[0:NS, i * 512:(i + 1) * 512], b_tmbA[:, c * NS:(c + 1) * NS],
                        wx_bf[:, half * 2048 + i * 512:half * 2048 + (i + 1) * 512],
                        start=True, stop=True),
                      reads=K_WX + [("Bb", c), ("PB", i)], writes=[("PB", i)])
                A("dve", lambda e, c=c, half=half: e.tensor_tensor(
                    st_run[:, half * 2048:(half + 1) * 2048], st_run[:, half * 2048:(half + 1) * 2048],
                    PB[0:NS, 0:2048], ALU.add),
                  reads=K_STR + PBK, writes=K_STR + PBK)


    def pw_stats(tiles, col0):
        for i, ap in enumerate(tiles):
            h = i % 2
            A("sp", lambda e, ap=ap, h=h: e.dma_start(out=pwf[:, h * 512:(h + 1) * 512], in_=ap),
              reads=[("pwf", h)], writes=[("pwf", h)], dma="Q_s")
            col = col0 + i % 8
            if i < 8:
                A("dve", lambda e, h=h, col=col: e.tensor_reduce(
                    pwacc[:, col:col + 1], pwf[:, h * 512:(h + 1) * 512],
                    AX.X, ALU.add, apply_absolute_value=True),
                  reads=[("pwf", h), "pwacc"], writes=["pwacc"])
            else:
                A("dve", lambda e, h=h: e.tensor_reduce(
                    pwacc[:, 24:25], pwf[:, h * 512:(h + 1) * 512], AX.X, ALU.add,
                    apply_absolute_value=True),
                  reads=[("pwf", h), "pwacc"], writes=["pwacc"])
                A("dve", lambda e, col=col: e.tensor_tensor(
                    pwacc[:, col:col + 1], pwacc[:, col:col + 1], pwacc[:, 24:25], ALU.add),
                  reads=["pwacc"], writes=["pwacc"])

    A("dve", lambda e: e.memset(pwacc[:], 0.0), reads=["pwacc"], writes=["pwacc"])

    def gu_tiles(hbm):
        return [hbm[t * 128:(t + 1) * 128, h * 512:(h + 1) * 512]
                for t in range(DFF // 128) for h in range(4)]

    wd_tiles = [wd_hbm[t * 128:(t + 1) * 128, q * 512:(q + 1) * 512]
                for t in range(D // 128) for q in range(DFF // 512)]
    stats_jobs = [lambda: pw_stats(gu_tiles(wg_hbm), 0),
                  lambda: pw_stats(gu_tiles(wu_hbm), 8),
                  lambda: pw_stats(wd_tiles, 16)]
    npre_seg = max(NSEG - 1, 1)
    done = 0
    for seg in range(NSEG):
        front(seg)
        if seg < NSEG - 1:
            upto = (seg + 1) * 3 // npre_seg
            for job in stats_jobs[done:upto]:
                job()
            done = upto
    for job in stats_jobs[done:]:
        job()

    for col, lo, n in ((0, 0, 8), (1, 8, 8), (2, 16, 8)):
        A("dve", lambda e, col=col, lo=lo, n=n: e.tensor_reduce(qsc[:, col:col + 1],
                                                                pwacc[:, lo:lo + n], AX.X, ALU.add),
          reads=["pwacc", "rms_sc"], writes=["rms_sc"])
    A("pe", lambda e: e.matmul(PS7[0:3, 0:1], qsc[:, 0:3], ones_cf[:, 0:1], start=True, stop=True),
      reads=["rms_sc", "ones_cf", "PS7"], writes=["PS7"])
    A("dve", lambda e: e.tensor_copy(coef_s[0:3, 0:1], PS7[0:3, 0:1]),
      reads=["PS7", "coef_s"], writes=["coef_s", "PS7"])
    A("act", lambda e: e.dma_start(out=row_dram[3:4, 0:3].rearrange("o (a u) -> (o a) u", a=3),
                                   in_=coef_s[0:3, 0:1]),
      reads=["coef_s"], writes=["rowch", "row_d4"], dma="Q_t")
    A("sp", lambda e: e.dma_start(out=qrow[0:1, 0:3], in_=row_dram[3:4, 0:3]),
      reads=["row_d4", "rms_sc"], writes=["rms_sc"], dma="Q_s")
    A("dve", lambda e: e.tensor_scalar(qrow[0:1, 0:4], qrow[0:1, 0:4], 1.0 / (DFF * D), 1e-5,
                                       ALU.mult, ALU.add),
      reads=["rms_sc"], writes=["rms_sc"])
    A("dve", lambda e: e.reciprocal(qrow[0:1, 4:8], qrow[0:1, 0:4]), reads=["rms_sc"], writes=["rms_sc"])
    A("pe", lambda e: e.matmul(PS7[0:128, 0:4], ones_rf[0:1, :], qrow[0:1, 4:8], start=True, stop=True),
      reads=["rms_sc", "ones_rf", "PS7"], writes=["PS7"])
    A("dve", lambda e: e.tensor_copy(qsc[:, 0:4], PS7[0:128, 0:4]),
      reads=["PS7", "rms_sc"], writes=["rms_sc", "PS7"])
    A("pe", lambda e: e.matmul(PS7[0:128, 4:7], ones_rf[0:1, :], qrow[0:1, 0:3], start=True, stop=True),
      reads=["rms_sc", "ones_rf", "PS7"], writes=["PS7"])
    A("dve", lambda e: e.tensor_copy(wsraw[:, 0:3], PS7[0:128, 4:7]),
      reads=["PS7", "rms_sc"], writes=["rms_sc", "PS7"])
    def q_gu(m, r):
        hbm = (wg_hbm, wu_hbm)[m]
        nat = (wqg_all, wqu_all)[m]
        kk = ("qg", "qu")[m]
        for t in range(DF8 // 128):
            row0 = (r * (DF8 // 128) + t) * 128
            if t % 2 == 0:
                zb = z_st[:, 0:D]
                zkeys = ["z_st", ("zs", 0), ("zs", 1), ("zs", 2), ("zs", 3)]
            else:
                zb = pwf[:, :].bitcast(BF16)
                zkeys = [("pwf", 0), ("pwf", 1)]
            for h in range(2):
                q = (t * 2 + h) % 4
                xq = x_st[:, q * 1024:(q + 1) * 1024]
                A("sp", lambda e, xq=xq, row0=row0, h=h, hbm=hbm: e.dma_start(
                    out=xq, in_=hbm[row0:row0 + 128, h * 1024:(h + 1) * 1024]),
                  reads=[("x_stq", q), ("x_st", q // 2)],
                  writes=[("x_stq", q), ("x_st", q // 2)], dma="Q_s")
                A("dve", lambda e, xq=xq, m=m: e.tensor_scalar(xq, xq, qsc[:, m:m + 1], CBIG,
                                                               ALU.mult, ALU.add),
                  reads=[("x_stq", q), "rms_sc"], writes=[("x_stq", q)])
                A("dve", lambda e, xq=xq: e.tensor_scalar(xq, xq, CBIG, None, ALU.subtract),
                  reads=[("x_stq", q)], writes=[("x_stq", q)])
                A("dve", lambda e, xq=xq, zb=zb, h=h: e.tensor_scalar(
                    zb[:, h * 1024:(h + 1) * 1024], xq, 1.0, -1.0, ALU.min, ALU.max),
                  reads=[("x_stq", q)] + zkeys, writes=zkeys)
            A("act", lambda e, nat=nat, row0=row0, zb=zb: e.dma_start(
                out=nat[row0:row0 + 128, :], in_=zb[:, 0:D]),
              reads=zkeys, writes=[kk + "ch", (kk, r, t)], dma="Q_t")

    def q_d(r):
        for t in range(D // 128):
            q = t % 4
            if t % 2 == 0:
                zb = z_st[:, 0:DD8]
                zkeys = ["z_st", ("zs", 0), ("zs", 1)]
            else:
                zb = pwf[:, :].bitcast(BF16)[:, 0:DD8]
                zkeys = [("pwf", 0), ("pwf", 1)]
            xq = x_st[:, q * 1024:(q + 1) * 1024]
            A("sp", lambda e, xq=xq, r=r, t=t: e.dma_start(
                out=xq, in_=wd_hbm[t * 128:(t + 1) * 128, r * DD8:(r + 1) * DD8]),
              reads=[("x_stq", q), ("x_st", q // 2)],
              writes=[("x_stq", q), ("x_st", q // 2)], dma="Q_s")
            A("dve", lambda e, xq=xq: e.tensor_scalar(xq, xq, qsc[:, 2:3], CBIG,
                                                      ALU.mult, ALU.add),
              reads=[("x_stq", q), "rms_sc"], writes=[("x_stq", q)])
            A("dve", lambda e, xq=xq: e.tensor_scalar(xq, xq, CBIG, None, ALU.subtract),
              reads=[("x_stq", q)], writes=[("x_stq", q)])
            A("dve", lambda e, xq=xq, zb=zb: e.tensor_scalar(zb, xq, 1.0, -1.0, ALU.min, ALU.max),
              reads=[("x_stq", q)] + zkeys, writes=zkeys)
            A("act", lambda e, r=r, t=t, zb=zb: e.dma_start(
                out=wqd_all[t * 128:(t + 1) * 128, r * DD8:(r + 1) * DD8], in_=zb),
              reads=zkeys, writes=["qdch", ("qd", r, t)], dma="Q_t")

    pw_units = ([lambda m=m, r=r: q_gu(m, r) for m in range(2) for r in range(DFF // DF8)]
                + [lambda r=r: q_d(r) for r in range(DFF // DD8)])
    PWL = (len(pw_units) + nch - 1) // nch

    for c in range(nch):
        p = c % 2
        A("sp", lambda e, c=c, p=p: e.dma_start(out=xs_db[:, p * DI:(p + 1) * DI],
                                                in_=xs_dram[c * 128:(c + 1) * 128, :]),
          reads=[("xs_d", c), ("xs_db", p)], writes=[("xs_db", p)], dma="Q_s")
        A("act", lambda e, c=c: e.dma_start(out=aflat_h[:], in_=ahl_dram[c:c + 1, :]),
          reads=["ahl_d"] + K_AFH, writes=K_AFH, dma="Q_t")
        A("act", lambda e, c=c: e.dma_start(out=aflat_l[:], in_=ahl_dram[nch + c:nch + c + 1, :]),
          reads=["ahl_d2"] + K_AFL, writes=K_AFL, dma="Q_t")
        A("pe", lambda e, c=c: e.matmul(PS4[:, 0:128], b_fmb[0:NS, c * CH:(c + 1) * CH],
                                        c_fmb[0:NS, c * CH:(c + 1) * CH], start=True, stop=True),
          reads=[("bf", c), ("cf", c), "PS4"], writes=["PS4"])
        A("dve", lambda e: e.tensor_copy(s_bf[:], PS4[:, 0:128]),
          reads=["PS4", "s_bf"], writes=["PS4", "s_bf"])
        for q in range(4):
            for i in range(4):
                A("pe", lambda e, q=q, i=i: e.matmul(
                    PB[:, i * 512:(i + 1) * 512], ones_rb[:],
                    aflat_h[0:1, q * 2048 + i * 512:q * 2048 + (i + 1) * 512],
                    start=True, stop=False),
                  reads=K_AFH + ["ones_rb"], writes=[("PB", i)])
                A("pe", lambda e, q=q, i=i: e.matmul(
                    PB[:, i * 512:(i + 1) * 512], ones_rb[:],
                    aflat_l[0:1, q * 2048 + i * 512:q * 2048 + (i + 1) * 512],
                    start=False, stop=False),
                  reads=K_AFL + ["ones_rb"], writes=[])
                A("pe", lambda e, q=q, i=i: e.matmul(
                    PB[:, i * 512:(i + 1) * 512].rearrange("p (r n) -> p r n", r=4),
                    triu[:], maskr[:, :].unsqueeze(1).broadcast_to((128, 4, 128)),
                    start=False, stop=True),
                  reads=["triu", "maskr"], writes=[("PB", i)])
            for hl in range(16):
                h = q * 16 + hl
                eb = e_bufs[:, (h % 2) * CH:(h % 2 + 1) * CH]
                gb = g_bufs[:, (h % 2) * CH:(h % 2 + 1) * CH]
                A("act", lambda e, c=c, hl=hl, h=h, eb=eb: e.activation(
                    eb, PB[:, hl * 128:(hl + 1) * 128], AF.Exp,
                    bias=ev_tm[:, c * H + h:c * H + h + 1]),
                  reads=[("PB", hl // 4), ("evt", c), ("ebuf", h % 2)], writes=[("ebuf", h % 2)])
                A("dve", lambda e, eb=eb, gb=gb: e.tensor_tensor(gb, s_bf[:], eb, ALU.mult),
                  reads=["s_bf", ("ebuf", h % 2), ("gbuf", h % 2)], writes=[("gbuf", h % 2)])
                pg, pgk = (PS5, "PS5") if (h // 8) % 2 == 0 else (PS6, "PS6")
                A("pe", lambda e, h=h, gb=gb, pg=pg, p=p: e.matmul(
                    pg[:, (h % 8) * 64:(h % 8 + 1) * 64], gb,
                    xs_db[:, p * DI + h * 64:p * DI + (h + 1) * 64], start=True, stop=True),
                  reads=[("gbuf", h % 2), ("xs_db", p), pgk],
                  writes=[pgk] if h % 8 == 7 else [])
                if h % 8 == 7:
                    gg = h // 8
                    A("dve", lambda e, c=c, gg=gg, p=p: e.tensor_tensor(
                        u2_bf[:, :].rearrange("p (r n) -> p r n", r=8),
                        xs_db[:, p * DI + gg * 512:p * DI + (gg + 1) * 512]
                        .rearrange("p (r n) -> p r n", r=8),
                        c0_tm[:, c * H + gg * 8:c * H + (gg + 1) * 8].unsqueeze(2)
                        .broadcast_to((128, 8, 64)), ALU.mult),
                      reads=[("xs_db", p), ("c0t", c), "scz"], writes=["scz"])
                    A("dve", lambda e, gg=gg, pg=pg, pgk=pgk: e.tensor_tensor(
                        y_st[:, gg * 512:(gg + 1) * 512], pg[:, :], u2_bf[:], ALU.add),
                      reads=["scz", pgk, "y_st"], writes=["y_st", pgk])
        A("act", lambda e, c=c: e.dma_start(out=y_dram[c * 128:(c + 1) * 128, :], in_=y_st[:, :]),
          reads=["y_st"], writes=["ych", ("y_d", c)], dma="Q_t")
        for u in pw_units[c * PWL:(c + 1) * PWL]:
            u()

    for c in range(nch):
        p = c % 2
        A("sp", lambda e, c=c, p=p: e.dma_start(out=xs_db[:, p * DI:(p + 1) * DI],
                                                in_=z_dram[c * 128:(c + 1) * 128, :]),
          reads=[("z_d", c), ("xs_db", p)], writes=[("xs_db", p)], dma="Q_s")
        A("sp", lambda e, c=c: e.dma_start(out=y_st[:, :], in_=y_dram[c * 128:(c + 1) * 128, :]),
          reads=[("y_d", c), "y_st"], writes=["y_st"], dma="Q_s")
        A("sp", lambda e, c=c: e.dma_start(out=prev_bf[:, :], in_=prev_dram[c * NS:(c + 1) * NS, :]),
          reads=[("prevd", c)] + K_PRV, writes=K_PRV, dma="Q_s")
        for gg in range(8):
            pg, pgk = (PS5, "PS5") if gg % 2 == 0 else (PS6, "PS6")
            A("pe", lambda e, c=c, gg=gg, pg=pg: e.matmul(
                pg[:, :], c_fmb[0:NS, c * CH:(c + 1) * CH],
                prev_bf[:, gg * 512:(gg + 1) * 512], start=True, stop=True),
              reads=K_PRV + [("cf", c), pgk], writes=[pgk])
            A("dve", lambda e, c=c, gg=gg, pg=pg, pgk=pgk: e.tensor_tensor(
                u2_bf[:, :].rearrange("p (r n) -> p r n", r=8),
                pg[:, :].rearrange("p (r n) -> p r n", r=8),
                qd_tm[:, c * H + gg * 8:c * H + (gg + 1) * 8].unsqueeze(2).broadcast_to((128, 8, 64)),
                ALU.mult),
              reads=[pgk, ("qdt", c), "scz"], writes=["scz", pgk])
            A("dve", lambda e, gg=gg: e.tensor_tensor(y_st[:, gg * 512:(gg + 1) * 512],
                                                      y_st[:, gg * 512:(gg + 1) * 512],
                                                      u2_bf[:], ALU.add),
              reads=["scz", "y_st"], writes=["y_st"])
        for gg in range(8):
            A("dve", lambda e, gg=gg: e.tensor_tensor(
                x_st[:, 0:512], y_st[:, gg * 512:(gg + 1) * 512], y_st[:, gg * 512:(gg + 1) * 512],
                ALU.mult),
              reads=["y_st", ("x_st", 0)], writes=[("x_st", 0)])
            A("dve", lambda e, gg=gg: e.tensor_reduce(
                (rw1[:, 0:1] if gg == 0 else rw2[:, 0:1]), x_st[:, 0:512], AX.X, ALU.add),
              reads=[("x_st", 0), "rw1", "rw2"], writes=[("rw1" if gg == 0 else "rw2")])
            if gg > 0:
                A("dve", lambda e: e.tensor_tensor(rw1[:], rw1[:], rw2[:], ALU.add),
                  reads=["rw1", "rw2"], writes=["rw1"])
        A("dve", lambda e: e.tensor_scalar(rw1[:], rw1[:], 1.0 / DI, 1e-6, ALU.mult, ALU.add),
          reads=["rw1"], writes=["rw1"])
        A("dve", lambda e: e.reciprocal(rw2[:], rw1[:]), reads=["rw1", "rw2"], writes=["rw2"])
        A("act", lambda e: e.activation(rw2[:], rw2[:], AF.Sqrt), reads=["rw2"], writes=["rw2"])
        A("act", lambda e, p=p: e.activation(xs_st[:, :], xs_db[:, p * DI:(p + 1) * DI], AF.Silu),
          reads=[("xs_db", p), "xs_st"], writes=["xs_st"])
        A("dve", lambda e: e.tensor_scalar(y_st[:, :], y_st[:, :], rw2[:], None, ALU.mult),
          reads=["y_st", "rw2"], writes=["y_st"])
        A("dve", lambda e: e.tensor_tensor(y_st[:, :], y_st[:, :], xs_st[:, :], ALU.mult),
          reads=["y_st", "xs_st"], writes=["y_st"])
        A("act", lambda e, c=c: e.dma_start(out=y_dram[c * 128:(c + 1) * 128, :], in_=y_st[:, :]),
          reads=["y_st"], writes=["ych", ("y_d", c)], dma="Q_t")

    def psg(t):
        return PB[:, (t % 4) * 512:(t % 4 + 1) * 512] if t < 4 else (PS4, PS5, PS6, PS7)[t - 4][:, :]

    def psk(t):
        return ("PB", t) if t < 4 else ("PS4", "PS5", "PS6", "PS7")[t - 4]

    for j in range(32):
        A("act", lambda e, j=j: e.dma_start_transpose(y_fm[:, j * t_loc:(j + 1) * t_loc],
                                                      y_dram[:, j * 128:(j + 1) * 128]),
          reads=[("y_d", c) for c in range(nch)] + K_YFM[j], writes=K_YFM[j], dma="Q_t")
    for dgr in range(4):
        for j in range(32):
            A("pool", lambda e, dgr=dgr, j=j: e.dma_start(
                out=wo_buf[:, (j % 2) * 512:(j % 2) * 512 + 512],
                in_=w_out[j * 128:(j + 1) * 128, dgr * 512:(dgr + 1) * 512]),
              reads=K_WO[j % 2], writes=K_WO[j % 2] + [("wo_l", j % 2)], dma="Q_x")
            for t in range(ntt):
                A("pe", lambda e, j=j, t=t: e.matmul(
                    psg(t), y_fm[:, j * t_loc + t * 128:j * t_loc + (t + 1) * 128],
                    wo_buf[:, (j % 2) * 512:(j % 2) * 512 + 512],
                    start=(j == 0), stop=(j == 31)),
                  reads=K_YFM[j] + K_WO[j % 2] + [("wo_l", j % 2)] + ([psk(t)] if j == 0 else []),
                  writes=[psk(t)] if j == 31 else [])
        for t in range(ntt):
            p = t % 2
            A("sp", lambda e, dgr=dgr, t=t, p=p: e.dma_start(
                out=x_st[:, p * D + dgr * 512:p * D + (dgr + 1) * 512],
                in_=x_sl[t * 128:(t + 1) * 128, dgr * 512:(dgr + 1) * 512]),
              reads=[("x_st", p)], writes=[("x_st", p)], dma="Q_s")
            A("dve", lambda e, dgr=dgr, t=t, p=p: e.tensor_tensor(
                x_st[:, p * D + dgr * 512:p * D + (dgr + 1) * 512],
                x_st[:, p * D + dgr * 512:p * D + (dgr + 1) * 512], psg(t), ALU.add),
              reads=[("x_st", p), psk(t)], writes=[("x_st", p), psk(t)])
            A("act", lambda e, dgr=dgr, t=t, p=p: e.dma_start(
                out=gu_dram[t * 128:(t + 1) * 128, dgr * 512:(dgr + 1) * 512],
                in_=x_st[:, p * D + dgr * 512:p * D + (dgr + 1) * 512]),
              reads=[("x_st", p)], writes=[("r1ch", t % 2), ("r1_d", t, dgr)], dma="Q_t")
    for c in range(ntt):
        p = c % 2
        A("sp", lambda e, c=c, p=p: e.dma_start(out=x_st[:, p * D:(p + 1) * D],
                                                in_=gu_dram[c * 128:(c + 1) * 128, 0:D]),
          reads=[("r1_d", c, dgr) for dgr in range(4)] + [("x_st", p)],
          writes=[("x_st", p)], dma="Q_s")
        for bg in range(4):
            A("dve", lambda e, p=p, bg=bg: e.bn_stats(
                sc_a[:, bg * 6:(bg + 1) * 6], x_st[:, p * D + bg * 512:p * D + (bg + 1) * 512]),
              reads=[("x_st", p), "scz"], writes=["scz"])
        A("dve", lambda e: e.bn_aggr(sc_b[:, 0:2], sc_a[:, 0:24]), reads=["scz"], writes=["scz"])
        A("dve", lambda e: e.tensor_scalar(rw1[:], sc_b[:, 1:2], 1.0, 1e-5, ALU.mult, ALU.add),
          reads=["scz", "rw1"], writes=["rw1"])
        A("dve", lambda e: e.reciprocal(rw2[:], rw1[:]), reads=["rw1", "rw2"], writes=["rw2"])
        A("act", lambda e: e.activation(rw2[:], rw2[:], AF.Sqrt), reads=["rw2"], writes=["rw2"])
        A("dve", lambda e, p=p: e.tensor_scalar(x_st[:, p * D:(p + 1) * D], x_st[:, p * D:(p + 1) * D],
                                                sc_b[:, 0:1], rw2[:], ALU.subtract, ALU.mult),
          reads=[("x_st", p), "scz", "rw2"], writes=[("x_st", p)])
        A("act", lambda e, c=c, p=p: e.dma_start(out=ln1_dram[c * 128:(c + 1) * 128, :],
                                                 in_=x_st[:, p * D:(p + 1) * D]),
          reads=[("x_st", p)], writes=["lnch", ("ln1_d", c)], dma="Q_t")
        A("dve", lambda e, p=p: e.tensor_reduce(rw1[:], x_st[:, p * D:(p + 1) * D], AX.X, ALU.max,
                                                apply_absolute_value=True),
          reads=[("x_st", p), "rw1"], writes=["rw1"])
        A("dve", lambda e: e.tensor_scalar(rw1[:], rw1[:], 1.0 / 127.0, 1e-5, ALU.mult, ALU.add),
          reads=["rw1"], writes=["rw1"])
        A("dve", lambda e, c=c: e.tensor_copy(xscal[:, c:c + 1], rw1[:]),
          reads=["rw1", "xscal"], writes=["xscal"])
        A("dve", lambda e: e.reciprocal(rw2[:], rw1[:]), reads=["rw1", "rw2"], writes=["rw2"])
        A("dve", lambda e, p=p: e.tensor_scalar(x_st[:, p * D:(p + 1) * D], x_st[:, p * D:(p + 1) * D],
                                                rw2[:], CBIG, ALU.mult, ALU.add),
          reads=[("x_st", p), "rw2"], writes=[("x_st", p)])
        A("dve", lambda e, p=p: e.tensor_scalar(z_st[:, 0:D], x_st[:, p * D:(p + 1) * D],
                                                CBIG, None, ALU.subtract),
          reads=[("x_st", p), "z_st", ("zs", 0), ("zs", 1), ("zs", 2), ("zs", 3)],
          writes=["z_st", ("zs", 0), ("zs", 1), ("zs", 2), ("zs", 3)])
        A("act", lambda e, c=c: e.dma_start(out=xbf_dram[c * 128:(c + 1) * 128, :], in_=z_st[:, 0:D]),
          reads=["z_st"], writes=["xqch", ("xq_d", c)], dma="Q_t")
    for j in range(nk):
        A("act", lambda e, j=j: e.dma_start_transpose(xq_fm[:, j * t_loc:(j + 1) * t_loc],
                                                      xbf_dram[:, j * 128:(j + 1) * 128]),
          reads=[("xq_d", c) for c in range(ntt)] + K_XQ[j], writes=K_XQ[j], dma="Q_t")

    def psg(t):
        return PB[:, (t % 4) * 512:(t % 4 + 1) * 512] if t < 4 else (PS4, PS5, PS6, PS7)[t - 4][:, :]

    def psk(t):
        return ("PB", t) if t < 4 else ("PS4", "PS5", "PS6", "PS7")[t - 4]

    A("dve", lambda e: e.memset(runmax[:], 0.0), reads=["rms_sc"], writes=["rms_sc"])
    for t in range(ntt):
        A("dve", lambda e, t=t: e.tensor_tensor(xscal[:, 8 + t:9 + t], xscal[:, t:t + 1],
                                                wsraw[:, 0:1], ALU.mult),
          reads=["xscal", "rms_sc"], writes=["xscal"])
        A("dve", lambda e, t=t: e.tensor_tensor(xscal[:, 16 + t:17 + t], xscal[:, t:t + 1],
                                                wsraw[:, 1:2], ALU.mult),
          reads=["xscal", "rms_sc"], writes=["xscal"])
    for ig in range(16):
        for mm, kk in ((0, "qg"), (1, "qu")):
            alln = wqg_all if mm == 0 else wqu_all
            for j in range(nk):
                A("act", lambda e, alln=alln, j=j, ig=ig: e.dma_start_transpose(
                    wff_buf[:, (j % 2) * 4096:(j % 2) * 4096 + 512],
                    alln[ig * 512:(ig + 1) * 512, j * 128:(j + 1) * 128]),
                  reads=[(kk, (ig * 512) // DF8, ((ig * 512) % DF8) // 128 + dtt)
                         for dtt in range(4)] + K_WF[j % 2],
                  writes=K_WF[j % 2] + [("wff_l", j)], dma="Q_x")
                for t in range(ntt):
                    A("pe", lambda e, j=j, t=t: e.matmul(
                        psg(t), xq_fm[:, j * t_loc + t * 128:j * t_loc + (t + 1) * 128],
                        wff_buf[:, (j % 2) * 4096:(j % 2) * 4096 + 512],
                        start=(j == 0), stop=(j == nk - 1)),
                      reads=K_XQ[j] + K_WF[j % 2] + [("wff_l", j)] + ([psk(t)] if j == 0 else []),
                      writes=[psk(t)] if j == nk - 1 else [])
            if mm == 0:
                for t in range(ntt):
                    A("act", lambda e, t=t: e.activation(
                        xs_db[:, :].bitcast(F32)[:, t * 512:(t + 1) * 512], psg(t), AF.Silu,
                        scale=xscal[:, 8 + t:9 + t]),
                      reads=[psk(t), "xs_db_f", "xscal", ("xs_db", 0), ("xs_db", 1)],
                      writes=[psk(t), "xs_db_f", ("xs_db", 0), ("xs_db", 1)])
            else:
                for t in range(ntt):
                    A("dve", lambda e, t=t: e.scalar_tensor_tensor(
                        x_st[:, (t % 2) * D:(t % 2) * D + 512], psg(t), xscal[:, 16 + t:17 + t],
                        xs_db[:, :].bitcast(F32)[:, t * 512:(t + 1) * 512], ALU.mult, ALU.mult),
                      reads=[psk(t), "xs_db_f", "xscal", ("x_st", t % 2)],
                      writes=[("x_st", t % 2), psk(t)])
                    A("dve", lambda e, t=t: e.tensor_reduce(
                        rw1[:], x_st[:, (t % 2) * D:(t % 2) * D + 512], AX.X, ALU.max,
                        apply_absolute_value=True),
                      reads=[("x_st", t % 2), "rw1"], writes=["rw1"])
                    A("dve", lambda e, t=t: e.tensor_tensor(runmax[:, t:t + 1], runmax[:, t:t + 1],
                                                            rw1[:], ALU.max),
                      reads=["rw1", "rms_sc"], writes=["rms_sc"])
                    A("act", lambda e, t=t, ig=ig: e.dma_start(
                        out=gu_dram[t * 128:(t + 1) * 128, ig * 512:(ig + 1) * 512],
                        in_=x_st[:, (t % 2) * D:(t % 2) * D + 512]),
                      reads=[("x_st", t % 2), ("r1_d2",)], writes=[("guch", t % 4), ("gu_d", ig, t)], dma="Q_t")

    A("dve", lambda e: e.tensor_scalar(runmax[:], runmax[:], 1.0 / 127.0, 1e-5, ALU.mult, ALU.add),
      reads=["rms_sc"], writes=["rms_sc"])
    for t in range(ntt):
        A("dve", lambda e, t=t: e.tensor_tensor(xscal[:, 24 + t:25 + t], runmax[:, t:t + 1],
                                                wsraw[:, 2:3], ALU.mult),
          reads=["xscal", "rms_sc"], writes=["xscal"])
    A("dve", lambda e: e.reciprocal(wqacc[:, 0:8], runmax[:]), reads=["rms_sc", "rms_sc"],
      writes=["rms_sc"])
    for t in range(ntt):
        for ig in range(4):
            p = ig % 2
            A("sp", lambda e, t=t, ig=ig, p=p: e.dma_start(
                out=x_st[:, p * D:(p + 1) * D],
                in_=gu_dram[t * 128:(t + 1) * 128, ig * D:(ig + 1) * D]),
              reads=[("gu_d", gg, t) for gg in range(ig * 4, ig * 4 + 4)] + [("x_st", p)],
              writes=[("x_st", p)], dma="Q_s")
            A("dve", lambda e, t=t, p=p: e.tensor_scalar(x_st[:, p * D:(p + 1) * D],
                                                         x_st[:, p * D:(p + 1) * D],
                                                         wqacc[:, t:t + 1], CBIG, ALU.mult, ALU.add),
              reads=[("x_st", p), "rms_sc"], writes=[("x_st", p)])
            A("dve", lambda e, p=p: e.tensor_scalar(z_st[:, 0:D], x_st[:, p * D:(p + 1) * D],
                                                    CBIG, None, ALU.subtract),
              reads=[("x_st", p), "z_st"], writes=["z_st"])
            A("act", lambda e, t=t, ig=ig: e.dma_start(
                out=guq_dram[t * 128:(t + 1) * 128, ig * D:(ig + 1) * D], in_=z_st[:, 0:D]),
              reads=["z_st"], writes=["guqch", ("guq_d", t, ig)], dma="Q_t")

    GUQD = [("guq_d", t, ig) for t in range(8) for ig in range(4)]
    for qq in range(NQ):
        for j in range(DFF // 128):
            A("act", lambda e, j=j, qq=qq: e.dma_start_transpose(
                guq_fm[:, j * TQ:(j + 1) * TQ],
                guq_dram[qq * TQ:(qq + 1) * TQ, j * 128:(j + 1) * 128]),
              reads=GUQD + K_GQ[j], writes=K_GQ[j], dma="Q_t")
        for j in range(DFF // 128):
            A("act", lambda e, j=j: e.dma_start_transpose(
                wff_buf[:, (j % 2) * 4096 + 2048:(j % 2) * 4096 + 4096],
                wqd_all[0:D, j * 128:(j + 1) * 128]),
              reads=[("qd", (j * 128) // DD8, tt) for tt in range(D // 128)] + K_WF[j % 2],
              writes=K_WF[j % 2] + [("wfd_l", j)], dma="Q_x")
            for t in range(TQ // 128):
                for dgr in range(4):
                    bk = (t * 4 + dgr)
                    A("pe", lambda e, j=j, t=t, dgr=dgr, qq=qq, bk=bk: e.matmul(
                        psg(bk), guq_fm[:, j * TQ + t * 128:j * TQ + (t + 1) * 128],
                        wff_buf[:, (j % 2) * 4096 + 2048 + dgr * 512:(j % 2) * 4096 + 2048 + (dgr + 1) * 512],
                        start=(j == 0), stop=(j == DFF // 128 - 1)),
                      reads=K_GQ[j] + K_WF[j % 2] + [("wfd_l", j)] + ([psk(bk)] if j == 0 else []),
                      writes=[psk(bk)] if j == DFF // 128 - 1 else [])
        for t in range(TQ // 128):
            c = qq * (TQ // 128) + t
            p = t % 2
            A("sp", lambda e, c=c, p=p: e.dma_start(out=x_st[:, p * D:(p + 1) * D],
                                                    in_=ln1_dram[c * 128:(c + 1) * 128, :]),
              reads=[("ln1_d", c), ("x_st", p)], writes=[("x_st", p)], dma="Q_s")
            for dgr in range(4):
                bk = t * 4 + dgr
                A("dve", lambda e, t=t, dgr=dgr, p=p, bk=bk, c=c: e.scalar_tensor_tensor(
                    x_st[:, p * D + dgr * 512:p * D + (dgr + 1) * 512],
                    psg(bk), xscal[:, 24 + c:25 + c],
                    x_st[:, p * D + dgr * 512:p * D + (dgr + 1) * 512], ALU.mult, ALU.add),
                  reads=[("x_st", p), psk(bk), "xscal"], writes=[("x_st", p), psk(bk)])
            for bg in range(4):
                A("dve", lambda e, p=p, bg=bg: e.bn_stats(
                    sc_a[:, bg * 6:(bg + 1) * 6], x_st[:, p * D + bg * 512:p * D + (bg + 1) * 512]),
                  reads=[("x_st", p), "scz"], writes=["scz"])
            A("dve", lambda e: e.bn_aggr(sc_b[:, 0:2], sc_a[:, 0:24]),
              reads=["scz", "scz"], writes=["scz"])
            A("dve", lambda e: e.tensor_scalar(rw1[:], sc_b[:, 1:2], 1.0, 1e-5, ALU.mult, ALU.add),
              reads=["scz", "rw1"], writes=["rw1"])
            A("dve", lambda e: e.reciprocal(rw2[:], rw1[:]), reads=["rw1", "rw2"], writes=["rw2"])
            A("act", lambda e: e.activation(rw2[:], rw2[:], AF.Sqrt), reads=["rw2"], writes=["rw2"])
            A("dve", lambda e, p=p: e.tensor_scalar(x_st[:, p * D:(p + 1) * D],
                                                    x_st[:, p * D:(p + 1) * D],
                                                    sc_b[:, 0:1], rw2[:], ALU.subtract, ALU.mult),
              reads=[("x_st", p), "scz", "rw2"], writes=[("x_st", p)])
            A("pool", lambda e, c=c, p=p: e.dma_start(out=out_hbm[c * 128:(c + 1) * 128, :],
                                                      in_=x_st[:, p * D:(p + 1) * D]),
              reads=[("x_st", p)], writes=["outch", ("out_d", c)], dma="Q_o")

    def dmp(name, src, rkeys):
        if name in dbg_out:
            A("pool", lambda e, s=src: e.dma_start(out=dbg_out[name], in_=s), reads=rkeys, writes=["dbgch"], dma="Q_o")

    dmp("dtl", dtl_fm[:], ["dtl"])
    if "xfm0" in dbg_out:
        A("pool", lambda e: e.dma_start(out=dbg_out["xfm0"], in_=x_fm[:, 0:t_loc]),
          reads=K_XFM[0], writes=["dbgch"], dma="Q_o")
    dmp("dt", dt_loc[:], ["dt"])
    dmp("u", u_fm[:], ["u"])
    dmp("aloc", aloc_fm[:], ["aloc"])
    dmp("cum", cum_l[:], ["cum"])
    dmp("ev", ev_fm[:], ["ev"])
    if "xsilu" in dbg_out:
        for c in range(nch):
            A("pool", lambda e, c=c: e.dma_start(out=dbg_out["xsilu"][c * 128:(c + 1) * 128, :],
                                                 in_=xs_dram[c * 128:(c + 1) * 128, :].bitcast(BF16))
              if False else e.dma_start(out=dbg_out["xsilu"][c * 128:(c + 1) * 128, :],
                                        in_=xs_dram[c * 128:(c + 1) * 128, :]),
              reads=[("xs_d", c)], writes=["dbgch"], dma="Q_o")
    if "zdmp" in dbg_out:
        for c in range(nch):
            A("pool", lambda e, c=c: e.dma_start(out=dbg_out["zdmp"][c * 128:(c + 1) * 128, :],
                                                 in_=z_dram[c * 128:(c + 1) * 128, :]),
              reads=[("z_d", c)], writes=["dbgch"], dma="Q_o")
    if "guf" in dbg_out:
        for t in range(ntt):
            A("pool", lambda e, t=t: e.dma_start(out=dbg_out["guf"][t * 128:(t + 1) * 128, :],
                                                 in_=gu_dram[t * 128:(t + 1) * 128, :]),
              reads=[("gu_d", ig, t) for ig in range(16)], writes=["dbgch"], dma="Q_o")
    if "xqd" in dbg_out:
        for c in range(ntt):
            A("pool", lambda e, c=c: e.dma_start(out=dbg_out["xqd"][c * 128:(c + 1) * 128, :],
                                                 in_=xbf_dram[c * 128:(c + 1) * 128, :]),
              reads=[("xq_d", c)], writes=["dbgch"], dma="Q_o")
    if "qrowd" in dbg_out:
        A("pool", lambda e: e.dma_start(out=dbg_out["qrowd"], in_=qrow[0:1, 0:8]),
          reads=["rms_sc"], writes=["dbgch"], dma="Q_o")
    if "ymix" in dbg_out:
        for c in range(nch):
            A("pool", lambda e, c=c: e.dma_start(out=dbg_out["ymix"][c * 128:(c + 1) * 128, :],
                                                 in_=y_dram[c * 128:(c + 1) * 128, :]),
              reads=[("y_d", c)], writes=["dbgch"], dma="Q_o")
    if "ln1" in dbg_out:
        for c in range(ntt):
            A("pool", lambda e, c=c: e.dma_start(out=dbg_out["ln1"][c * 128:(c + 1) * 128, :],
                                                 in_=ln1_dram[c * 128:(c + 1) * 128, :]),
              reads=[("ln1_d", c)], writes=["dbgch"], dma="Q_o")

    sems = {}
    sem_ctx = []
    names = ["S_" + e for e in Emitter.ENGS] + sorted(em.dma_counts.keys())
    for nm in names:
        s = nc.semaphore(nm)
        sem_ctx.append(s)
        sems[nm] = s.__enter__()
    finals = {("S_" + e): c for e, c in em.counts.items() if c}
    finals.update({k: v for k, v in em.dma_counts.items() if v})
    per_eng = {e: [] for e in Emitter.ENGS}
    for (eng, fn, waits, sem, inc) in em.steps:
        per_eng[eng].append((fn, waits, sem, inc))
    blk = nc.Block()
    block = blk.__enter__()

    def runner(e):
        def run(obj):
            for (fn, waits, sem, inc) in per_eng[e]:
                for (s, cc) in waits:
                    obj.wait_ge(sems[s], cc)
                fn(obj).then_inc(sems[sem], inc)
            if e == "pool":
                for s, cc in sorted(finals.items()):
                    obj.wait_ge(sems[s], cc)
        return run

    block.tensor(runner("pe"))
    block.scalar(runner("act"))
    block.vector(runner("dve"))
    block.gpsimd(runner("pool"))
    block.sync(runner("sp"))
    blk.__exit__(None, None, None)
    for s in reversed(sem_ctx):
        s.__exit__(None, None, None)
    for t in reversed(ctx):
        t.__exit__(None, None, None)
    return nc


def make_core_inputs(core, full, t_loc, ncores):
    cps = ncores // 2
    b, slot = core // cps, core % cps
    x = np.asarray(full["L0_attn_x"])
    t0 = slot * t_loc
    d = {}
    d["x_sl"] = np.ascontiguousarray(x[b, t0:t0 + t_loc, :])
    npre = max(cps - 1, 1)
    xp = np.zeros((npre * t_loc, D), np.float32)
    if t0 > 0:
        xp[npre * t_loc - t0:] = x[b, 0:t0, :]
    d["x_pre"] = xp
    if t0 + t_loc < x.shape[1]:
        d["x_ex"] = np.ascontiguousarray(x[b, t0 + t_loc:t0 + t_loc + 1, :])
        bm = 1.0
    else:
        d["x_ex"] = np.zeros((1, D), np.float32)
        bm = 0.0
    d["w_in"] = np.asarray(full["L0_attn_W_in"])
    d["w_out"] = np.asarray(full["L0_attn_W_out"])
    d["wg_hbm"] = np.ascontiguousarray(np.asarray(full["L0_ffn_w_gate"]))
    d["wu_hbm"] = np.ascontiguousarray(np.asarray(full["L0_ffn_w_up"]))
    d["wd_hbm"] = np.ascontiguousarray(np.asarray(full["L0_ffn_w_down"]))
    d["a_log"] = np.asarray(full["L0_attn_A_log"]).reshape(H, 1).astype(np.float32)
    d["dt_bias"] = np.asarray(full["L0_attn_dt_bias"]).reshape(H, 1).astype(np.float32)
    d["bnb_in"] = np.asarray(full["L0_attn_B_norm_bias"]).reshape(1, NS).astype(np.float32)
    d["cnb_in"] = np.asarray(full["L0_attn_C_norm_bias"]).reshape(1, NS).astype(np.float32)
    d["identf_in"] = np.eye(128, dtype=np.float32)
    d["triu_in"] = np.triu(np.ones((128, 128), np.float32))
    d["maskr_in"] = (-1e4 * np.eye(128)).astype(np.float32)
    d["bmask_in"] = np.full((H, 1), bm, np.float32)
    return d


def kernel(**inputs):
    t_loc, ncores = 1024, NC
    nc = build_program(t_loc=t_loc, ncores=ncores)
    in_maps = [make_core_inputs(c, inputs, t_loc, ncores) for c in range(ncores)]
    res = run_bass_kernel_spmd(nc, in_maps, core_ids=list(range(ncores)))
    out = np.zeros((2, 4096, D), np.float32)
    cps = ncores // 2
    for c in range(ncores):
        b, slot = c // cps, c % cps
        out[b, slot * t_loc:(slot + 1) * t_loc, :] = res.results[c]["out"]
    return out


NC = 8


# version 09
# speedup vs baseline: 1.2815x; 1.0184x over previous
import numpy as np
import concourse.bass as bass
import concourse.mybir as mybir
from concourse.bass_utils import run_bass_kernel_spmd

F32 = mybir.dt.float32
BF16 = mybir.dt.bfloat16
I32 = mybir.dt.int32
ALU = mybir.AluOpType
AF = mybir.ActivationFunctionType
AX = mybir.AxisListType

D = 2048
DI = 4096
H = 64
NS = 64
DP = 8480
DFF = 8192
CBIG = 12582912.0
TWO_PI = float(2 * np.pi)
PI = float(np.pi)
OFF_B = 2 * DI
OFF_DT = 2 * DI + 2 * NS
OFF_TH = OFF_DT + 2 * H


class Emitter:
    ENGS = ("pe", "act", "dve", "pool", "sp")

    def __init__(self):
        self.steps = []
        self.writers = {}
        self.readers = {}
        self.counts = {e: 0 for e in self.ENGS}
        self.dma_counts = {}
        self.known = {e: {} for e in self.ENGS}

    def add(self, eng, fn, reads=(), writes=(), dma=None):
        need = {}
        for k in reads:
            w = self.writers.get(k)
            if w:
                need[w[0]] = max(need.get(w[0], 0), w[1])
        for k in writes:
            w = self.writers.get(k)
            if w:
                need[w[0]] = max(need.get(w[0], 0), w[1])
            for s, c in self.readers.get(k, ()):
                need[s] = max(need.get(s, 0), c)
        kn = self.known[eng]
        waits = []
        for s, c in sorted(need.items(), key=str):
            if kn.get(s, 0) < c:
                waits.append((s, c))
                kn[s] = c
        if dma is None:
            self.counts[eng] += 1
            sem, cnt, inc = "S_" + eng, self.counts[eng], 1
        else:
            key = writes[0] if writes else reads[0]
            dma = "D" + ("s" if eng == "pool" else "h") + repr(key)
            self.dma_counts[dma] = self.dma_counts.get(dma, 0) + 16
            sem, cnt, inc = dma, self.dma_counts[dma], 16
        self.steps.append((eng, fn, waits, sem, inc))
        for k in reads:
            self.readers.setdefault(k, []).append((sem, cnt))
        for k in writes:
            self.writers[k] = (sem, cnt)
            self.readers[k] = []


def build_program(t_loc=1024, ncores=8, dbg=()):
    CH = 128
    nch = t_loc // CH
    ntt = nch
    nk = 16
    TL1 = t_loc + 1
    ntg = t_loc // 512
    DF8 = DFF // 8
    DD8 = DFF // 8
    NQ = 4
    TQ = t_loc // NQ
    NSEG = max(ncores // 2, 1)
    assert DD8 <= D and DF8 <= DFF
    nc = bass.Bass("TRN2", target_bir_lowering=False)
    em = Emitter()
    A = em.add
    ctx = []

    def sbuf(name, shape, dtype):
        t = nc.sbuf_tensor(name, shape, dtype)
        ctx.append(t)
        return t.__enter__()

    def psumt(name, shape, dtype):
        t = nc.psum_tensor(name, shape, dtype)
        ctx.append(t)
        return t.__enter__()

    def din(name, shape, dt=F32, kind="ExternalInput"):
        return nc.dram_tensor(name, shape, dt, kind=kind).ap()

    def dit(name, shape, dt, shared=False):
        if shared:
            return nc.dram_tensor(name, shape, dt, addr_space="Shared").ap()
        return nc.dram_tensor(name, shape, dt).ap()

    x_sl = din("x_sl", [t_loc, D])
    x_pre = din("x_pre", [max(NSEG - 1, 1) * t_loc, D])
    x_ex = din("x_ex", [1, D])
    w_in = din("w_in", [D, DP])
    w_out = din("w_out", [DI, D])
    wg_hbm = din("wg_hbm", [DFF, D])
    wu_hbm = din("wu_hbm", [DFF, D])
    wd_hbm = din("wd_hbm", [D, DFF])
    a_log = din("a_log", [H, 1])
    dt_bias = din("dt_bias", [H, 1])
    bnb_in = din("bnb_in", [1, NS])
    cnb_in = din("cnb_in", [1, NS])
    identf_in = din("identf_in", [128, 128])
    triu_in = din("triu_in", [128, 128])
    maskr_in = din("maskr_in", [128, 128])
    bmask_in = din("bmask_in", [H, 1])
    out_hbm = din("out", [t_loc, D], F32, "ExternalOutput")

    xbf_dram = dit("xbf_dram", [t_loc, D], BF16)
    ahl_dram = dit("ahl_dram", [2 * nch, H * CH], BF16)
    z_dram = dit("z_dram", [t_loc, DI], BF16)
    xs_dram = dit("xs_dram", [t_loc, DI], BF16)
    prev_dram = dit("prev_dram", [nch * NS, DI], BF16)
    y_dram = dit("y_dram", [t_loc, DI], BF16)
    ln1_dram = dit("ln1_dram", [t_loc, D], F32)
    gu_dram = dit("gu_dram", [t_loc, DFF], F32)
    guq_dram = dit("guq_dram", [t_loc, DFF], BF16)
    wqt_dram = dit("wqt_dram", [max(DF8, D), max(D, DD8)], BF16)
    row_dram = dit("row_dram", [4, 256], F32)
    wqg_all = dit("wqg_all", [DFF, D], BF16)
    wqu_all = dit("wqu_all", [DFF, D], BF16)
    wqd_all = dit("wqd_all", [D, DFF], BF16)

    dbg_out = {}
    for name, shape in dbg:
        dbg_out[name] = din("dbg_" + name, list(shape), F32, "ExternalOutput")

    identf = sbuf("identf", [64, 64], F32)
    identb = sbuf("identb", [128, 128], BF16)
    triu = sbuf("triu", [128, 128], BF16)
    maskr = sbuf("maskr", [128, 128], BF16)
    ones_rb = sbuf("ones_rb", [1, 128], BF16)
    ones_rf = sbuf("ones_rf", [1, 128], F32)
    ones_cf = sbuf("ones_cf", [128, 1], F32)
    aneg = sbuf("aneg", [H, 1], F32)
    alog_s = sbuf("alog_s", [H, 1], F32)
    dtb_s = sbuf("dtb_s", [H, 1], F32)
    bnb_m = sbuf("bnb_m", [128, NS], F32)
    cnb_m = sbuf("cnb_m", [128, NS], F32)
    bmask_s = sbuf("bmask_s", [H, 1], F32)
    AR = sbuf("AR", [128, 49152], BF16)
    ARf = AR[:, :].bitcast(F32)
    ARi = AR[:, :].bitcast(I32)
    _p2b = 33 * 512
    rr_f = ARf[0:32, _p2b:_p2b + t_loc]
    rr_i = ARi[0:32, _p2b + t_loc:_p2b + 2 * t_loc]
    rr_g = rr_f
    rr_m = ARf[0:32, _p2b + t_loc:_p2b + 2 * t_loc]
    cum_l = ARf[0:32, _p2b + 2 * t_loc:_p2b + 3 * t_loc]
    cs_bf = AR[0:32, 2 * (_p2b + 3 * t_loc):2 * (_p2b + 3 * t_loc) + 2 * t_loc]
    th_sg = ARf[0:32, _p2b + 4 * t_loc:_p2b + 4 * t_loc + 512]
    P2B_KEYS = [("AR", b) for b in range(33, 33 + (4 * t_loc + 512 + 511) // 512)]
    dtl_fm = ARf[:, _p2b + 4 * t_loc + 512:_p2b + 5 * t_loc + 513]
    wsm = AR[:, 46336:46336 + nk * 160]
    ev_tm = ARf[:, 41 * 512:41 * 512 + nch * H]
    qd_tm = AR[:, 43008:43008 + nch * H]
    ws_tm = AR[:, 43008 + 512:43008 + 512 + nch * H]
    c0_tm = AR[:, 44032:44032 + nch * H]
    aflat_h = AR[0:1, 0:H * CH]
    K_AFH = [("AR", b) for b in range((H * CH + 1023) // 1024)]
    aflat_l = AR[0:1, 8 * 1024:8 * 1024 + H * CH]
    K_AFL = [("AR", b) for b in range(8, 8 + (H * CH + 1023) // 1024)]
    x_st = sbuf("x_st", [128, 2 * D], F32)
    z_st = sbuf("z_st", [128, D], BF16)
    xs_st = sbuf("xs_st", [128, DI], BF16)
    y_st = sbuf("y_st", [128, DI], BF16)
    xs_db = sbuf("xs_db", [128, 2 * DI], BF16)
    xe_fm = sbuf("xe_fm", [128, 16], BF16)
    dt_loc = sbuf("dt_loc", [H, TL1], F32)
    lamS = sbuf("lamS", [H, TL1], F32)
    f1 = sbuf("f1", [H, TL1], F32)
    f2 = sbuf("f2", [H, TL1], F32)
    sc128 = f2[:, 0:128]
    wg_fm = sbuf("wg_fm", [H, t_loc], F32)
    u_fm = sbuf("u_fm", [H, t_loc], F32)
    aloc_fm = sbuf("aloc_fm", [H, t_loc], F32)
    ev_fm = sbuf("ev_fm", [H, t_loc], F32)
    ah_fm = sbuf("ah_fm", [H, t_loc], BF16)
    al_fm = sbuf("al_fm", [H, t_loc], BF16)
    vrow = sbuf("vrow", [1, 512], F32)
    carry = sbuf("carry", [32, 1], F32)
    cos_tm = sbuf("cos_tm", [128, nch * 32], BF16)
    sin_tm = sbuf("sin_tm", [128, nch * 32], BF16)
    bc_t = sbuf("bc_t", [128, nch * 128], F32)
    bcf = sbuf("bcf", [128, 2 * NS], F32)
    b_tmbA = sbuf("b_tmbA", [128, nch * NS], BF16)
    b_fmb = sbuf("b_fmb", [NS, t_loc], BF16)
    c_fmb = sbuf("c_fmb", [NS, t_loc], BF16)
    rms_sc = sbuf("rms_sc", [128, NS], F32)
    wqacc = rms_sc[:, 0:24]
    qsc = rms_sc[:, 24:28]
    runmax = rms_sc[:, 28:36]
    qrow = rms_sc[0:1, 40:48]
    wsraw = rms_sc[:, 36:39]
    rw1 = sbuf("rw1", [128, 1], F32)
    rw2 = sbuf("rw2", [128, 1], F32)
    sdd_tm = sbuf("sdd_tm", [128, nch], F32)
    xscal = sbuf("xscal", [128, 4 * 8], F32)
    pwf = sbuf("pwf", [128, 1024], F32)
    pwacc = sbuf("pwacc", [128, 25], F32)

    cd_col = sbuf("cd_col", [H, 1], F32)
    cd_row = sbuf("cd_row", [1, H], F32)
    s_bf = sbuf("s_bf", [128, CH], BF16)
    sc128b = s_bf
    u2_bf = sbuf("u2_bf", [128, 512], BF16)
    sc_a = u2_bf[:, :].bitcast(F32)[:, 0:64]
    sc_b = u2_bf[:, :].bitcast(F32)[:, 64:128]
    coef_s = sbuf("coef_s", [4, H], F32)

    def ark(lo, n):
        return [("AR", b) for b in range(lo // 1024, (lo + n + 1023) // 1024)]

    x_fm = AR[:, 0:nk * t_loc]
    K_XFM = [ark(j * t_loc, t_loc) for j in range(nk)]
    w_buf = AR[:, nk * t_loc:nk * t_loc + 2 * 512]
    K_WB = [ark(nk * t_loc + p * 512, 512) for p in range(2)]
    st_run = ARf[0:NS, 8704:12800]
    K_STR = ark(17408, 2 * DI)
    wx_bf = AR[:, 25600:29696]
    K_WX = ark(25600, DI)
    prev_bf = AR[0:NS, 29696:33792]
    K_PRV = ark(29696, DI)
    y_fm = AR[:, 0:32 * t_loc]
    K_YFM = [ark(j * t_loc, t_loc) for j in range(32)]
    wo_buf = AR[:, 32 * t_loc:32 * t_loc + 2 * 512]
    K_WO = [ark(32 * t_loc + p * 512, 512) for p in range(2)]
    xq_fm = AR[:, 0:nk * t_loc]
    K_XQ = K_XFM
    wff_buf = AR[:, nk * t_loc:nk * t_loc + 8 * 1024]
    K_WF = [ark(nk * t_loc + p * 4096, 4096) for p in range(2)]
    guq_fm = AR[:, 24 * 1024:24 * 1024 + (DFF // 128) * TQ]
    K_GQ = [ark(24 * 1024 + j * TQ, TQ) for j in range(DFF // 128)]

    PB = psumt("PB", [128, 2048], F32)
    PBb = PB[:, :].bitcast(BF16)
    PS4 = psumt("PS4", [128, 512], F32)
    PS5 = psumt("PS5", [128, 512], F32)
    PS6 = psumt("PS6", [128, 512], F32)
    PS7 = psumt("PS7", [128, 512], F32)
    PBK = [("PB", 0), ("PB", 1), ("PB", 2), ("PB", 3)]

    A("pool", lambda e: e.dma_start(out=identb[:], in_=identf_in), writes=["p0ch", "identb"], dma="Q_x")
    A("pool", lambda e: e.dma_start(out=triu[:], in_=triu_in), writes=["p0ch", "triu"], dma="Q_x")
    A("pool", lambda e: e.dma_start(out=maskr[:], in_=maskr_in), writes=["p0ch", "maskr"], dma="Q_x")
    A("sp", lambda e: e.dma_start(out=identf[:], in_=identf_in[0:64, 0:64]), writes=["p0chs", "identf"], dma="Q_s")
    for t, s, k in ((alog_s, a_log, "alog"), (dtb_s, dt_bias, "dtb"), (bmask_s, bmask_in, "bmask")):
        A("sp", lambda e, t=t, s=s: e.dma_start(out=t[:], in_=s), writes=["p0chs", k], dma="Q_s")
    A("dve", lambda e: e.memset(ones_rb[:], 1.0), writes=["ones_rb"])
    A("dve", lambda e: e.memset(ones_rf[:], 1.0), writes=["ones_rf"])
    A("dve", lambda e: e.memset(ones_cf[:], 1.0), writes=["ones_cf"])
    A("act", lambda e: e.activation(aneg[:], alog_s[:], AF.Exp), reads=["alog"], writes=["aneg"])
    A("dve", lambda e: e.tensor_scalar(aneg[:], aneg[:], -1.0, None, ALU.mult),
      reads=["aneg"], writes=["aneg"])
    for src, dstm, k in ((bnb_in, bnb_m, "bnb_m"), (cnb_in, cnb_m, "cnb_m")):
        A("sp", lambda e, s=src: e.dma_start(out=vrow[0:1, 0:NS], in_=s), reads=["vrow"],
          writes=["vrow"], dma="Q_s")
        A("pe", lambda e: e.matmul(PS4[:, 0:NS], ones_rf[:], vrow[0:1, 0:NS], start=True, stop=True),
          reads=["ones_rf", "vrow", "PS4"], writes=["PS4"])
        A("dve", lambda e, d=dstm: e.tensor_copy(d[:], PS4[:, 0:NS]), reads=["PS4"], writes=["PS4", k])

    for j in range(nk):
        A("pool", lambda e, j=j: e.dma_start(out=wsm[:, j * 160:j * 160 + 128],
                                             in_=w_in[j * 128:(j + 1) * 128, OFF_DT:OFF_DT + 128]),
          writes=[("wsmch", j % 2), ("wsm", j)], dma="Q_x")
        A("pool", lambda e, j=j: e.dma_start(out=wsm[:, j * 160 + 128:(j + 1) * 160],
                                             in_=w_in[j * 128:(j + 1) * 128, OFF_TH:OFF_TH + 32]),
          writes=[("wsmch", j % 2), ("wsm2", j)], dma="Q_x")

    def front(seg):
        own = seg == NSEG - 1
        if own:
            xsrc, xex_src = x_sl, x_ex
        elif seg == NSEG - 2:
            xsrc = x_pre[seg * t_loc:(seg + 1) * t_loc, :]
            xex_src = x_sl[0:1, :]
        else:
            xsrc = x_pre[seg * t_loc:(seg + 1) * t_loc, :]
            xex_src = x_pre[(seg + 1) * t_loc:(seg + 1) * t_loc + 1, :]
        A("pool", lambda e: e.dma_start(out=xbf_dram[:, :], in_=xsrc[:, :]), writes=["xbfd"], dma="Q_x")
        for j in range(nk):
            A("act", lambda e, j=j: e.dma_start_transpose(x_fm[:, j * t_loc:(j + 1) * t_loc],
                                                          xbf_dram[:, j * 128:(j + 1) * 128]),
              reads=["xbfd"] + K_XFM[j], writes=K_XFM[j], dma="Q_t")
        A("sp", lambda e: e.dma_start(out=x_st[0:1, 0:D], in_=xex_src), reads=[("x_st", 0)],
          writes=[("x_st", 0)], dma="Q_s")
        for j in range(16):
            A("pe", lambda e, j=j: e.matmul(PS4[:, j:j + 1], x_st[0:1, j * 128:(j + 1) * 128],
                                            ones_rf[0:1, 0:1], start=True, stop=True),
              reads=[("x_st", 0), "ones_rf"] + (["PS4"] if j == 0 else []),
              writes=["PS4"] if j == 15 else [])
        A("dve", lambda e: e.tensor_copy(xe_fm[:], PS4[:, 0:16]), reads=["PS4"], writes=["PS4", "xe_fm"])

        for g in range(ntg):
            for j in range(nk):
                A("pe", lambda e, g=g, j=j: e.matmul(PS7[0:128, 0:512], wsm[:, j * 160:j * 160 + 128],
                                                     x_fm[:, j * t_loc + g * 512:j * t_loc + (g + 1) * 512],
                                                     start=(j == 0), stop=(j == nk - 1)),
                  reads=K_XFM[j] + [("wsm", j)] + (["PS7"] if j == 0 else []),
                  writes=["PS7"] if j == nk - 1 else [])
            A("dve", lambda e, g=g: e.tensor_copy(dtl_fm[:, g * 512:(g + 1) * 512], PS7[0:128, 0:512]),
              reads=["PS7", "dtl"], writes=["PS7", "dtl"] + [("wst", cc) for cc in range(nch)])
        for j in range(nk):
            A("pe", lambda e, j=j: e.matmul(PS5[0:128, 0:1], wsm[:, j * 160:j * 160 + 128], xe_fm[:, j:j + 1],
                                            start=(j == 0), stop=(j == nk - 1)),
              reads=["xe_fm", ("wsm", j)] + (["PS5"] if j == 0 else []),
              writes=["PS5"] if j == nk - 1 else [])
        A("dve", lambda e: e.tensor_copy(dtl_fm[:, t_loc:TL1], PS5[0:128, 0:1]),
          reads=["PS5", "dtl"], writes=["PS5", "dtl"] + [("wst", cc) for cc in range(nch)])

        def psg(t):
            return PB[:, (t % 4) * 512:(t % 4 + 1) * 512] if t < 4 else (PS4, PS5, PS6, PS7)[t - 4][:, :]

        def psk(t):
            return ("PB", t) if t < 4 else ("PS4", "PS5", "PS6", "PS7")[t - 4]

        for g in (([16] + list(range(8, 16)) + list(range(8))) if own
                  else ([16] + list(range(8, 16)))):
            gw = 512 if g < 16 else 128
            goff = g * 512 if g < 16 else OFF_B
            for j in range(nk):
                A("pool", lambda e, g=g, j=j, gw=gw, goff=goff: e.dma_start(
                    out=w_buf[:, (j % 2) * 512:(j % 2) * 512 + gw],
                    in_=w_in[j * 128:(j + 1) * 128, goff:goff + gw]),
                  reads=K_WB[j % 2], writes=K_WB[j % 2] + [("w_l", j % 2)], dma="Q_x")
                for t in range(ntt):
                    A("pe", lambda e, g=g, j=j, t=t, gw=gw: e.matmul(
                        psg(t)[:, 0:gw], x_fm[:, j * t_loc + t * 128:j * t_loc + (t + 1) * 128],
                        w_buf[:, (j % 2) * 512:(j % 2) * 512 + gw],
                        start=(j == 0), stop=(j == nk - 1)),
                      reads=K_XFM[j] + K_WB[j % 2] + [("w_l", j % 2)] + ([psk(t)] if j == 0 else []),
                      writes=[psk(t)] if j == nk - 1 else [])
            for t in range(ntt):
                if g < 8:
                    A("dve", lambda e, g=g, t=t: e.tensor_copy(z_st[:, (t % 4) * 512:(t % 4 + 1) * 512],
                                                               psg(t)),
                      reads=[psk(t), ("zs", t % 4)], writes=[psk(t), ("zs", t % 4)])
                    A("act", lambda e, g=g, t=t: e.dma_start(
                        out=z_dram[t * 128:(t + 1) * 128, g * 512:(g + 1) * 512],
                        in_=z_st[:, (t % 4) * 512:(t % 4 + 1) * 512]),
                      reads=[("zs", t % 4)], writes=[("zch", t % 2), ("zs", t % 4), ("z_d", t)], dma="Q_t")
                elif g < 16:
                    A("act", lambda e, g=g, t=t: e.activation(xs_st[:, (t % 4) * 512:(t % 4 + 1) * 512],
                                                              psg(t), AF.Silu),
                      reads=[psk(t), ("xss", t % 4)], writes=[psk(t), ("xss", t % 4)])
                    A("act", lambda e, g=g, t=t: e.dma_start(
                        out=xs_dram[t * 128:(t + 1) * 128, (g - 8) * 512:(g - 7) * 512],
                        in_=xs_st[:, (t % 4) * 512:(t % 4 + 1) * 512]),
                      reads=[("xss", t % 4)], writes=[("xsch", t % 2), ("xss", t % 4), ("xs_d", t)], dma="Q_t")
                else:
                    A("dve", lambda e, t=t: e.tensor_copy(bc_t[:, t * 128:(t + 1) * 128],
                                                          psg(t)[:, 0:128]),
                      reads=[psk(t), "bc_t"], writes=[psk(t), "bc_t", ("bc", t)])

        A("dve", lambda e: e.tensor_scalar(f1[:], dtl_fm[0:64, :], dtb_s[:], None, ALU.add),
          reads=["dtl", "dtb", "f1"], writes=["f1"])
        A("act", lambda e: e.activation(f2[:], f1[:], AF.Abs), reads=["f1", "f2"], writes=["f2"])
        A("act", lambda e: e.activation(f2[:], f2[:], AF.Exp, scale=-1.0), reads=["f2"], writes=["f2"])
        A("act", lambda e: e.activation(f2[:], f2[:], AF.Ln, bias=1.0), reads=["f2"], writes=["f2"])
        A("dve", lambda e: e.tensor_scalar(f1[:], f1[:], 0.0, None, ALU.max), reads=["f1"], writes=["f1"])
        A("dve", lambda e: e.tensor_tensor(dt_loc[:], f1[:], f2[:], ALU.add),
          reads=["f1", "f2", "dt"], writes=["dt"])
        A("dve", lambda e: e.tensor_scalar(dt_loc[:], dt_loc[:], 0.5, 1e-4, ALU.min, ALU.max),
          reads=["dt"], writes=["dt"])
        A("act", lambda e: e.activation(lamS[:], dtl_fm[64:128, :], AF.Sigmoid),
          reads=["dtl", "lamS"], writes=["lamS"])
        A("dve", lambda e: e.tensor_tensor(wg_fm[:], dt_loc[:, 0:t_loc], lamS[:, 0:t_loc], ALU.mult),
          reads=["dt", "lamS", "wg"], writes=["wg"])
        A("dve", lambda e: e.tensor_tensor(lamS[:], dt_loc[:], lamS[:], ALU.mult),
          reads=["dt", "lamS"], writes=["lamS"])
        A("dve", lambda e: e.tensor_tensor(lamS[:], dt_loc[:], lamS[:], ALU.subtract),
          reads=["dt", "lamS"], writes=["lamS"])
        if own:
            A("dve", lambda e: e.tensor_scalar(lamS[:, t_loc:TL1], lamS[:, t_loc:TL1],
                                               bmask_s[:], None, ALU.mult),
              reads=["lamS", "bmask"], writes=["lamS"])
        A("dve", lambda e: e.tensor_tensor(u_fm[:], wg_fm[:], lamS[:, 1:TL1], ALU.add),
          reads=["wg", "lamS", "u"], writes=["u"])
        A("dve", lambda e: e.tensor_scalar(f1[:, 0:t_loc], dt_loc[:, 0:t_loc], aneg[:], None, ALU.mult),
          reads=["dt", "aneg", "f1"], writes=["f1"])
        for c in range(nch):
            A("dve", lambda e, c=c: e.tensor_tensor_scan(
                aloc_fm[:, c * CH:(c + 1) * CH], ones_cf[0:H, 0:1].broadcast_to((H, CH)),
                f1[:, c * CH:(c + 1) * CH], 0.0, ALU.mult, ALU.add),
              reads=["f1", "ones_cf", "aloc"], writes=["aloc"])
        A("act", lambda e: e.activation(ev_fm[:], u_fm[:], AF.Ln), reads=["u", "ev"], writes=["ev"])
        A("dve", lambda e: e.tensor_tensor(ev_fm[:], ev_fm[:], aloc_fm[:], ALU.subtract),
          reads=["ev", "aloc"], writes=["ev"])
        A("dve", lambda e: e.tensor_copy(ah_fm[:], aloc_fm[:]), reads=["aloc", "ah"], writes=["ah"])
        A("dve", lambda e: e.tensor_copy(f1[:, 0:t_loc], ah_fm[:]), reads=["ah", "f1"], writes=["f1"])
        A("dve", lambda e: e.tensor_tensor(f1[:, 0:t_loc], aloc_fm[:], f1[:, 0:t_loc], ALU.subtract),
          reads=["f1", "aloc"], writes=["f1"])
        A("dve", lambda e: e.tensor_copy(al_fm[:], f1[:, 0:t_loc]), reads=["f1", "al"], writes=["al"])
        if own:
            for c in range(nch):
                A("act", lambda e, c=c: e.dma_start(
                    out=ahl_dram[c:c + 1, :].rearrange("o (h t) -> (o h) t", h=H),
                    in_=ah_fm[:, c * CH:(c + 1) * CH]),
                  reads=["ah"], writes=["ahl_d"], dma="Q_t")
                A("act", lambda e, c=c: e.dma_start(
                    out=ahl_dram[nch + c:nch + c + 1, :].rearrange("o (h t) -> (o h) t", h=H),
                    in_=al_fm[:, c * CH:(c + 1) * CH]),
                  reads=["al"], writes=["ahl_d2"], dma="Q_t")

        for g in range(ntg):
            for j in range(nk):
                A("pe", lambda e, g=g, j=j: e.matmul(PS6[0:32, 0:512], wsm[:, j * 160 + 128:(j + 1) * 160],
                                                     x_fm[:, j * t_loc + g * 512:j * t_loc + (g + 1) * 512],
                                                     start=(j == 0), stop=(j == nk - 1)),
                  reads=K_XFM[j] + [("wsm2", j)] + (["PS6"] if j == 0 else []),
                  writes=["PS6"] if j == nk - 1 else [])
            A("dve", lambda e: e.tensor_copy(th_sg[:], PS6[0:32, 0:512]),
              reads=["PS6", "th_sg"], writes=["th_sg", "PS6"])
            A("pe", lambda e, g=g: e.matmul(PS5[0:1, 0:512], ones_cf[0:H, 0:1],
                                            dt_loc[:, g * 512:(g + 1) * 512], start=True, stop=True),
              reads=["dt", "ones_cf", "PS5"], writes=["PS5"])
            A("dve", lambda e: e.tensor_scalar(vrow[:], PS5[0:1, 0:512], 1.0 / H, None, ALU.mult),
              reads=["PS5", "vrow"], writes=["vrow", "PS5"])
            A("pe", lambda e: e.matmul(PS5[0:32, 0:512], ones_rf[0:1, 0:32], vrow[:], start=True, stop=True),
              reads=["vrow", "ones_rf", "PS5"], writes=["PS5"])
            A("dve", lambda e: e.scalar_tensor_tensor(th_sg[:], th_sg[:], -1.0, PS5[0:32, 0:512],
                                                      ALU.mult, ALU.mult),
              reads=["th_sg", "PS5"], writes=["th_sg", "PS5"])
            A("dve", lambda e, g=g: e.tensor_copy(cum_l[:, g * 512:(g + 1) * 512], th_sg[:]),
              reads=["th_sg", "cum"], writes=["cum"])
        if seg == 0:
            A("dve", lambda e: e.memset(carry[:], 0.0), reads=["carry"], writes=["carry"])
        for g in range(ntg):
            A("dve", lambda e, g=g: e.tensor_tensor_scan(
                cum_l[:, g * 512:(g + 1) * 512], ones_cf[0:32, 0:1].broadcast_to((32, 512)),
                cum_l[:, g * 512:(g + 1) * 512], carry[0:32, 0:1], ALU.mult, ALU.add),
              reads=["cum", "carry", "ones_cf"], writes=["cum"])
            A("dve", lambda e, g=g: e.tensor_copy(carry[:], cum_l[:, g * 512 + 511:g * 512 + 512]),
              reads=["cum", "carry"], writes=["carry"])
        for phase, key, off in ((0.0, "sin", t_loc), (0.25, "cos", 0)):
            A("dve", lambda e, ph=phase: e.tensor_scalar(rr_f[:], cum_l[:], 1.0 / TWO_PI, ph,
                                                         ALU.mult, ALU.add),
              reads=["cum", "rr_f"], writes=["rr_f"])
            A("dve", lambda e: e.tensor_copy(rr_i[:], rr_f[:]), reads=["rr_f", "rr_i"], writes=["rr_i"])
            A("dve", lambda e: e.tensor_copy(rr_g[:], rr_i[:]), reads=["rr_i", "rr_g"], writes=["rr_g"])
            A("dve", lambda e: e.scalar_tensor_tensor(rr_f[:], rr_g[:], -TWO_PI, cum_l[:],
                                                      ALU.mult, ALU.add),
              reads=["rr_g", "cum", "rr_f"], writes=["rr_f"])
            if phase == 0.25:
                A("dve", lambda e: e.tensor_scalar(rr_f[:], rr_f[:], PI / 2, None, ALU.add),
                  reads=["rr_f"], writes=["rr_f"])
            A("dve", lambda e: e.tensor_scalar(rr_m[:], rr_f[:], PI, None, ALU.is_gt),
              reads=["rr_f", "rr_i"], writes=["rr_i"])
            A("dve", lambda e: e.scalar_tensor_tensor(rr_f[:], rr_m[:], -TWO_PI, rr_f[:],
                                                      ALU.mult, ALU.add),
              reads=["rr_f", "rr_i"], writes=["rr_f"])
            A("dve", lambda e: e.tensor_scalar(rr_m[:], rr_f[:], -PI, None, ALU.is_lt),
              reads=["rr_f", "rr_i"], writes=["rr_i"])
            A("dve", lambda e: e.scalar_tensor_tensor(rr_f[:], rr_m[:], TWO_PI, rr_f[:],
                                                      ALU.mult, ALU.add),
              reads=["rr_f", "rr_i"], writes=["rr_f"])
            A("act", lambda e, off=off: e.activation(cs_bf[:, off:off + t_loc], rr_f[:], AF.Sin),
              reads=["rr_f"], writes=[key])
        for c in range(nch):
            A("pe", lambda e, c=c: e.transpose(PBb[:, 0:32], cs_bf[:, c * CH:(c + 1) * CH],
                                               identb[0:32, 0:32]),
              reads=["cos", "identb", ("PB", 0)], writes=[("PB", 0)])
            A("dve", lambda e, c=c: e.tensor_copy(cos_tm[:, c * 32:(c + 1) * 32], PBb[:, 0:32]),
              reads=[("PB", 0), "cos_tm"], writes=[("PB", 0), "cos_tm"])
            A("pe", lambda e, c=c: e.transpose(PBb[:, 1024:1056],
                                               cs_bf[:, t_loc + c * CH:t_loc + (c + 1) * CH],
                                               identb[0:32, 0:32]),
              reads=["sin", "identb", ("PB", 1)], writes=[("PB", 1)])
            A("dve", lambda e, c=c: e.tensor_copy(sin_tm[:, c * 32:(c + 1) * 32], PBb[:, 1024:1056]),
              reads=[("PB", 1), "sin_tm"], writes=[("PB", 1), "sin_tm"])

        A("dve", lambda e: e.engine_nop(),
          reads=["cum", "rr_f", "rr_i", "rr_g", "cos", "sin", "th_sg", "carry"],
          writes=P2B_KEYS + ["ev_tm", "qd_tm", "ws_tm"])

        for c in range(nch):
            for nm, off, nbm, nbk, dof in ((("B", 0, bnb_m, "bnb_m", 0),) +
                                           ((("C", NS, cnb_m, "cnb_m", NS),) if own else ())):
                dst = bcf[:, dof:dof + NS]
                sl = bc_t[:, c * 128 + off:c * 128 + off + NS]
                A("dve", lambda e, sl=sl: e.tensor_tensor(rms_sc[:], sl, sl, ALU.mult),
                  reads=[("bc", c), "rms_sc"], writes=["rms_sc"])
                A("dve", lambda e: e.tensor_reduce(rw1[:], rms_sc[:], AX.X, ALU.add),
                  reads=["rms_sc", "rw1"], writes=["rw1"])
                A("dve", lambda e: e.tensor_scalar(rw1[:], rw1[:], 1.0 / NS, 1e-6, ALU.mult, ALU.add),
                  reads=["rw1"], writes=["rw1"])
                A("dve", lambda e: e.reciprocal(rw2[:], rw1[:]), reads=["rw1", "rw2"], writes=["rw2"])
                A("act", lambda e: e.activation(rw2[:], rw2[:], AF.Sqrt), reads=["rw2"], writes=["rw2"])
                A("dve", lambda e, sl=sl: e.tensor_scalar(rms_sc[:], sl, rw2[:], None, ALU.mult),
                  reads=[("bc", c), "rw2", "rms_sc"], writes=["rms_sc"])
                A("dve", lambda e, nbm=nbm: e.tensor_tensor(rms_sc[:], rms_sc[:], nbm[:], ALU.mult),
                  reads=["rms_sc", nbk], writes=["rms_sc"])
                cslice = cos_tm[:, c * 32:(c + 1) * 32]
                sslice = sin_tm[:, c * 32:(c + 1) * 32]
                A("dve", lambda e, cs=cslice: e.tensor_tensor(sc_a[:, 0:32], rms_sc[:, 0:32], cs, ALU.mult),
                  reads=["rms_sc", "cos_tm", "scz"], writes=["scz"])
                A("dve", lambda e, ss=sslice: e.tensor_tensor(sc_a[:, 32:64], rms_sc[:, 32:64], ss, ALU.mult),
                  reads=["rms_sc", "sin_tm", "scz"], writes=["scz"])
                A("dve", lambda e, dst=dst: e.tensor_tensor(
                    dst[:, 0:32], sc_a[:, 0:32], sc_a[:, 32:64], ALU.subtract),
                  reads=["scz", "bcf"], writes=["bcf"])
                A("dve", lambda e, ss=sslice: e.tensor_tensor(sc_b[:, 0:32], rms_sc[:, 0:32], ss, ALU.mult),
                  reads=["rms_sc", "sin_tm", "scz"], writes=["scz"])
                A("dve", lambda e, cs=cslice: e.tensor_tensor(sc_b[:, 32:64], rms_sc[:, 32:64], cs, ALU.mult),
                  reads=["rms_sc", "cos_tm", "scz"], writes=["scz"])
                A("dve", lambda e, dst=dst: e.tensor_tensor(
                    dst[:, 32:NS], sc_b[:, 0:32], sc_b[:, 32:64], ALU.add),
                  reads=["scz", "bcf"], writes=["bcf"])
            A("dve", lambda e, c=c: e.tensor_copy(b_tmbA[:, c * NS:(c + 1) * NS], bcf[:, 0:NS]),
              reads=["bcf", "b_tmbA"], writes=["b_tmbA", ("Bb", c)])
            if not own:
                continue
            A("dve", lambda e, c=c: e.tensor_tensor(rms_sc[:], bcf[:, 0:NS], bcf[:, NS:2 * NS], ALU.mult),
              reads=["bcf", "rms_sc"], writes=["rms_sc"])
            A("dve", lambda e, c=c: e.tensor_reduce(sdd_tm[:, c:c + 1], rms_sc[:], AX.X, ALU.add),
              reads=["rms_sc", "sdd"], writes=["sdd"])
            A("pe", lambda e, c=c: e.transpose(PBb[0:64, 2048:2176], b_tmbA[:, c * NS:(c + 1) * NS],
                                               identb[:]),
              reads=[("Bb", c), "identb", ("PB", 2)], writes=[("PB", 2)])
            A("dve", lambda e, c=c: e.tensor_copy(b_fmb[:, c * CH:(c + 1) * CH], PBb[0:64, 2048:2176]),
              reads=[("PB", 2), "b_fmb"], writes=[("PB", 2), "b_fmb", ("bf", c)])
            A("dve", lambda e, c=c: e.tensor_copy(sc128b[:, 0:NS], bcf[:, NS:2 * NS]),
              reads=["bcf", "s_bf"], writes=["s_bf"])
            A("pe", lambda e, c=c: e.transpose(PBb[0:64, 3072:3200], sc128b[:, 0:NS], identb[:]),
              reads=["s_bf", "identb", ("PB", 3)], writes=[("PB", 3)])
            A("dve", lambda e, c=c: e.tensor_copy(c_fmb[:, c * CH:(c + 1) * CH], PBb[0:64, 3072:3200]),
              reads=[("PB", 3), "c_fmb"], writes=[("PB", 3), "c_fmb", ("cf", c)])

        for c in range(nch):
            A("act", lambda e, c=c: e.activation(sc128[:], ev_fm[:, c * CH:(c + 1) * CH], AF.Exp,
                                                 bias=aloc_fm[:, c * CH + CH - 1:c * CH + CH]),
              reads=["ev", "aloc", "f2"], writes=["f2"])
            A("dve", lambda e: e.tensor_copy(sc128b[0:H, 0:CH], sc128[:]),
              reads=["f2", "s_bf"], writes=["s_bf"])
            A("pe", lambda e, c=c: e.transpose(PBb[:, 3072:3136], sc128b[0:H, 0:CH], identb[0:H, 0:H]),
              reads=["s_bf", "identb", ("PB", 3)], writes=[("PB", 3)])
            A("dve", lambda e, c=c: e.tensor_copy(ws_tm[:, c * H:(c + 1) * H], PBb[:, 3072:3136]),
              reads=[("PB", 3), "ws_tm", "dtl"], writes=[("PB", 3), "ws_tm", ("wst", c)])
            if not own:
                continue
            A("pe", lambda e, c=c: e.transpose(PB[:, 512:576], ev_fm[:, c * CH:(c + 1) * CH],
                                               identf[0:64, 0:64]),
              reads=["ev", "identf", ("PB", 1)], writes=[("PB", 1)])
            A("dve", lambda e, c=c: e.tensor_copy(ev_tm[:, c * H:(c + 1) * H], PB[:, 512:576]),
              reads=[("PB", 1), "ev_tm"], writes=[("PB", 1), "ev_tm", ("evt", c)])
            A("act", lambda e, c=c: e.activation(sc128[:], aloc_fm[:, c * CH:(c + 1) * CH], AF.Exp),
              reads=["aloc", "f2"], writes=["f2"])
            A("dve", lambda e: e.tensor_copy(sc128b[0:H, 0:CH], sc128[:]),
              reads=["f2", "s_bf"], writes=["s_bf"])
            A("pe", lambda e, c=c: e.transpose(PBb[:, 2048:2112], sc128b[0:H, 0:CH], identb[0:H, 0:H]),
              reads=["s_bf", "identb", ("PB", 2)], writes=[("PB", 2)])
            A("dve", lambda e, c=c: e.tensor_copy(qd_tm[:, c * H:(c + 1) * H], PBb[:, 2048:2112]),
              reads=[("PB", 2), "qd_tm", "dtl"], writes=[("PB", 2), "qd_tm", ("qdt", c)])
            A("pe", lambda e, c=c: e.transpose(PB[:, 576:640], wg_fm[:, c * CH:(c + 1) * CH],
                                               identf[0:64, 0:64]),
              reads=["wg", "identf", ("PB", 1)], writes=[("PB", 1)])
            A("dve", lambda e, c=c: e.tensor_scalar(c0_tm[:, c * H:(c + 1) * H], PB[:, 576:640],
                                                    sdd_tm[:, c:c + 1], None, ALU.mult),
              reads=[("PB", 1), "sdd", "c0_tm", "dtl"], writes=[("PB", 1), "c0_tm", ("c0t", c)])

        if seg == 0:
            A("dve", lambda e: e.memset(st_run[:, :], 0.0), reads=K_STR, writes=K_STR)
        for c in range(nch):
            p = c % 2
            A("sp", lambda e, c=c, p=p: e.dma_start(out=xs_db[:, p * DI:(p + 1) * DI],
                                                    in_=xs_dram[c * 128:(c + 1) * 128, :]),
              reads=[("xs_d", c), ("xs_db", p)], writes=[("xs_db", p)], dma="Q_s")
            A("dve", lambda e, c=c, p=p: e.tensor_tensor(
                wx_bf[:, :].rearrange("p (h q) -> p h q", h=H),
                xs_db[:, p * DI:(p + 1) * DI].rearrange("p (h q) -> p h q", h=H),
                ws_tm[:, c * H:(c + 1) * H].unsqueeze(2).broadcast_to((128, H, NS)), ALU.mult),
              reads=[("xs_db", p), ("wst", c)] + K_WX, writes=K_WX)
            if own:
                A("dve", lambda e: e.tensor_copy(prev_bf[:, :], st_run[:, :]),
                  reads=K_STR + K_PRV, writes=K_PRV)
                A("act", lambda e, c=c: e.dma_start(out=prev_dram[c * NS:(c + 1) * NS, :],
                                                    in_=prev_bf[:, :]),
                  reads=K_PRV, writes=["prch", ("prevd", c)], dma="Q_t")
            A("act", lambda e, c=c: e.activation(cd_col[:], aloc_fm[:, c * CH + CH - 1:c * CH + CH], AF.Exp),
              reads=["aloc", "cd_col"], writes=["cd_col"])
            A("pe", lambda e: e.transpose(PS4[0:1, 0:64], cd_col[:], identf[:]),
              reads=["cd_col", "identf", "PS4"], writes=["PS4"])
            A("dve", lambda e: e.tensor_copy(cd_row[:], PS4[0:1, 0:64]),
              reads=["PS4", "cd_row"], writes=["cd_row", "PS4"])
            A("pe", lambda e: e.matmul(PS7[0:NS, 0:NS], ones_rf[0:1, 0:NS], cd_row[:], start=True, stop=True),
              reads=["cd_row", "ones_rf", "PS7"], writes=["PS7"])
            A("dve", lambda e: e.tensor_tensor(
                st_run[:, :].rearrange("n (h q) -> n h q", h=H),
                st_run[:, :].rearrange("n (h q) -> n h q", h=H),
                PS7[0:NS, 0:NS].unsqueeze(2).broadcast_to((NS, H, NS)), ALU.mult),
              reads=K_STR + ["PS7"] + (K_PRV if own else []), writes=K_STR + ["PS7"])
            for half in range(2):
                for i in range(4):
                    A("pe", lambda e, c=c, half=half, i=i: e.matmul(
                        PB[0:NS, i * 512:(i + 1) * 512], b_tmbA[:, c * NS:(c + 1) * NS],
                        wx_bf[:, half * 2048 + i * 512:half * 2048 + (i + 1) * 512],
                        start=True, stop=True),
                      reads=K_WX + [("Bb", c), ("PB", i)], writes=[("PB", i)])
                A("dve", lambda e, c=c, half=half: e.tensor_tensor(
                    st_run[:, half * 2048:(half + 1) * 2048], st_run[:, half * 2048:(half + 1) * 2048],
                    PB[0:NS, 0:2048], ALU.add),
                  reads=K_STR + PBK, writes=K_STR + PBK)


    def pw_stats(tiles, col0):
        for i, ap in enumerate(tiles):
            h = i % 2
            A("sp", lambda e, ap=ap, h=h: e.dma_start(out=pwf[:, h * 512:(h + 1) * 512], in_=ap),
              reads=[("pwf", h)], writes=[("pwf", h)], dma="Q_s")
            col = col0 + i % 8
            if i < 8:
                A("dve", lambda e, h=h, col=col: e.tensor_reduce(
                    pwacc[:, col:col + 1], pwf[:, h * 512:(h + 1) * 512],
                    AX.X, ALU.add, apply_absolute_value=True),
                  reads=[("pwf", h), "pwacc"], writes=["pwacc"])
            else:
                A("dve", lambda e, h=h: e.tensor_reduce(
                    pwacc[:, 24:25], pwf[:, h * 512:(h + 1) * 512], AX.X, ALU.add,
                    apply_absolute_value=True),
                  reads=[("pwf", h), "pwacc"], writes=["pwacc"])
                A("dve", lambda e, col=col: e.tensor_tensor(
                    pwacc[:, col:col + 1], pwacc[:, col:col + 1], pwacc[:, 24:25], ALU.add),
                  reads=["pwacc"], writes=["pwacc"])

    A("dve", lambda e: e.memset(pwacc[:], 0.0), reads=["pwacc"], writes=["pwacc"])

    def gu_tiles(hbm):
        return [hbm[t * 128:(t + 1) * 128, h * 512:(h + 1) * 512]
                for t in range(DFF // 128) for h in range(4)]

    wd_tiles = [wd_hbm[t * 128:(t + 1) * 128, q * 512:(q + 1) * 512]
                for t in range(D // 128) for q in range(DFF // 512)]
    stats_jobs = [lambda: pw_stats(gu_tiles(wg_hbm), 0),
                  lambda: pw_stats(gu_tiles(wu_hbm), 8),
                  lambda: pw_stats(wd_tiles, 16)]
    npre_seg = max(NSEG - 1, 1)
    done = 0
    for seg in range(NSEG):
        front(seg)
        if seg < NSEG - 1:
            upto = (seg + 1) * 3 // npre_seg
            for job in stats_jobs[done:upto]:
                job()
            done = upto
    for job in stats_jobs[done:]:
        job()

    for col, lo, n in ((0, 0, 8), (1, 8, 8), (2, 16, 8)):
        A("dve", lambda e, col=col, lo=lo, n=n: e.tensor_reduce(qsc[:, col:col + 1],
                                                                pwacc[:, lo:lo + n], AX.X, ALU.add),
          reads=["pwacc", "rms_sc"], writes=["rms_sc"])
    A("pe", lambda e: e.matmul(PS7[0:3, 0:1], qsc[:, 0:3], ones_cf[:, 0:1], start=True, stop=True),
      reads=["rms_sc", "ones_cf", "PS7"], writes=["PS7"])
    A("dve", lambda e: e.tensor_copy(coef_s[0:3, 0:1], PS7[0:3, 0:1]),
      reads=["PS7", "coef_s"], writes=["coef_s", "PS7"])
    A("act", lambda e: e.dma_start(out=row_dram[3:4, 0:3].rearrange("o (a u) -> (o a) u", a=3),
                                   in_=coef_s[0:3, 0:1]),
      reads=["coef_s"], writes=["rowch", "row_d4"], dma="Q_t")
    A("sp", lambda e: e.dma_start(out=qrow[0:1, 0:3], in_=row_dram[3:4, 0:3]),
      reads=["row_d4", "rms_sc"], writes=["rms_sc"], dma="Q_s")
    A("dve", lambda e: e.tensor_scalar(qrow[0:1, 0:4], qrow[0:1, 0:4], 1.0 / (DFF * D), 1e-5,
                                       ALU.mult, ALU.add),
      reads=["rms_sc"], writes=["rms_sc"])
    A("dve", lambda e: e.reciprocal(qrow[0:1, 4:8], qrow[0:1, 0:4]), reads=["rms_sc"], writes=["rms_sc"])
    A("pe", lambda e: e.matmul(PS7[0:128, 0:4], ones_rf[0:1, :], qrow[0:1, 4:8], start=True, stop=True),
      reads=["rms_sc", "ones_rf", "PS7"], writes=["PS7"])
    A("dve", lambda e: e.tensor_copy(qsc[:, 0:4], PS7[0:128, 0:4]),
      reads=["PS7", "rms_sc"], writes=["rms_sc", "PS7"])
    A("pe", lambda e: e.matmul(PS7[0:128, 4:7], ones_rf[0:1, :], qrow[0:1, 0:3], start=True, stop=True),
      reads=["rms_sc", "ones_rf", "PS7"], writes=["PS7"])
    A("dve", lambda e: e.tensor_copy(wsraw[:, 0:3], PS7[0:128, 4:7]),
      reads=["PS7", "rms_sc"], writes=["rms_sc", "PS7"])
    def q_gu(m, r):
        hbm = (wg_hbm, wu_hbm)[m]
        nat = (wqg_all, wqu_all)[m]
        kk = ("qg", "qu")[m]
        for t in range(DF8 // 128):
            row0 = (r * (DF8 // 128) + t) * 128
            if t % 2 == 0:
                zb = z_st[:, 0:D]
                zkeys = ["z_st", ("zs", 0), ("zs", 1), ("zs", 2), ("zs", 3)]
            else:
                zb = pwf[:, :].bitcast(BF16)
                zkeys = [("pwf", 0), ("pwf", 1)]
            for h in range(2):
                q = (t * 2 + h) % 4
                xq = x_st[:, q * 1024:(q + 1) * 1024]
                A("sp", lambda e, xq=xq, row0=row0, h=h, hbm=hbm: e.dma_start(
                    out=xq, in_=hbm[row0:row0 + 128, h * 1024:(h + 1) * 1024]),
                  reads=[("x_stq", q), ("x_st", q // 2)],
                  writes=[("x_stq", q), ("x_st", q // 2)], dma="Q_s")
                A("dve", lambda e, xq=xq, m=m: e.tensor_scalar(xq, xq, qsc[:, m:m + 1], CBIG,
                                                               ALU.mult, ALU.add),
                  reads=[("x_stq", q), "rms_sc"], writes=[("x_stq", q)])
                A("dve", lambda e, xq=xq: e.tensor_scalar(xq, xq, CBIG, None, ALU.subtract),
                  reads=[("x_stq", q)], writes=[("x_stq", q)])
                A("dve", lambda e, xq=xq, zb=zb, h=h: e.tensor_scalar(
                    zb[:, h * 1024:(h + 1) * 1024], xq, 1.0, -1.0, ALU.min, ALU.max),
                  reads=[("x_stq", q)] + zkeys, writes=zkeys)
            A("act", lambda e, nat=nat, row0=row0, zb=zb: e.dma_start(
                out=nat[row0:row0 + 128, :], in_=zb[:, 0:D]),
              reads=zkeys, writes=[kk + "ch", (kk, r, t)], dma="Q_t")

    def q_d(r):
        for t in range(D // 128):
            q = t % 4
            if t % 2 == 0:
                zb = z_st[:, 0:DD8]
                zkeys = ["z_st", ("zs", 0), ("zs", 1)]
            else:
                zb = pwf[:, :].bitcast(BF16)[:, 0:DD8]
                zkeys = [("pwf", 0), ("pwf", 1)]
            xq = x_st[:, q * 1024:(q + 1) * 1024]
            A("sp", lambda e, xq=xq, r=r, t=t: e.dma_start(
                out=xq, in_=wd_hbm[t * 128:(t + 1) * 128, r * DD8:(r + 1) * DD8]),
              reads=[("x_stq", q), ("x_st", q // 2)],
              writes=[("x_stq", q), ("x_st", q // 2)], dma="Q_s")
            A("dve", lambda e, xq=xq: e.tensor_scalar(xq, xq, qsc[:, 2:3], CBIG,
                                                      ALU.mult, ALU.add),
              reads=[("x_stq", q), "rms_sc"], writes=[("x_stq", q)])
            A("dve", lambda e, xq=xq: e.tensor_scalar(xq, xq, CBIG, None, ALU.subtract),
              reads=[("x_stq", q)], writes=[("x_stq", q)])
            A("dve", lambda e, xq=xq, zb=zb: e.tensor_scalar(zb, xq, 1.0, -1.0, ALU.min, ALU.max),
              reads=[("x_stq", q)] + zkeys, writes=zkeys)
            A("act", lambda e, r=r, t=t, zb=zb: e.dma_start(
                out=wqd_all[t * 128:(t + 1) * 128, r * DD8:(r + 1) * DD8], in_=zb),
              reads=zkeys, writes=["qdch", ("qd", r, t)], dma="Q_t")

    pw_units = ([lambda m=m, r=r: q_gu(m, r) for m in range(2) for r in range(DFF // DF8)]
                + [lambda r=r: q_d(r) for r in range(DFF // DD8)])
    PWL = (len(pw_units) + nch - 1) // nch

    ebg = AR[:, 25600:26624]
    gbg = AR[:, 26624:27648]
    A("dve", lambda e: e.engine_nop(), reads=[("AR", 25), ("AR", 26)],
      writes=[("AR", 25), ("AR", 26), ("bce", 0), ("bce", 1), ("bcg", 0), ("bcg", 1)])
    for c in range(nch):
        p = c % 2
        A("sp", lambda e, c=c, p=p: e.dma_start(out=xs_db[:, p * DI:(p + 1) * DI],
                                                in_=xs_dram[c * 128:(c + 1) * 128, :]),
          reads=[("xs_d", c), ("xs_db", p)], writes=[("xs_db", p)], dma="Q_s")
        A("act", lambda e, c=c: e.dma_start(out=aflat_h[:], in_=ahl_dram[c:c + 1, :]),
          reads=["ahl_d"] + K_AFH, writes=K_AFH, dma="Q_t")
        A("act", lambda e, c=c: e.dma_start(out=aflat_l[:], in_=ahl_dram[nch + c:nch + c + 1, :]),
          reads=["ahl_d2"] + K_AFL, writes=K_AFL, dma="Q_t")
        A("pe", lambda e, c=c: e.matmul(PS4[:, 0:128], b_fmb[0:NS, c * CH:(c + 1) * CH],
                                        c_fmb[0:NS, c * CH:(c + 1) * CH], start=True, stop=True),
          reads=[("bf", c), ("cf", c), "PS4"], writes=["PS4"])
        A("dve", lambda e: e.tensor_copy(s_bf[:], PS4[:, 0:128]),
          reads=["PS4", "s_bf"], writes=["PS4", "s_bf"])
        for q in range(4):
            for i in range(4):
                A("pe", lambda e, q=q, i=i: e.matmul(
                    PB[:, i * 512:(i + 1) * 512], ones_rb[:],
                    aflat_h[0:1, q * 2048 + i * 512:q * 2048 + (i + 1) * 512],
                    start=True, stop=False),
                  reads=K_AFH + ["ones_rb"], writes=[("PB", i)])
                A("pe", lambda e, q=q, i=i: e.matmul(
                    PB[:, i * 512:(i + 1) * 512], ones_rb[:],
                    aflat_l[0:1, q * 2048 + i * 512:q * 2048 + (i + 1) * 512],
                    start=False, stop=False),
                  reads=K_AFL + ["ones_rb"], writes=[])
                A("pe", lambda e, q=q, i=i: e.matmul(
                    PB[:, i * 512:(i + 1) * 512].rearrange("p (r n) -> p r n", r=4),
                    triu[:], maskr[:, :].unsqueeze(1).broadcast_to((128, 4, 128)),
                    start=False, stop=True),
                  reads=["triu", "maskr"], writes=[("PB", i)])
            for i in range(4):
                eb = ebg[:, (i % 2) * 512:(i % 2) * 512 + 512]
                gb = gbg[:, (i % 2) * 512:(i % 2) * 512 + 512]
                h0 = q * 16 + i * 4
                A("dve", lambda e, i=i, h0=h0, c=c: e.tensor_tensor(
                    PB[:, i * 512:(i + 1) * 512].rearrange("p (r n) -> p r n", r=4),
                    PB[:, i * 512:(i + 1) * 512].rearrange("p (r n) -> p r n", r=4),
                    ev_tm[:, c * H + h0:c * H + h0 + 4].unsqueeze(2)
                    .broadcast_to((128, 4, CH)), ALU.add),
                  reads=[("PB", i), ("evt", c)], writes=[("PB", i)])
                A("act", lambda e, i=i, eb=eb: e.activation(eb, PB[:, i * 512:(i + 1) * 512],
                                                            AF.Exp),
                  reads=[("PB", i), ("bce", i % 2), ("AR", 25)], writes=[("bce", i % 2)])
                A("dve", lambda e, eb=eb, gb=gb: e.tensor_tensor(
                    gb.rearrange("p (r n) -> p r n", r=4),
                    eb.rearrange("p (r n) -> p r n", r=4),
                    s_bf[:, :].unsqueeze(1).broadcast_to((128, 4, CH)), ALU.mult),
                  reads=["s_bf", ("bce", i % 2), ("bcg", i % 2), ("AR", 25), ("AR", 26)],
                  writes=[("bcg", i % 2)])
                for hl4 in range(4):
                    h = q * 16 + i * 4 + hl4
                    pg, pgk = (PS5, "PS5") if (h // 8) % 2 == 0 else (PS6, "PS6")
                    A("pe", lambda e, h=h, gb=gb, hl4=hl4, pg=pg, p=p: e.matmul(
                        pg[:, (h % 8) * 64:(h % 8 + 1) * 64], gb[:, hl4 * 128:(hl4 + 1) * 128],
                        xs_db[:, p * DI + h * 64:p * DI + (h + 1) * 64], start=True, stop=True),
                      reads=[("bcg", i % 2), ("AR", 26), ("xs_db", p), pgk],
                      writes=[pgk] if h % 8 == 7 else [])
                    if h % 8 != 7:
                        continue
                    gg = h // 8
                    A("dve", lambda e, c=c, gg=gg, p=p: e.tensor_tensor(
                        u2_bf[:, :].rearrange("p (r n) -> p r n", r=8),
                        xs_db[:, p * DI + gg * 512:p * DI + (gg + 1) * 512]
                        .rearrange("p (r n) -> p r n", r=8),
                        c0_tm[:, c * H + gg * 8:c * H + (gg + 1) * 8].unsqueeze(2)
                        .broadcast_to((128, 8, 64)), ALU.mult),
                      reads=[("xs_db", p), ("c0t", c), "scz"], writes=["scz"])
                    A("dve", lambda e, gg=gg, pg=pg, pgk=pgk: e.tensor_tensor(
                        y_st[:, gg * 512:(gg + 1) * 512], pg[:, :], u2_bf[:], ALU.add),
                      reads=["scz", pgk, "y_st"], writes=["y_st", pgk])
        A("act", lambda e, c=c: e.dma_start(out=y_dram[c * 128:(c + 1) * 128, :], in_=y_st[:, :]),
          reads=["y_st"], writes=["ych", ("y_d", c)], dma="Q_t")
        for u in pw_units[c * PWL:(c + 1) * PWL]:
            u()

    for c in range(nch):
        p = c % 2
        A("sp", lambda e, c=c, p=p: e.dma_start(out=xs_db[:, p * DI:(p + 1) * DI],
                                                in_=z_dram[c * 128:(c + 1) * 128, :]),
          reads=[("z_d", c), ("xs_db", p)], writes=[("xs_db", p)], dma="Q_s")
        A("sp", lambda e, c=c: e.dma_start(out=y_st[:, :], in_=y_dram[c * 128:(c + 1) * 128, :]),
          reads=[("y_d", c), "y_st"], writes=["y_st"], dma="Q_s")
        A("sp", lambda e, c=c: e.dma_start(out=prev_bf[:, :], in_=prev_dram[c * NS:(c + 1) * NS, :]),
          reads=[("prevd", c)] + K_PRV, writes=K_PRV, dma="Q_s")
        for gg in range(8):
            pg, pgk = (PS5, "PS5") if gg % 2 == 0 else (PS6, "PS6")
            A("pe", lambda e, c=c, gg=gg, pg=pg: e.matmul(
                pg[:, :], c_fmb[0:NS, c * CH:(c + 1) * CH],
                prev_bf[:, gg * 512:(gg + 1) * 512], start=True, stop=True),
              reads=K_PRV + [("cf", c), pgk], writes=[pgk])
            A("dve", lambda e, c=c, gg=gg, pg=pg, pgk=pgk: e.tensor_tensor(
                u2_bf[:, :].rearrange("p (r n) -> p r n", r=8),
                pg[:, :].rearrange("p (r n) -> p r n", r=8),
                qd_tm[:, c * H + gg * 8:c * H + (gg + 1) * 8].unsqueeze(2).broadcast_to((128, 8, 64)),
                ALU.mult),
              reads=[pgk, ("qdt", c), "scz"], writes=["scz", pgk])
            A("dve", lambda e, gg=gg: e.tensor_tensor(y_st[:, gg * 512:(gg + 1) * 512],
                                                      y_st[:, gg * 512:(gg + 1) * 512],
                                                      u2_bf[:], ALU.add),
              reads=["scz", "y_st"], writes=["y_st"])
        for gg in range(8):
            A("dve", lambda e, gg=gg: e.tensor_tensor(
                x_st[:, 0:512], y_st[:, gg * 512:(gg + 1) * 512], y_st[:, gg * 512:(gg + 1) * 512],
                ALU.mult),
              reads=["y_st", ("x_st", 0)], writes=[("x_st", 0)])
            A("dve", lambda e, gg=gg: e.tensor_reduce(
                (rw1[:, 0:1] if gg == 0 else rw2[:, 0:1]), x_st[:, 0:512], AX.X, ALU.add),
              reads=[("x_st", 0), "rw1", "rw2"], writes=[("rw1" if gg == 0 else "rw2")])
            if gg > 0:
                A("dve", lambda e: e.tensor_tensor(rw1[:], rw1[:], rw2[:], ALU.add),
                  reads=["rw1", "rw2"], writes=["rw1"])
        A("dve", lambda e: e.tensor_scalar(rw1[:], rw1[:], 1.0 / DI, 1e-6, ALU.mult, ALU.add),
          reads=["rw1"], writes=["rw1"])
        A("dve", lambda e: e.reciprocal(rw2[:], rw1[:]), reads=["rw1", "rw2"], writes=["rw2"])
        A("act", lambda e: e.activation(rw2[:], rw2[:], AF.Sqrt), reads=["rw2"], writes=["rw2"])
        A("act", lambda e, p=p: e.activation(xs_st[:, :], xs_db[:, p * DI:(p + 1) * DI], AF.Silu),
          reads=[("xs_db", p), "xs_st"], writes=["xs_st"])
        A("dve", lambda e: e.tensor_scalar(y_st[:, :], y_st[:, :], rw2[:], None, ALU.mult),
          reads=["y_st", "rw2"], writes=["y_st"])
        A("dve", lambda e: e.tensor_tensor(y_st[:, :], y_st[:, :], xs_st[:, :], ALU.mult),
          reads=["y_st", "xs_st"], writes=["y_st"])
        A("act", lambda e, c=c: e.dma_start(out=y_dram[c * 128:(c + 1) * 128, :], in_=y_st[:, :]),
          reads=["y_st"], writes=["ych", ("y_d", c)], dma="Q_t")

    def psg(t):
        return PB[:, (t % 4) * 512:(t % 4 + 1) * 512] if t < 4 else (PS4, PS5, PS6, PS7)[t - 4][:, :]

    def psk(t):
        return ("PB", t) if t < 4 else ("PS4", "PS5", "PS6", "PS7")[t - 4]

    for j in range(32):
        A("act", lambda e, j=j: e.dma_start_transpose(y_fm[:, j * t_loc:(j + 1) * t_loc],
                                                      y_dram[:, j * 128:(j + 1) * 128]),
          reads=[("y_d", c) for c in range(nch)] + K_YFM[j], writes=K_YFM[j], dma="Q_t")
    for dgr in range(4):
        for j in range(32):
            A("pool", lambda e, dgr=dgr, j=j: e.dma_start(
                out=wo_buf[:, (j % 2) * 512:(j % 2) * 512 + 512],
                in_=w_out[j * 128:(j + 1) * 128, dgr * 512:(dgr + 1) * 512]),
              reads=K_WO[j % 2], writes=K_WO[j % 2] + [("wo_l", j % 2)], dma="Q_x")
            for t in range(ntt):
                A("pe", lambda e, j=j, t=t: e.matmul(
                    psg(t), y_fm[:, j * t_loc + t * 128:j * t_loc + (t + 1) * 128],
                    wo_buf[:, (j % 2) * 512:(j % 2) * 512 + 512],
                    start=(j == 0), stop=(j == 31)),
                  reads=K_YFM[j] + K_WO[j % 2] + [("wo_l", j % 2)] + ([psk(t)] if j == 0 else []),
                  writes=[psk(t)] if j == 31 else [])
        for t in range(ntt):
            p = t % 2
            A("sp", lambda e, dgr=dgr, t=t, p=p: e.dma_start(
                out=x_st[:, p * D + dgr * 512:p * D + (dgr + 1) * 512],
                in_=x_sl[t * 128:(t + 1) * 128, dgr * 512:(dgr + 1) * 512]),
              reads=[("x_st", p)], writes=[("x_st", p)], dma="Q_s")
            A("dve", lambda e, dgr=dgr, t=t, p=p: e.tensor_tensor(
                x_st[:, p * D + dgr * 512:p * D + (dgr + 1) * 512],
                x_st[:, p * D + dgr * 512:p * D + (dgr + 1) * 512], psg(t), ALU.add),
              reads=[("x_st", p), psk(t)], writes=[("x_st", p), psk(t)])
            A("act", lambda e, dgr=dgr, t=t, p=p: e.dma_start(
                out=gu_dram[t * 128:(t + 1) * 128, dgr * 512:(dgr + 1) * 512],
                in_=x_st[:, p * D + dgr * 512:p * D + (dgr + 1) * 512]),
              reads=[("x_st", p)], writes=[("r1ch", t % 2), ("r1_d", t, dgr)], dma="Q_t")
    for c in range(ntt):
        p = c % 2
        A("sp", lambda e, c=c, p=p: e.dma_start(out=x_st[:, p * D:(p + 1) * D],
                                                in_=gu_dram[c * 128:(c + 1) * 128, 0:D]),
          reads=[("r1_d", c, dgr) for dgr in range(4)] + [("x_st", p)],
          writes=[("x_st", p)], dma="Q_s")
        for bg in range(4):
            A("dve", lambda e, p=p, bg=bg: e.bn_stats(
                sc_a[:, bg * 6:(bg + 1) * 6], x_st[:, p * D + bg * 512:p * D + (bg + 1) * 512]),
              reads=[("x_st", p), "scz"], writes=["scz"])
        A("dve", lambda e: e.bn_aggr(sc_b[:, 0:2], sc_a[:, 0:24]), reads=["scz"], writes=["scz"])
        A("dve", lambda e: e.tensor_scalar(rw1[:], sc_b[:, 1:2], 1.0, 1e-5, ALU.mult, ALU.add),
          reads=["scz", "rw1"], writes=["rw1"])
        A("dve", lambda e: e.reciprocal(rw2[:], rw1[:]), reads=["rw1", "rw2"], writes=["rw2"])
        A("act", lambda e: e.activation(rw2[:], rw2[:], AF.Sqrt), reads=["rw2"], writes=["rw2"])
        A("dve", lambda e, p=p: e.tensor_scalar(x_st[:, p * D:(p + 1) * D], x_st[:, p * D:(p + 1) * D],
                                                sc_b[:, 0:1], rw2[:], ALU.subtract, ALU.mult),
          reads=[("x_st", p), "scz", "rw2"], writes=[("x_st", p)])
        A("act", lambda e, c=c, p=p: e.dma_start(out=ln1_dram[c * 128:(c + 1) * 128, :],
                                                 in_=x_st[:, p * D:(p + 1) * D]),
          reads=[("x_st", p)], writes=["lnch", ("ln1_d", c)], dma="Q_t")
        A("dve", lambda e, p=p: e.tensor_reduce(rw1[:], x_st[:, p * D:(p + 1) * D], AX.X, ALU.max,
                                                apply_absolute_value=True),
          reads=[("x_st", p), "rw1"], writes=["rw1"])
        A("dve", lambda e: e.tensor_scalar(rw1[:], rw1[:], 1.0 / 127.0, 1e-5, ALU.mult, ALU.add),
          reads=["rw1"], writes=["rw1"])
        A("dve", lambda e, c=c: e.tensor_copy(xscal[:, c:c + 1], rw1[:]),
          reads=["rw1", "xscal"], writes=["xscal"])
        A("dve", lambda e: e.reciprocal(rw2[:], rw1[:]), reads=["rw1", "rw2"], writes=["rw2"])
        A("dve", lambda e, p=p: e.tensor_scalar(x_st[:, p * D:(p + 1) * D], x_st[:, p * D:(p + 1) * D],
                                                rw2[:], CBIG, ALU.mult, ALU.add),
          reads=[("x_st", p), "rw2"], writes=[("x_st", p)])
        A("dve", lambda e, p=p: e.tensor_scalar(z_st[:, 0:D], x_st[:, p * D:(p + 1) * D],
                                                CBIG, None, ALU.subtract),
          reads=[("x_st", p), "z_st", ("zs", 0), ("zs", 1), ("zs", 2), ("zs", 3)],
          writes=["z_st", ("zs", 0), ("zs", 1), ("zs", 2), ("zs", 3)])
        A("act", lambda e, c=c: e.dma_start(out=xbf_dram[c * 128:(c + 1) * 128, :], in_=z_st[:, 0:D]),
          reads=["z_st"], writes=["xqch", ("xq_d", c)], dma="Q_t")
    for j in range(nk):
        A("act", lambda e, j=j: e.dma_start_transpose(xq_fm[:, j * t_loc:(j + 1) * t_loc],
                                                      xbf_dram[:, j * 128:(j + 1) * 128]),
          reads=[("xq_d", c) for c in range(ntt)] + K_XQ[j], writes=K_XQ[j], dma="Q_t")

    def psg(t):
        return PB[:, (t % 4) * 512:(t % 4 + 1) * 512] if t < 4 else (PS4, PS5, PS6, PS7)[t - 4][:, :]

    def psk(t):
        return ("PB", t) if t < 4 else ("PS4", "PS5", "PS6", "PS7")[t - 4]

    A("dve", lambda e: e.memset(runmax[:], 0.0), reads=["rms_sc"], writes=["rms_sc"])
    for t in range(ntt):
        A("dve", lambda e, t=t: e.tensor_tensor(xscal[:, 8 + t:9 + t], xscal[:, t:t + 1],
                                                wsraw[:, 0:1], ALU.mult),
          reads=["xscal", "rms_sc"], writes=["xscal"])
        A("dve", lambda e, t=t: e.tensor_tensor(xscal[:, 16 + t:17 + t], xscal[:, t:t + 1],
                                                wsraw[:, 1:2], ALU.mult),
          reads=["xscal", "rms_sc"], writes=["xscal"])
    for ig in range(16):
        for mm, kk in ((0, "qg"), (1, "qu")):
            alln = wqg_all if mm == 0 else wqu_all
            for j in range(nk):
                A("act", lambda e, alln=alln, j=j, ig=ig: e.dma_start_transpose(
                    wff_buf[:, (j % 2) * 4096:(j % 2) * 4096 + 512],
                    alln[ig * 512:(ig + 1) * 512, j * 128:(j + 1) * 128]),
                  reads=[(kk, (ig * 512) // DF8, ((ig * 512) % DF8) // 128 + dtt)
                         for dtt in range(4)] + K_WF[j % 2],
                  writes=K_WF[j % 2] + [("wff_l", j)], dma="Q_x")
                for t in range(ntt):
                    A("pe", lambda e, j=j, t=t: e.matmul(
                        psg(t), xq_fm[:, j * t_loc + t * 128:j * t_loc + (t + 1) * 128],
                        wff_buf[:, (j % 2) * 4096:(j % 2) * 4096 + 512],
                        start=(j == 0), stop=(j == nk - 1)),
                      reads=K_XQ[j] + K_WF[j % 2] + [("wff_l", j)] + ([psk(t)] if j == 0 else []),
                      writes=[psk(t)] if j == nk - 1 else [])
            if mm == 0:
                for t in range(ntt):
                    A("act", lambda e, t=t: e.activation(
                        xs_db[:, :].bitcast(F32)[:, t * 512:(t + 1) * 512], psg(t), AF.Silu,
                        scale=xscal[:, 8 + t:9 + t]),
                      reads=[psk(t), "xs_db_f", "xscal", ("xs_db", 0), ("xs_db", 1)],
                      writes=[psk(t), "xs_db_f", ("xs_db", 0), ("xs_db", 1)])
            else:
                for t in range(ntt):
                    A("dve", lambda e, t=t: e.scalar_tensor_tensor(
                        x_st[:, (t % 2) * D:(t % 2) * D + 512], psg(t), xscal[:, 16 + t:17 + t],
                        xs_db[:, :].bitcast(F32)[:, t * 512:(t + 1) * 512], ALU.mult, ALU.mult),
                      reads=[psk(t), "xs_db_f", "xscal", ("x_st", t % 2)],
                      writes=[("x_st", t % 2), psk(t)])
                    A("dve", lambda e, t=t: e.tensor_reduce(
                        rw1[:], x_st[:, (t % 2) * D:(t % 2) * D + 512], AX.X, ALU.max,
                        apply_absolute_value=True),
                      reads=[("x_st", t % 2), "rw1"], writes=["rw1"])
                    A("dve", lambda e, t=t: e.tensor_tensor(runmax[:, t:t + 1], runmax[:, t:t + 1],
                                                            rw1[:], ALU.max),
                      reads=["rw1", "rms_sc"], writes=["rms_sc"])
                    A("act", lambda e, t=t, ig=ig: e.dma_start(
                        out=gu_dram[t * 128:(t + 1) * 128, ig * 512:(ig + 1) * 512],
                        in_=x_st[:, (t % 2) * D:(t % 2) * D + 512]),
                      reads=[("x_st", t % 2), ("r1_d2",)], writes=[("guch", t % 4), ("gu_d", ig, t)], dma="Q_t")

    A("dve", lambda e: e.tensor_scalar(runmax[:], runmax[:], 1.0 / 127.0, 1e-5, ALU.mult, ALU.add),
      reads=["rms_sc"], writes=["rms_sc"])
    for t in range(ntt):
        A("dve", lambda e, t=t: e.tensor_tensor(xscal[:, 24 + t:25 + t], runmax[:, t:t + 1],
                                                wsraw[:, 2:3], ALU.mult),
          reads=["xscal", "rms_sc"], writes=["xscal"])
    A("dve", lambda e: e.reciprocal(wqacc[:, 0:8], runmax[:]), reads=["rms_sc", "rms_sc"],
      writes=["rms_sc"])
    for t in range(ntt):
        for ig in range(4):
            p = ig % 2
            A("sp", lambda e, t=t, ig=ig, p=p: e.dma_start(
                out=x_st[:, p * D:(p + 1) * D],
                in_=gu_dram[t * 128:(t + 1) * 128, ig * D:(ig + 1) * D]),
              reads=[("gu_d", gg, t) for gg in range(ig * 4, ig * 4 + 4)] + [("x_st", p)],
              writes=[("x_st", p)], dma="Q_s")
            A("dve", lambda e, t=t, p=p: e.tensor_scalar(x_st[:, p * D:(p + 1) * D],
                                                         x_st[:, p * D:(p + 1) * D],
                                                         wqacc[:, t:t + 1], CBIG, ALU.mult, ALU.add),
              reads=[("x_st", p), "rms_sc"], writes=[("x_st", p)])
            A("dve", lambda e, p=p: e.tensor_scalar(z_st[:, 0:D], x_st[:, p * D:(p + 1) * D],
                                                    CBIG, None, ALU.subtract),
              reads=[("x_st", p), "z_st"], writes=["z_st"])
            A("act", lambda e, t=t, ig=ig: e.dma_start(
                out=guq_dram[t * 128:(t + 1) * 128, ig * D:(ig + 1) * D], in_=z_st[:, 0:D]),
              reads=["z_st"], writes=["guqch", ("guq_d", t, ig)], dma="Q_t")

    GUQD = [("guq_d", t, ig) for t in range(8) for ig in range(4)]
    for qq in range(NQ):
        for j in range(DFF // 128):
            A("act", lambda e, j=j, qq=qq: e.dma_start_transpose(
                guq_fm[:, j * TQ:(j + 1) * TQ],
                guq_dram[qq * TQ:(qq + 1) * TQ, j * 128:(j + 1) * 128]),
              reads=GUQD + K_GQ[j], writes=K_GQ[j], dma="Q_t")
        for j in range(DFF // 128):
            A("act", lambda e, j=j: e.dma_start_transpose(
                wff_buf[:, (j % 2) * 4096 + 2048:(j % 2) * 4096 + 4096],
                wqd_all[0:D, j * 128:(j + 1) * 128]),
              reads=[("qd", (j * 128) // DD8, tt) for tt in range(D // 128)] + K_WF[j % 2],
              writes=K_WF[j % 2] + [("wfd_l", j)], dma="Q_x")
            for t in range(TQ // 128):
                for dgr in range(4):
                    bk = (t * 4 + dgr)
                    A("pe", lambda e, j=j, t=t, dgr=dgr, qq=qq, bk=bk: e.matmul(
                        psg(bk), guq_fm[:, j * TQ + t * 128:j * TQ + (t + 1) * 128],
                        wff_buf[:, (j % 2) * 4096 + 2048 + dgr * 512:(j % 2) * 4096 + 2048 + (dgr + 1) * 512],
                        start=(j == 0), stop=(j == DFF // 128 - 1)),
                      reads=K_GQ[j] + K_WF[j % 2] + [("wfd_l", j)] + ([psk(bk)] if j == 0 else []),
                      writes=[psk(bk)] if j == DFF // 128 - 1 else [])
        for t in range(TQ // 128):
            c = qq * (TQ // 128) + t
            p = t % 2
            A("sp", lambda e, c=c, p=p: e.dma_start(out=x_st[:, p * D:(p + 1) * D],
                                                    in_=ln1_dram[c * 128:(c + 1) * 128, :]),
              reads=[("ln1_d", c), ("x_st", p)], writes=[("x_st", p)], dma="Q_s")
            for dgr in range(4):
                bk = t * 4 + dgr
                A("dve", lambda e, t=t, dgr=dgr, p=p, bk=bk, c=c: e.scalar_tensor_tensor(
                    x_st[:, p * D + dgr * 512:p * D + (dgr + 1) * 512],
                    psg(bk), xscal[:, 24 + c:25 + c],
                    x_st[:, p * D + dgr * 512:p * D + (dgr + 1) * 512], ALU.mult, ALU.add),
                  reads=[("x_st", p), psk(bk), "xscal"], writes=[("x_st", p), psk(bk)])
            for bg in range(4):
                A("dve", lambda e, p=p, bg=bg: e.bn_stats(
                    sc_a[:, bg * 6:(bg + 1) * 6], x_st[:, p * D + bg * 512:p * D + (bg + 1) * 512]),
                  reads=[("x_st", p), "scz"], writes=["scz"])
            A("dve", lambda e: e.bn_aggr(sc_b[:, 0:2], sc_a[:, 0:24]),
              reads=["scz", "scz"], writes=["scz"])
            A("dve", lambda e: e.tensor_scalar(rw1[:], sc_b[:, 1:2], 1.0, 1e-5, ALU.mult, ALU.add),
              reads=["scz", "rw1"], writes=["rw1"])
            A("dve", lambda e: e.reciprocal(rw2[:], rw1[:]), reads=["rw1", "rw2"], writes=["rw2"])
            A("act", lambda e: e.activation(rw2[:], rw2[:], AF.Sqrt), reads=["rw2"], writes=["rw2"])
            A("dve", lambda e, p=p: e.tensor_scalar(x_st[:, p * D:(p + 1) * D],
                                                    x_st[:, p * D:(p + 1) * D],
                                                    sc_b[:, 0:1], rw2[:], ALU.subtract, ALU.mult),
              reads=[("x_st", p), "scz", "rw2"], writes=[("x_st", p)])
            A("pool", lambda e, c=c, p=p: e.dma_start(out=out_hbm[c * 128:(c + 1) * 128, :],
                                                      in_=x_st[:, p * D:(p + 1) * D]),
              reads=[("x_st", p)], writes=["outch", ("out_d", c)], dma="Q_o")

    def dmp(name, src, rkeys):
        if name in dbg_out:
            A("pool", lambda e, s=src: e.dma_start(out=dbg_out[name], in_=s), reads=rkeys, writes=["dbgch"], dma="Q_o")

    dmp("dtl", dtl_fm[:], ["dtl"])
    if "xfm0" in dbg_out:
        A("pool", lambda e: e.dma_start(out=dbg_out["xfm0"], in_=x_fm[:, 0:t_loc]),
          reads=K_XFM[0], writes=["dbgch"], dma="Q_o")
    dmp("dt", dt_loc[:], ["dt"])
    dmp("u", u_fm[:], ["u"])
    dmp("aloc", aloc_fm[:], ["aloc"])
    dmp("cum", cum_l[:], ["cum"])
    dmp("ev", ev_fm[:], ["ev"])
    if "xsilu" in dbg_out:
        for c in range(nch):
            A("pool", lambda e, c=c: e.dma_start(out=dbg_out["xsilu"][c * 128:(c + 1) * 128, :],
                                                 in_=xs_dram[c * 128:(c + 1) * 128, :].bitcast(BF16))
              if False else e.dma_start(out=dbg_out["xsilu"][c * 128:(c + 1) * 128, :],
                                        in_=xs_dram[c * 128:(c + 1) * 128, :]),
              reads=[("xs_d", c)], writes=["dbgch"], dma="Q_o")
    if "zdmp" in dbg_out:
        for c in range(nch):
            A("pool", lambda e, c=c: e.dma_start(out=dbg_out["zdmp"][c * 128:(c + 1) * 128, :],
                                                 in_=z_dram[c * 128:(c + 1) * 128, :]),
              reads=[("z_d", c)], writes=["dbgch"], dma="Q_o")
    if "guf" in dbg_out:
        for t in range(ntt):
            A("pool", lambda e, t=t: e.dma_start(out=dbg_out["guf"][t * 128:(t + 1) * 128, :],
                                                 in_=gu_dram[t * 128:(t + 1) * 128, :]),
              reads=[("gu_d", ig, t) for ig in range(16)], writes=["dbgch"], dma="Q_o")
    if "xqd" in dbg_out:
        for c in range(ntt):
            A("pool", lambda e, c=c: e.dma_start(out=dbg_out["xqd"][c * 128:(c + 1) * 128, :],
                                                 in_=xbf_dram[c * 128:(c + 1) * 128, :]),
              reads=[("xq_d", c)], writes=["dbgch"], dma="Q_o")
    if "qrowd" in dbg_out:
        A("pool", lambda e: e.dma_start(out=dbg_out["qrowd"], in_=qrow[0:1, 0:8]),
          reads=["rms_sc"], writes=["dbgch"], dma="Q_o")
    if "ymix" in dbg_out:
        for c in range(nch):
            A("pool", lambda e, c=c: e.dma_start(out=dbg_out["ymix"][c * 128:(c + 1) * 128, :],
                                                 in_=y_dram[c * 128:(c + 1) * 128, :]),
              reads=[("y_d", c)], writes=["dbgch"], dma="Q_o")
    if "ln1" in dbg_out:
        for c in range(ntt):
            A("pool", lambda e, c=c: e.dma_start(out=dbg_out["ln1"][c * 128:(c + 1) * 128, :],
                                                 in_=ln1_dram[c * 128:(c + 1) * 128, :]),
              reads=[("ln1_d", c)], writes=["dbgch"], dma="Q_o")

    sems = {}
    sem_ctx = []
    names = ["S_" + e for e in Emitter.ENGS] + sorted(em.dma_counts.keys())
    for nm in names:
        s = nc.semaphore(nm)
        sem_ctx.append(s)
        sems[nm] = s.__enter__()
    finals = {("S_" + e): c for e, c in em.counts.items() if c}
    finals.update({k: v for k, v in em.dma_counts.items() if v})
    per_eng = {e: [] for e in Emitter.ENGS}
    for (eng, fn, waits, sem, inc) in em.steps:
        per_eng[eng].append((fn, waits, sem, inc))
    blk = nc.Block()
    block = blk.__enter__()

    def runner(e):
        def run(obj):
            for (fn, waits, sem, inc) in per_eng[e]:
                for (s, cc) in waits:
                    obj.wait_ge(sems[s], cc)
                fn(obj).then_inc(sems[sem], inc)
            if e == "pool":
                for s, cc in sorted(finals.items()):
                    obj.wait_ge(sems[s], cc)
        return run

    block.tensor(runner("pe"))
    block.scalar(runner("act"))
    block.vector(runner("dve"))
    block.gpsimd(runner("pool"))
    block.sync(runner("sp"))
    blk.__exit__(None, None, None)
    for s in reversed(sem_ctx):
        s.__exit__(None, None, None)
    for t in reversed(ctx):
        t.__exit__(None, None, None)
    return nc


def make_core_inputs(core, full, t_loc, ncores):
    cps = ncores // 2
    b, slot = core // cps, core % cps
    x = np.asarray(full["L0_attn_x"])
    t0 = slot * t_loc
    d = {}
    d["x_sl"] = np.ascontiguousarray(x[b, t0:t0 + t_loc, :])
    npre = max(cps - 1, 1)
    xp = np.zeros((npre * t_loc, D), np.float32)
    if t0 > 0:
        xp[npre * t_loc - t0:] = x[b, 0:t0, :]
    d["x_pre"] = xp
    if t0 + t_loc < x.shape[1]:
        d["x_ex"] = np.ascontiguousarray(x[b, t0 + t_loc:t0 + t_loc + 1, :])
        bm = 1.0
    else:
        d["x_ex"] = np.zeros((1, D), np.float32)
        bm = 0.0
    d["w_in"] = np.asarray(full["L0_attn_W_in"])
    d["w_out"] = np.asarray(full["L0_attn_W_out"])
    d["wg_hbm"] = np.ascontiguousarray(np.asarray(full["L0_ffn_w_gate"]))
    d["wu_hbm"] = np.ascontiguousarray(np.asarray(full["L0_ffn_w_up"]))
    d["wd_hbm"] = np.ascontiguousarray(np.asarray(full["L0_ffn_w_down"]))
    d["a_log"] = np.asarray(full["L0_attn_A_log"]).reshape(H, 1).astype(np.float32)
    d["dt_bias"] = np.asarray(full["L0_attn_dt_bias"]).reshape(H, 1).astype(np.float32)
    d["bnb_in"] = np.asarray(full["L0_attn_B_norm_bias"]).reshape(1, NS).astype(np.float32)
    d["cnb_in"] = np.asarray(full["L0_attn_C_norm_bias"]).reshape(1, NS).astype(np.float32)
    d["identf_in"] = np.eye(128, dtype=np.float32)
    d["triu_in"] = np.triu(np.ones((128, 128), np.float32))
    d["maskr_in"] = (-1e4 * np.eye(128)).astype(np.float32)
    d["bmask_in"] = np.full((H, 1), bm, np.float32)
    return d


def kernel(**inputs):
    t_loc, ncores = 1024, NC
    nc = build_program(t_loc=t_loc, ncores=ncores)
    in_maps = [make_core_inputs(c, inputs, t_loc, ncores) for c in range(ncores)]
    res = run_bass_kernel_spmd(nc, in_maps, core_ids=list(range(ncores)))
    out = np.zeros((2, 4096, D), np.float32)
    cps = ncores // 2
    for c in range(ncores):
        b, slot = c // cps, c % cps
        out[b, slot * t_loc:(slot + 1) * t_loc, :] = res.results[c]["out"]
    return out


NC = 8


# version 11
# speedup vs baseline: 1.3621x; 1.0629x over previous
import numpy as np
import concourse.bass as bass
import concourse.mybir as mybir
from concourse.bass_utils import run_bass_kernel_spmd

F32 = mybir.dt.float32
BF16 = mybir.dt.bfloat16
I32 = mybir.dt.int32
ALU = mybir.AluOpType
AF = mybir.ActivationFunctionType
AX = mybir.AxisListType

D = 2048
DI = 4096
H = 64
NS = 64
DP = 8480
DFF = 8192
CBIG = 12582912.0
TWO_PI = float(2 * np.pi)
PI = float(np.pi)
OFF_B = 2 * DI
OFF_DT = 2 * DI + 2 * NS
OFF_TH = OFF_DT + 2 * H


class Emitter:
    ENGS = ("pe", "act", "dve", "pool", "sp")

    def __init__(self):
        self.steps = []
        self.writers = {}
        self.readers = {}
        self.counts = {e: 0 for e in self.ENGS}
        self.dma_counts = {}
        self.known = {e: {} for e in self.ENGS}

    def add(self, eng, fn, reads=(), writes=(), dma=None):
        need = {}
        for k in reads:
            w = self.writers.get(k)
            if w:
                need[w[0]] = max(need.get(w[0], 0), w[1])
        for k in writes:
            w = self.writers.get(k)
            if w:
                need[w[0]] = max(need.get(w[0], 0), w[1])
            for s, c in self.readers.get(k, ()):
                need[s] = max(need.get(s, 0), c)
        kn = self.known[eng]
        waits = []
        for s, c in sorted(need.items(), key=str):
            if kn.get(s, 0) < c:
                waits.append((s, c))
                kn[s] = c
        if dma is None:
            self.counts[eng] += 1
            sem, cnt, inc = "S_" + eng, self.counts[eng], 1
        else:
            key = writes[0] if writes else reads[0]
            dma = "D" + ("s" if eng == "pool" else "h") + repr(key)
            self.dma_counts[dma] = self.dma_counts.get(dma, 0) + 16
            sem, cnt, inc = dma, self.dma_counts[dma], 16
        self.steps.append((eng, fn, waits, sem, inc))
        for k in reads:
            self.readers.setdefault(k, []).append((sem, cnt))
        for k in writes:
            self.writers[k] = (sem, cnt)
            self.readers[k] = []


def build_program(t_loc=1024, ncores=8, dbg=()):
    CH = 128
    nch = t_loc // CH
    ntt = nch
    nk = 16
    TL1 = t_loc + 1
    ntg = t_loc // 512
    DF8 = DFF // 8
    DD8 = DFF // 8
    NQ = 4
    TQ = t_loc // NQ
    NSEG = max(ncores // 2, 1)
    assert DD8 <= D and DF8 <= DFF
    nc = bass.Bass("TRN2", target_bir_lowering=False)
    em = Emitter()
    A = em.add
    ctx = []

    def sbuf(name, shape, dtype):
        t = nc.sbuf_tensor(name, shape, dtype)
        ctx.append(t)
        return t.__enter__()

    def psumt(name, shape, dtype):
        t = nc.psum_tensor(name, shape, dtype)
        ctx.append(t)
        return t.__enter__()

    def din(name, shape, dt=F32, kind="ExternalInput"):
        return nc.dram_tensor(name, shape, dt, kind=kind).ap()

    def dit(name, shape, dt, shared=False):
        if shared:
            return nc.dram_tensor(name, shape, dt, addr_space="Shared").ap()
        return nc.dram_tensor(name, shape, dt).ap()

    x_sl = din("x_sl", [t_loc, D])
    x_pre = din("x_pre", [max(NSEG - 1, 1) * t_loc, D])
    x_ex = din("x_ex", [1, D])
    w_in = din("w_in", [D, DP])
    w_out = din("w_out", [DI, D])
    wg_hbm = din("wg_hbm", [DFF, D])
    wu_hbm = din("wu_hbm", [DFF, D])
    wd_hbm = din("wd_hbm", [D, DFF])
    a_log = din("a_log", [H, 1])
    dt_bias = din("dt_bias", [H, 1])
    bnb_in = din("bnb_in", [1, NS])
    cnb_in = din("cnb_in", [1, NS])
    identf_in = din("identf_in", [128, 128])
    triu_in = din("triu_in", [128, 128])
    maskr_in = din("maskr_in", [128, 128])
    bmask_in = din("bmask_in", [H, 1])
    out_hbm = din("out", [t_loc, D], F32, "ExternalOutput")

    xbf_dram = dit("xbf_dram", [t_loc, D], BF16)
    ahl_dram = dit("ahl_dram", [2 * nch, H * CH], BF16)
    z_dram = dit("z_dram", [t_loc, DI], BF16)
    xs_dram = dit("xs_dram", [t_loc, DI], BF16)
    prev_dram = dit("prev_dram", [nch * NS, DI], BF16)
    y_dram = dit("y_dram", [t_loc, DI], BF16)
    ln1_dram = dit("ln1_dram", [t_loc, D], F32)
    gu_dram = dit("gu_dram", [t_loc, DFF], F32)
    guq_dram = dit("guq_dram", [t_loc, DFF], BF16)
    wqt_dram = dit("wqt_dram", [max(DF8, D), max(D, DD8)], BF16)
    row_dram = dit("row_dram", [4, 256], F32)
    wqg_all = dit("wqg_all", [DFF, D], BF16)
    wqu_all = dit("wqu_all", [DFF, D], BF16)
    wqd_all = dit("wqd_all", [D, DFF], BF16)

    dbg_out = {}
    for name, shape in dbg:
        dbg_out[name] = din("dbg_" + name, list(shape), F32, "ExternalOutput")

    identf = sbuf("identf", [64, 64], F32)
    identb = sbuf("identb", [128, 128], BF16)
    triu = sbuf("triu", [128, 128], BF16)
    maskr = sbuf("maskr", [128, 128], BF16)
    ones_rb = sbuf("ones_rb", [1, 128], BF16)
    ones_rf = sbuf("ones_rf", [1, 128], F32)
    ones_cf = sbuf("ones_cf", [128, 1], F32)
    aneg = sbuf("aneg", [H, 1], F32)
    alog_s = sbuf("alog_s", [H, 1], F32)
    dtb_s = sbuf("dtb_s", [H, 1], F32)
    bnb_m = sbuf("bnb_m", [128, NS], F32)
    cnb_m = sbuf("cnb_m", [128, NS], F32)
    bmask_s = sbuf("bmask_s", [H, 1], F32)
    AR = sbuf("AR", [128, 49152], BF16)
    ARf = AR[:, :].bitcast(F32)
    ARi = AR[:, :].bitcast(I32)
    _p2b = 33 * 512
    rr_f = ARf[0:32, _p2b:_p2b + t_loc]
    rr_i = ARi[0:32, _p2b + t_loc:_p2b + 2 * t_loc]
    rr_g = rr_f
    rr_m = ARf[0:32, _p2b + t_loc:_p2b + 2 * t_loc]
    cum_l = ARf[0:32, _p2b + 2 * t_loc:_p2b + 3 * t_loc]
    cs_bf = AR[0:32, 2 * (_p2b + 3 * t_loc):2 * (_p2b + 3 * t_loc) + 2 * t_loc]
    th_sg = ARf[0:32, _p2b + 4 * t_loc:_p2b + 4 * t_loc + 512]
    P2B_KEYS = [("AR", b) for b in range(33, 33 + (4 * t_loc + 512 + 511) // 512)]
    dtl_fm = ARf[:, _p2b + 4 * t_loc + 512:_p2b + 5 * t_loc + 513]
    wsm = AR[:, 46336:46336 + nk * 160]
    ev_tm = ARf[:, 41 * 512:41 * 512 + nch * H]
    qd_tm = AR[:, 43008:43008 + nch * H]
    ws_tm = AR[:, 43008 + 512:43008 + 512 + nch * H]
    c0_tm = AR[:, 44032:44032 + nch * H]
    aflat_h = AR[0:1, 0:H * CH]
    K_AFH = [("AR", b) for b in range((H * CH + 1023) // 1024)]
    aflat_l = AR[0:1, 8 * 1024:8 * 1024 + H * CH]
    K_AFL = [("AR", b) for b in range(8, 8 + (H * CH + 1023) // 1024)]
    x_st = sbuf("x_st", [128, 2 * D], F32)
    z_st = sbuf("z_st", [128, D], BF16)
    xs_st = sbuf("xs_st", [128, DI], BF16)
    y_st = sbuf("y_st", [128, DI], BF16)
    xs_db = sbuf("xs_db", [128, 2 * DI], BF16)
    xe_fm = sbuf("xe_fm", [128, 16], BF16)
    dt_loc = sbuf("dt_loc", [H, TL1], F32)
    lamS = sbuf("lamS", [H, TL1], F32)
    f1 = sbuf("f1", [H, TL1], F32)
    f2 = sbuf("f2", [H, TL1], F32)
    sc128 = f2[:, 0:128]
    wg_fm = sbuf("wg_fm", [H, t_loc], F32)
    u_fm = sbuf("u_fm", [H, t_loc], F32)
    aloc_fm = sbuf("aloc_fm", [H, t_loc], F32)
    ev_fm = sbuf("ev_fm", [H, t_loc], F32)
    ah_fm = sbuf("ah_fm", [H, t_loc], BF16)
    al_fm = sbuf("al_fm", [H, t_loc], BF16)
    vrow = sbuf("vrow", [1, 512], F32)
    carry = sbuf("carry", [32, 1], F32)
    cos_tm = sbuf("cos_tm", [128, nch * 32], BF16)
    sin_tm = sbuf("sin_tm", [128, nch * 32], BF16)
    bc_t = sbuf("bc_t", [128, nch * 128], F32)
    bcf = sbuf("bcf", [128, 2 * NS], F32)
    b_tmbA = sbuf("b_tmbA", [128, nch * NS], BF16)
    b_fmb = sbuf("b_fmb", [NS, t_loc], BF16)
    c_fmb = sbuf("c_fmb", [NS, t_loc], BF16)
    rms_sc = sbuf("rms_sc", [128, NS], F32)
    wqacc = rms_sc[:, 0:24]
    qsc = rms_sc[:, 24:28]
    runmax = rms_sc[:, 28:36]
    qrow = rms_sc[0:1, 40:48]
    wsraw = rms_sc[:, 36:39]
    rw1 = sbuf("rw1", [128, 1], F32)
    rw2 = sbuf("rw2", [128, 1], F32)
    sdd_tm = sbuf("sdd_tm", [128, nch], F32)
    xscal = sbuf("xscal", [128, 4 * 8], F32)
    pwf = sbuf("pwf", [128, 1024], F32)
    pwacc = sbuf("pwacc", [128, 25], F32)

    cd_col = sbuf("cd_col", [H, 1], F32)
    cd_row = sbuf("cd_row", [1, H], F32)
    s_bf = sbuf("s_bf", [128, CH], BF16)
    sc128b = s_bf
    u2_bf = sbuf("u2_bf", [128, 512], BF16)
    sc_a = u2_bf[:, :].bitcast(F32)[:, 0:64]
    sc_b = u2_bf[:, :].bitcast(F32)[:, 64:128]
    coef_s = sbuf("coef_s", [4, H], F32)

    def ark(lo, n):
        return [("AR", b) for b in range(lo // 1024, (lo + n + 1023) // 1024)]

    x_fm = AR[:, 0:nk * t_loc]
    K_XFM = [ark(j * t_loc, t_loc) for j in range(nk)]
    w_buf = AR[:, nk * t_loc:nk * t_loc + 2 * 512]
    K_WB = [ark(nk * t_loc + p * 512, 512) for p in range(2)]
    st_run = ARf[0:NS, 8704:12800]
    K_STR = ark(17408, 2 * DI)
    wx_bf = AR[:, 25600:29696]
    K_WX = ark(25600, DI)
    prev_bf = AR[0:NS, 29696:33792]
    K_PRV = ark(29696, DI)
    y_fm = AR[:, 0:32 * t_loc]
    K_YFM = [ark(j * t_loc, t_loc) for j in range(32)]
    wo_buf = AR[:, 32 * t_loc:32 * t_loc + 4 * 512]
    K_WO = [ark(32 * t_loc + p * 512, 512) for p in range(4)]
    xq_fm = AR[:, 0:nk * t_loc]
    K_XQ = K_XFM
    wff_buf = AR[:, nk * t_loc:nk * t_loc + 8 * 1024]
    K_WF = [ark(nk * t_loc + p * 4096, 4096) for p in range(2)]
    guq_fm = AR[:, 24 * 1024:24 * 1024 + (DFF // 128) * TQ]
    K_GQ = [ark(24 * 1024 + j * TQ, TQ) for j in range(DFF // 128)]

    PB = psumt("PB", [128, 2048], F32)
    PBb = PB[:, :].bitcast(BF16)
    PS4 = psumt("PS4", [128, 512], F32)
    PS5 = psumt("PS5", [128, 512], F32)
    PS6 = psumt("PS6", [128, 512], F32)
    PS7 = psumt("PS7", [128, 512], F32)
    PBK = [("PB", 0), ("PB", 1), ("PB", 2), ("PB", 3)]

    A("pool", lambda e: e.dma_start(out=identb[:], in_=identf_in), writes=["p0ch", "identb"], dma="Q_x")
    A("pool", lambda e: e.dma_start(out=triu[:], in_=triu_in), writes=["p0ch", "triu"], dma="Q_x")
    A("pool", lambda e: e.dma_start(out=maskr[:], in_=maskr_in), writes=["p0ch", "maskr"], dma="Q_x")
    A("sp", lambda e: e.dma_start(out=identf[:], in_=identf_in[0:64, 0:64]), writes=["p0chs", "identf"], dma="Q_s")
    for t, s, k in ((alog_s, a_log, "alog"), (dtb_s, dt_bias, "dtb"), (bmask_s, bmask_in, "bmask")):
        A("sp", lambda e, t=t, s=s: e.dma_start(out=t[:], in_=s), writes=["p0chs", k], dma="Q_s")
    A("dve", lambda e: e.memset(ones_rb[:], 1.0), writes=["ones_rb"])
    A("dve", lambda e: e.memset(ones_rf[:], 1.0), writes=["ones_rf"])
    A("dve", lambda e: e.memset(ones_cf[:], 1.0), writes=["ones_cf"])
    A("act", lambda e: e.activation(aneg[:], alog_s[:], AF.Exp), reads=["alog"], writes=["aneg"])
    A("dve", lambda e: e.tensor_scalar(aneg[:], aneg[:], -1.0, None, ALU.mult),
      reads=["aneg"], writes=["aneg"])
    for src, dstm, k in ((bnb_in, bnb_m, "bnb_m"), (cnb_in, cnb_m, "cnb_m")):
        A("sp", lambda e, s=src: e.dma_start(out=vrow[0:1, 0:NS], in_=s), reads=["vrow"],
          writes=["vrow"], dma="Q_s")
        A("pe", lambda e: e.matmul(PS4[:, 0:NS], ones_rf[:], vrow[0:1, 0:NS], start=True, stop=True),
          reads=["ones_rf", "vrow", "PS4"], writes=["PS4"])
        A("dve", lambda e, d=dstm: e.tensor_copy(d[:], PS4[:, 0:NS]), reads=["PS4"], writes=["PS4", k])

    for j in range(nk):
        A("pool", lambda e, j=j: e.dma_start(out=wsm[:, j * 160:j * 160 + 128],
                                             in_=w_in[j * 128:(j + 1) * 128, OFF_DT:OFF_DT + 128]),
          writes=[("wsmch", j % 2), ("wsm", j)], dma="Q_x")
        A("pool", lambda e, j=j: e.dma_start(out=wsm[:, j * 160 + 128:(j + 1) * 160],
                                             in_=w_in[j * 128:(j + 1) * 128, OFF_TH:OFF_TH + 32]),
          writes=[("wsmch", j % 2), ("wsm2", j)], dma="Q_x")

    def front(seg):
        own = seg == NSEG - 1
        if own:
            xsrc, xex_src = x_sl, x_ex
        elif seg == NSEG - 2:
            xsrc = x_pre[seg * t_loc:(seg + 1) * t_loc, :]
            xex_src = x_sl[0:1, :]
        else:
            xsrc = x_pre[seg * t_loc:(seg + 1) * t_loc, :]
            xex_src = x_pre[(seg + 1) * t_loc:(seg + 1) * t_loc + 1, :]
        A("pool", lambda e: e.dma_start(out=xbf_dram[:, :], in_=xsrc[:, :]), writes=["xbfd"], dma="Q_x")
        for j in range(nk):
            A("act", lambda e, j=j: e.dma_start_transpose(x_fm[:, j * t_loc:(j + 1) * t_loc],
                                                          xbf_dram[:, j * 128:(j + 1) * 128]),
              reads=["xbfd"] + K_XFM[j], writes=K_XFM[j], dma="Q_t")
        A("sp", lambda e: e.dma_start(out=x_st[0:1, 0:D], in_=xex_src), reads=[("x_st", 0)],
          writes=[("x_st", 0)], dma="Q_s")
        for j in range(16):
            A("pe", lambda e, j=j: e.matmul(PS4[:, j:j + 1], x_st[0:1, j * 128:(j + 1) * 128],
                                            ones_rf[0:1, 0:1], start=True, stop=True),
              reads=[("x_st", 0), "ones_rf"] + (["PS4"] if j == 0 else []),
              writes=["PS4"] if j == 15 else [])
        A("dve", lambda e: e.tensor_copy(xe_fm[:], PS4[:, 0:16]), reads=["PS4"], writes=["PS4", "xe_fm"])

        for g in range(ntg):
            for j in range(nk):
                A("pe", lambda e, g=g, j=j: e.matmul(PS7[0:128, 0:512], wsm[:, j * 160:j * 160 + 128],
                                                     x_fm[:, j * t_loc + g * 512:j * t_loc + (g + 1) * 512],
                                                     start=(j == 0), stop=(j == nk - 1)),
                  reads=K_XFM[j] + [("wsm", j)] + (["PS7"] if j == 0 else []),
                  writes=["PS7"] if j == nk - 1 else [])
            A("dve", lambda e, g=g: e.tensor_copy(dtl_fm[:, g * 512:(g + 1) * 512], PS7[0:128, 0:512]),
              reads=["PS7", "dtl"], writes=["PS7", "dtl"] + [("wst", cc) for cc in range(nch)])
        for j in range(nk):
            A("pe", lambda e, j=j: e.matmul(PS5[0:128, 0:1], wsm[:, j * 160:j * 160 + 128], xe_fm[:, j:j + 1],
                                            start=(j == 0), stop=(j == nk - 1)),
              reads=["xe_fm", ("wsm", j)] + (["PS5"] if j == 0 else []),
              writes=["PS5"] if j == nk - 1 else [])
        A("dve", lambda e: e.tensor_copy(dtl_fm[:, t_loc:TL1], PS5[0:128, 0:1]),
          reads=["PS5", "dtl"], writes=["PS5", "dtl"] + [("wst", cc) for cc in range(nch)])

        def psg(t):
            return PB[:, (t % 4) * 512:(t % 4 + 1) * 512] if t < 4 else (PS4, PS5, PS6, PS7)[t - 4][:, :]

        def psk(t):
            return ("PB", t) if t < 4 else ("PS4", "PS5", "PS6", "PS7")[t - 4]

        for g in (([16] + list(range(8, 16)) + list(range(8))) if own
                  else ([16] + list(range(8, 16)))):
            gw = 512 if g < 16 else 128
            goff = g * 512 if g < 16 else OFF_B
            for j in range(nk):
                A("pool", lambda e, g=g, j=j, gw=gw, goff=goff: e.dma_start(
                    out=w_buf[:, (j % 2) * 512:(j % 2) * 512 + gw],
                    in_=w_in[j * 128:(j + 1) * 128, goff:goff + gw]),
                  reads=K_WB[j % 2], writes=K_WB[j % 2] + [("w_l", j % 2)], dma="Q_x")
                for t in range(ntt):
                    A("pe", lambda e, g=g, j=j, t=t, gw=gw: e.matmul(
                        psg(t)[:, 0:gw], x_fm[:, j * t_loc + t * 128:j * t_loc + (t + 1) * 128],
                        w_buf[:, (j % 2) * 512:(j % 2) * 512 + gw],
                        start=(j == 0), stop=(j == nk - 1)),
                      reads=K_XFM[j] + K_WB[j % 2] + [("w_l", j % 2)] + ([psk(t)] if j == 0 else []),
                      writes=[psk(t)] if j == nk - 1 else [])
            for t in range(ntt):
                if g < 8:
                    A("dve", lambda e, g=g, t=t: e.tensor_copy(z_st[:, (t % 4) * 512:(t % 4 + 1) * 512],
                                                               psg(t)),
                      reads=[psk(t), ("zs", t % 4)], writes=[psk(t), ("zs", t % 4)])
                    A("act", lambda e, g=g, t=t: e.dma_start(
                        out=z_dram[t * 128:(t + 1) * 128, g * 512:(g + 1) * 512],
                        in_=z_st[:, (t % 4) * 512:(t % 4 + 1) * 512]),
                      reads=[("zs", t % 4)], writes=[("zch", t % 2), ("zs", t % 4), ("z_d", t)], dma="Q_t")
                elif g < 16:
                    A("act", lambda e, g=g, t=t: e.activation(xs_st[:, (t % 4) * 512:(t % 4 + 1) * 512],
                                                              psg(t), AF.Silu),
                      reads=[psk(t), ("xss", t % 4)], writes=[psk(t), ("xss", t % 4)])
                    A("act", lambda e, g=g, t=t: e.dma_start(
                        out=xs_dram[t * 128:(t + 1) * 128, (g - 8) * 512:(g - 7) * 512],
                        in_=xs_st[:, (t % 4) * 512:(t % 4 + 1) * 512]),
                      reads=[("xss", t % 4)], writes=[("xsch", t % 2), ("xss", t % 4), ("xs_d", t)], dma="Q_t")
                else:
                    A("dve", lambda e, t=t: e.tensor_copy(bc_t[:, t * 128:(t + 1) * 128],
                                                          psg(t)[:, 0:128]),
                      reads=[psk(t), "bc_t"], writes=[psk(t), "bc_t", ("bc", t)])

        A("dve", lambda e: e.tensor_scalar(f1[:], dtl_fm[0:64, :], dtb_s[:], None, ALU.add),
          reads=["dtl", "dtb", "f1"], writes=["f1"])
        A("act", lambda e: e.activation(f2[:], f1[:], AF.Abs), reads=["f1", "f2"], writes=["f2"])
        A("act", lambda e: e.activation(f2[:], f2[:], AF.Exp, scale=-1.0), reads=["f2"], writes=["f2"])
        A("act", lambda e: e.activation(f2[:], f2[:], AF.Ln, bias=1.0), reads=["f2"], writes=["f2"])
        A("dve", lambda e: e.tensor_scalar(f1[:], f1[:], 0.0, None, ALU.max), reads=["f1"], writes=["f1"])
        A("dve", lambda e: e.tensor_tensor(dt_loc[:], f1[:], f2[:], ALU.add),
          reads=["f1", "f2", "dt"], writes=["dt"])
        A("dve", lambda e: e.tensor_scalar(dt_loc[:], dt_loc[:], 0.5, 1e-4, ALU.min, ALU.max),
          reads=["dt"], writes=["dt"])
        A("act", lambda e: e.activation(lamS[:], dtl_fm[64:128, :], AF.Sigmoid),
          reads=["dtl", "lamS"], writes=["lamS"])
        A("dve", lambda e: e.tensor_tensor(wg_fm[:], dt_loc[:, 0:t_loc], lamS[:, 0:t_loc], ALU.mult),
          reads=["dt", "lamS", "wg"], writes=["wg"])
        A("dve", lambda e: e.tensor_tensor(lamS[:], dt_loc[:], lamS[:], ALU.mult),
          reads=["dt", "lamS"], writes=["lamS"])
        A("dve", lambda e: e.tensor_tensor(lamS[:], dt_loc[:], lamS[:], ALU.subtract),
          reads=["dt", "lamS"], writes=["lamS"])
        if own:
            A("dve", lambda e: e.tensor_scalar(lamS[:, t_loc:TL1], lamS[:, t_loc:TL1],
                                               bmask_s[:], None, ALU.mult),
              reads=["lamS", "bmask"], writes=["lamS"])
        A("dve", lambda e: e.tensor_tensor(u_fm[:], wg_fm[:], lamS[:, 1:TL1], ALU.add),
          reads=["wg", "lamS", "u"], writes=["u"])
        A("dve", lambda e: e.tensor_scalar(f1[:, 0:t_loc], dt_loc[:, 0:t_loc], aneg[:], None, ALU.mult),
          reads=["dt", "aneg", "f1"], writes=["f1"])
        for c in range(nch):
            A("dve", lambda e, c=c: e.tensor_tensor_scan(
                aloc_fm[:, c * CH:(c + 1) * CH], ones_cf[0:H, 0:1].broadcast_to((H, CH)),
                f1[:, c * CH:(c + 1) * CH], 0.0, ALU.mult, ALU.add),
              reads=["f1", "ones_cf", "aloc"], writes=["aloc"])
        A("act", lambda e: e.activation(ev_fm[:], u_fm[:], AF.Ln), reads=["u", "ev"], writes=["ev"])
        A("dve", lambda e: e.tensor_tensor(ev_fm[:], ev_fm[:], aloc_fm[:], ALU.subtract),
          reads=["ev", "aloc"], writes=["ev"])
        A("dve", lambda e: e.tensor_copy(ah_fm[:], aloc_fm[:]), reads=["aloc", "ah"], writes=["ah"])
        A("dve", lambda e: e.tensor_copy(f1[:, 0:t_loc], ah_fm[:]), reads=["ah", "f1"], writes=["f1"])
        A("dve", lambda e: e.tensor_tensor(f1[:, 0:t_loc], aloc_fm[:], f1[:, 0:t_loc], ALU.subtract),
          reads=["f1", "aloc"], writes=["f1"])
        A("dve", lambda e: e.tensor_copy(al_fm[:], f1[:, 0:t_loc]), reads=["f1", "al"], writes=["al"])
        if own:
            for c in range(nch):
                A("act", lambda e, c=c: e.dma_start(
                    out=ahl_dram[c:c + 1, :].rearrange("o (h t) -> (o h) t", h=H),
                    in_=ah_fm[:, c * CH:(c + 1) * CH]),
                  reads=["ah"], writes=["ahl_d"], dma="Q_t")
                A("act", lambda e, c=c: e.dma_start(
                    out=ahl_dram[nch + c:nch + c + 1, :].rearrange("o (h t) -> (o h) t", h=H),
                    in_=al_fm[:, c * CH:(c + 1) * CH]),
                  reads=["al"], writes=["ahl_d2"], dma="Q_t")

        for g in range(ntg):
            for j in range(nk):
                A("pe", lambda e, g=g, j=j: e.matmul(PS6[0:32, 0:512], wsm[:, j * 160 + 128:(j + 1) * 160],
                                                     x_fm[:, j * t_loc + g * 512:j * t_loc + (g + 1) * 512],
                                                     start=(j == 0), stop=(j == nk - 1)),
                  reads=K_XFM[j] + [("wsm2", j)] + (["PS6"] if j == 0 else []),
                  writes=["PS6"] if j == nk - 1 else [])
            A("dve", lambda e: e.tensor_copy(th_sg[:], PS6[0:32, 0:512]),
              reads=["PS6", "th_sg"], writes=["th_sg", "PS6"])
            A("pe", lambda e, g=g: e.matmul(PS5[0:1, 0:512], ones_cf[0:H, 0:1],
                                            dt_loc[:, g * 512:(g + 1) * 512], start=True, stop=True),
              reads=["dt", "ones_cf", "PS5"], writes=["PS5"])
            A("dve", lambda e: e.tensor_scalar(vrow[:], PS5[0:1, 0:512], 1.0 / H, None, ALU.mult),
              reads=["PS5", "vrow"], writes=["vrow", "PS5"])
            A("pe", lambda e: e.matmul(PS5[0:32, 0:512], ones_rf[0:1, 0:32], vrow[:], start=True, stop=True),
              reads=["vrow", "ones_rf", "PS5"], writes=["PS5"])
            A("dve", lambda e: e.scalar_tensor_tensor(th_sg[:], th_sg[:], -1.0, PS5[0:32, 0:512],
                                                      ALU.mult, ALU.mult),
              reads=["th_sg", "PS5"], writes=["th_sg", "PS5"])
            A("dve", lambda e, g=g: e.tensor_copy(cum_l[:, g * 512:(g + 1) * 512], th_sg[:]),
              reads=["th_sg", "cum"], writes=["cum"])
        if seg == 0:
            A("dve", lambda e: e.memset(carry[:], 0.0), reads=["carry"], writes=["carry"])
        for g in range(ntg):
            A("dve", lambda e, g=g: e.tensor_tensor_scan(
                cum_l[:, g * 512:(g + 1) * 512], ones_cf[0:32, 0:1].broadcast_to((32, 512)),
                cum_l[:, g * 512:(g + 1) * 512], carry[0:32, 0:1], ALU.mult, ALU.add),
              reads=["cum", "carry", "ones_cf"], writes=["cum"])
            A("dve", lambda e, g=g: e.tensor_copy(carry[:], cum_l[:, g * 512 + 511:g * 512 + 512]),
              reads=["cum", "carry"], writes=["carry"])
        for phase, key, off in ((0.0, "sin", t_loc), (0.25, "cos", 0)):
            A("dve", lambda e, ph=phase: e.tensor_scalar(rr_f[:], cum_l[:], 1.0 / TWO_PI, ph,
                                                         ALU.mult, ALU.add),
              reads=["cum", "rr_f"], writes=["rr_f"])
            A("dve", lambda e: e.tensor_copy(rr_i[:], rr_f[:]), reads=["rr_f", "rr_i"], writes=["rr_i"])
            A("dve", lambda e: e.tensor_copy(rr_g[:], rr_i[:]), reads=["rr_i", "rr_g"], writes=["rr_g"])
            A("dve", lambda e: e.scalar_tensor_tensor(rr_f[:], rr_g[:], -TWO_PI, cum_l[:],
                                                      ALU.mult, ALU.add),
              reads=["rr_g", "cum", "rr_f"], writes=["rr_f"])
            if phase == 0.25:
                A("dve", lambda e: e.tensor_scalar(rr_f[:], rr_f[:], PI / 2, None, ALU.add),
                  reads=["rr_f"], writes=["rr_f"])
            A("dve", lambda e: e.tensor_scalar(rr_m[:], rr_f[:], PI, None, ALU.is_gt),
              reads=["rr_f", "rr_i"], writes=["rr_i"])
            A("dve", lambda e: e.scalar_tensor_tensor(rr_f[:], rr_m[:], -TWO_PI, rr_f[:],
                                                      ALU.mult, ALU.add),
              reads=["rr_f", "rr_i"], writes=["rr_f"])
            A("dve", lambda e: e.tensor_scalar(rr_m[:], rr_f[:], -PI, None, ALU.is_lt),
              reads=["rr_f", "rr_i"], writes=["rr_i"])
            A("dve", lambda e: e.scalar_tensor_tensor(rr_f[:], rr_m[:], TWO_PI, rr_f[:],
                                                      ALU.mult, ALU.add),
              reads=["rr_f", "rr_i"], writes=["rr_f"])
            A("act", lambda e, off=off: e.activation(cs_bf[:, off:off + t_loc], rr_f[:], AF.Sin),
              reads=["rr_f"], writes=[key])
        for c in range(nch):
            A("pe", lambda e, c=c: e.transpose(PBb[:, 0:32], cs_bf[:, c * CH:(c + 1) * CH],
                                               identb[0:32, 0:32]),
              reads=["cos", "identb", ("PB", 0)], writes=[("PB", 0)])
            A("dve", lambda e, c=c: e.tensor_copy(cos_tm[:, c * 32:(c + 1) * 32], PBb[:, 0:32]),
              reads=[("PB", 0), "cos_tm"], writes=[("PB", 0), "cos_tm"])
            A("pe", lambda e, c=c: e.transpose(PBb[:, 1024:1056],
                                               cs_bf[:, t_loc + c * CH:t_loc + (c + 1) * CH],
                                               identb[0:32, 0:32]),
              reads=["sin", "identb", ("PB", 1)], writes=[("PB", 1)])
            A("dve", lambda e, c=c: e.tensor_copy(sin_tm[:, c * 32:(c + 1) * 32], PBb[:, 1024:1056]),
              reads=[("PB", 1), "sin_tm"], writes=[("PB", 1), "sin_tm"])

        A("dve", lambda e: e.engine_nop(),
          reads=["cum", "rr_f", "rr_i", "rr_g", "cos", "sin", "th_sg", "carry"],
          writes=P2B_KEYS + ["ev_tm", "qd_tm", "ws_tm"])

        for c in range(nch):
            for nm, off, nbm, nbk, dof in ((("B", 0, bnb_m, "bnb_m", 0),) +
                                           ((("C", NS, cnb_m, "cnb_m", NS),) if own else ())):
                dst = bcf[:, dof:dof + NS]
                sl = bc_t[:, c * 128 + off:c * 128 + off + NS]
                A("dve", lambda e, sl=sl: e.tensor_tensor(rms_sc[:], sl, sl, ALU.mult),
                  reads=[("bc", c), "rms_sc"], writes=["rms_sc"])
                A("dve", lambda e: e.tensor_reduce(rw1[:], rms_sc[:], AX.X, ALU.add),
                  reads=["rms_sc", "rw1"], writes=["rw1"])
                A("dve", lambda e: e.tensor_scalar(rw1[:], rw1[:], 1.0 / NS, 1e-6, ALU.mult, ALU.add),
                  reads=["rw1"], writes=["rw1"])
                A("dve", lambda e: e.reciprocal(rw2[:], rw1[:]), reads=["rw1", "rw2"], writes=["rw2"])
                A("act", lambda e: e.activation(rw2[:], rw2[:], AF.Sqrt), reads=["rw2"], writes=["rw2"])
                A("dve", lambda e, sl=sl: e.tensor_scalar(rms_sc[:], sl, rw2[:], None, ALU.mult),
                  reads=[("bc", c), "rw2", "rms_sc"], writes=["rms_sc"])
                A("dve", lambda e, nbm=nbm: e.tensor_tensor(rms_sc[:], rms_sc[:], nbm[:], ALU.mult),
                  reads=["rms_sc", nbk], writes=["rms_sc"])
                cslice = cos_tm[:, c * 32:(c + 1) * 32]
                sslice = sin_tm[:, c * 32:(c + 1) * 32]
                A("dve", lambda e, cs=cslice: e.tensor_tensor(sc_a[:, 0:32], rms_sc[:, 0:32], cs, ALU.mult),
                  reads=["rms_sc", "cos_tm", "scz"], writes=["scz"])
                A("dve", lambda e, ss=sslice: e.tensor_tensor(sc_a[:, 32:64], rms_sc[:, 32:64], ss, ALU.mult),
                  reads=["rms_sc", "sin_tm", "scz"], writes=["scz"])
                A("dve", lambda e, dst=dst: e.tensor_tensor(
                    dst[:, 0:32], sc_a[:, 0:32], sc_a[:, 32:64], ALU.subtract),
                  reads=["scz", "bcf"], writes=["bcf"])
                A("dve", lambda e, ss=sslice: e.tensor_tensor(sc_b[:, 0:32], rms_sc[:, 0:32], ss, ALU.mult),
                  reads=["rms_sc", "sin_tm", "scz"], writes=["scz"])
                A("dve", lambda e, cs=cslice: e.tensor_tensor(sc_b[:, 32:64], rms_sc[:, 32:64], cs, ALU.mult),
                  reads=["rms_sc", "cos_tm", "scz"], writes=["scz"])
                A("dve", lambda e, dst=dst: e.tensor_tensor(
                    dst[:, 32:NS], sc_b[:, 0:32], sc_b[:, 32:64], ALU.add),
                  reads=["scz", "bcf"], writes=["bcf"])
            A("dve", lambda e, c=c: e.tensor_copy(b_tmbA[:, c * NS:(c + 1) * NS], bcf[:, 0:NS]),
              reads=["bcf", "b_tmbA"], writes=["b_tmbA", ("Bb", c)])
            if not own:
                continue
            A("dve", lambda e, c=c: e.tensor_tensor(rms_sc[:], bcf[:, 0:NS], bcf[:, NS:2 * NS], ALU.mult),
              reads=["bcf", "rms_sc"], writes=["rms_sc"])
            A("dve", lambda e, c=c: e.tensor_reduce(sdd_tm[:, c:c + 1], rms_sc[:], AX.X, ALU.add),
              reads=["rms_sc", "sdd"], writes=["sdd"])
            A("pe", lambda e, c=c: e.transpose(PBb[0:64, 2048:2176], b_tmbA[:, c * NS:(c + 1) * NS],
                                               identb[:]),
              reads=[("Bb", c), "identb", ("PB", 2)], writes=[("PB", 2)])
            A("dve", lambda e, c=c: e.tensor_copy(b_fmb[:, c * CH:(c + 1) * CH], PBb[0:64, 2048:2176]),
              reads=[("PB", 2), "b_fmb"], writes=[("PB", 2), "b_fmb", ("bf", c)])
            A("dve", lambda e, c=c: e.tensor_copy(sc128b[:, 0:NS], bcf[:, NS:2 * NS]),
              reads=["bcf", "s_bf"], writes=["s_bf"])
            A("pe", lambda e, c=c: e.transpose(PBb[0:64, 3072:3200], sc128b[:, 0:NS], identb[:]),
              reads=["s_bf", "identb", ("PB", 3)], writes=[("PB", 3)])
            A("dve", lambda e, c=c: e.tensor_copy(c_fmb[:, c * CH:(c + 1) * CH], PBb[0:64, 3072:3200]),
              reads=[("PB", 3), "c_fmb"], writes=[("PB", 3), "c_fmb", ("cf", c)])

        for c in range(nch):
            A("act", lambda e, c=c: e.activation(sc128[:], ev_fm[:, c * CH:(c + 1) * CH], AF.Exp,
                                                 bias=aloc_fm[:, c * CH + CH - 1:c * CH + CH]),
              reads=["ev", "aloc", "f2"], writes=["f2"])
            A("dve", lambda e: e.tensor_copy(sc128b[0:H, 0:CH], sc128[:]),
              reads=["f2", "s_bf"], writes=["s_bf"])
            A("pe", lambda e, c=c: e.transpose(PBb[:, 3072:3136], sc128b[0:H, 0:CH], identb[0:H, 0:H]),
              reads=["s_bf", "identb", ("PB", 3)], writes=[("PB", 3)])
            A("dve", lambda e, c=c: e.tensor_copy(ws_tm[:, c * H:(c + 1) * H], PBb[:, 3072:3136]),
              reads=[("PB", 3), "ws_tm", "dtl"], writes=[("PB", 3), "ws_tm", ("wst", c)])
            if not own:
                continue
            A("pe", lambda e, c=c: e.transpose(PB[:, 512:576], ev_fm[:, c * CH:(c + 1) * CH],
                                               identf[0:64, 0:64]),
              reads=["ev", "identf", ("PB", 1)], writes=[("PB", 1)])
            A("dve", lambda e, c=c: e.tensor_copy(ev_tm[:, c * H:(c + 1) * H], PB[:, 512:576]),
              reads=[("PB", 1), "ev_tm"], writes=[("PB", 1), "ev_tm", ("evt", c)])
            A("act", lambda e, c=c: e.activation(sc128[:], aloc_fm[:, c * CH:(c + 1) * CH], AF.Exp),
              reads=["aloc", "f2"], writes=["f2"])
            A("dve", lambda e: e.tensor_copy(sc128b[0:H, 0:CH], sc128[:]),
              reads=["f2", "s_bf"], writes=["s_bf"])
            A("pe", lambda e, c=c: e.transpose(PBb[:, 2048:2112], sc128b[0:H, 0:CH], identb[0:H, 0:H]),
              reads=["s_bf", "identb", ("PB", 2)], writes=[("PB", 2)])
            A("dve", lambda e, c=c: e.tensor_copy(qd_tm[:, c * H:(c + 1) * H], PBb[:, 2048:2112]),
              reads=[("PB", 2), "qd_tm", "dtl"], writes=[("PB", 2), "qd_tm", ("qdt", c)])
            A("pe", lambda e, c=c: e.transpose(PB[:, 576:640], wg_fm[:, c * CH:(c + 1) * CH],
                                               identf[0:64, 0:64]),
              reads=["wg", "identf", ("PB", 1)], writes=[("PB", 1)])
            A("dve", lambda e, c=c: e.tensor_scalar(c0_tm[:, c * H:(c + 1) * H], PB[:, 576:640],
                                                    sdd_tm[:, c:c + 1], None, ALU.mult),
              reads=[("PB", 1), "sdd", "c0_tm", "dtl"], writes=[("PB", 1), "c0_tm", ("c0t", c)])

        if seg == 0:
            A("dve", lambda e: e.memset(st_run[:, :], 0.0), reads=K_STR, writes=K_STR)
        for c in range(nch):
            p = c % 2
            A("sp", lambda e, c=c, p=p: e.dma_start(out=xs_db[:, p * DI:(p + 1) * DI],
                                                    in_=xs_dram[c * 128:(c + 1) * 128, :]),
              reads=[("xs_d", c), ("xs_db", p)], writes=[("xs_db", p)], dma="Q_s")
            A("dve", lambda e, c=c, p=p: e.tensor_tensor(
                wx_bf[:, :].rearrange("p (h q) -> p h q", h=H),
                xs_db[:, p * DI:(p + 1) * DI].rearrange("p (h q) -> p h q", h=H),
                ws_tm[:, c * H:(c + 1) * H].unsqueeze(2).broadcast_to((128, H, NS)), ALU.mult),
              reads=[("xs_db", p), ("wst", c)] + K_WX, writes=K_WX)
            if own:
                A("dve", lambda e: e.tensor_copy(prev_bf[:, :], st_run[:, :]),
                  reads=K_STR + K_PRV, writes=K_PRV)
                A("act", lambda e, c=c: e.dma_start(out=prev_dram[c * NS:(c + 1) * NS, :],
                                                    in_=prev_bf[:, :]),
                  reads=K_PRV, writes=["prch", ("prevd", c)], dma="Q_t")
            A("act", lambda e, c=c: e.activation(cd_col[:], aloc_fm[:, c * CH + CH - 1:c * CH + CH], AF.Exp),
              reads=["aloc", "cd_col"], writes=["cd_col"])
            A("pe", lambda e: e.transpose(PS4[0:1, 0:64], cd_col[:], identf[:]),
              reads=["cd_col", "identf", "PS4"], writes=["PS4"])
            A("dve", lambda e: e.tensor_copy(cd_row[:], PS4[0:1, 0:64]),
              reads=["PS4", "cd_row"], writes=["cd_row", "PS4"])
            A("pe", lambda e: e.matmul(PS7[0:NS, 0:NS], ones_rf[0:1, 0:NS], cd_row[:], start=True, stop=True),
              reads=["cd_row", "ones_rf", "PS7"], writes=["PS7"])
            A("dve", lambda e: e.tensor_tensor(
                st_run[:, :].rearrange("n (h q) -> n h q", h=H),
                st_run[:, :].rearrange("n (h q) -> n h q", h=H),
                PS7[0:NS, 0:NS].unsqueeze(2).broadcast_to((NS, H, NS)), ALU.mult),
              reads=K_STR + ["PS7"] + (K_PRV if own else []), writes=K_STR + ["PS7"])
            for half in range(2):
                for i in range(4):
                    A("pe", lambda e, c=c, half=half, i=i: e.matmul(
                        PB[0:NS, i * 512:(i + 1) * 512], b_tmbA[:, c * NS:(c + 1) * NS],
                        wx_bf[:, half * 2048 + i * 512:half * 2048 + (i + 1) * 512],
                        start=True, stop=True),
                      reads=K_WX + [("Bb", c), ("PB", i)], writes=[("PB", i)])
                A("dve", lambda e, c=c, half=half: e.tensor_tensor(
                    st_run[:, half * 2048:(half + 1) * 2048], st_run[:, half * 2048:(half + 1) * 2048],
                    PB[0:NS, 0:2048], ALU.add),
                  reads=K_STR + PBK, writes=K_STR + PBK)


    def pw_stats(tiles, col0):
        for i, ap in enumerate(tiles):
            h = i % 2
            A("sp", lambda e, ap=ap, h=h: e.dma_start(out=pwf[:, h * 512:(h + 1) * 512], in_=ap),
              reads=[("pwf", h)], writes=[("pwf", h)], dma="Q_s")
            col = col0 + i % 8
            if i < 8:
                A("dve", lambda e, h=h, col=col: e.tensor_reduce(
                    pwacc[:, col:col + 1], pwf[:, h * 512:(h + 1) * 512],
                    AX.X, ALU.add, apply_absolute_value=True),
                  reads=[("pwf", h), "pwacc"], writes=["pwacc"])
            else:
                A("dve", lambda e, h=h: e.tensor_reduce(
                    pwacc[:, 24:25], pwf[:, h * 512:(h + 1) * 512], AX.X, ALU.add,
                    apply_absolute_value=True),
                  reads=[("pwf", h), "pwacc"], writes=["pwacc"])
                A("dve", lambda e, col=col: e.tensor_tensor(
                    pwacc[:, col:col + 1], pwacc[:, col:col + 1], pwacc[:, 24:25], ALU.add),
                  reads=["pwacc"], writes=["pwacc"])

    A("dve", lambda e: e.memset(pwacc[:], 0.0), reads=["pwacc"], writes=["pwacc"])

    def gu_tiles(hbm):
        return [hbm[t * 128:(t + 1) * 128, h * 512:(h + 1) * 512]
                for t in range(DFF // 128) for h in range(4)]

    wd_tiles = [wd_hbm[t * 128:(t + 1) * 128, q * 512:(q + 1) * 512]
                for t in range(D // 128) for q in range(DFF // 512)]
    stats_jobs = [lambda: pw_stats(gu_tiles(wg_hbm), 0),
                  lambda: pw_stats(gu_tiles(wu_hbm), 8),
                  lambda: pw_stats(wd_tiles, 16)]
    npre_seg = max(NSEG - 1, 1)
    done = 0
    for seg in range(NSEG):
        front(seg)
        if seg < NSEG - 1:
            upto = (seg + 1) * 3 // npre_seg
            for job in stats_jobs[done:upto]:
                job()
            done = upto
    for job in stats_jobs[done:]:
        job()

    for col, lo, n in ((0, 0, 8), (1, 8, 8), (2, 16, 8)):
        A("dve", lambda e, col=col, lo=lo, n=n: e.tensor_reduce(qsc[:, col:col + 1],
                                                                pwacc[:, lo:lo + n], AX.X, ALU.add),
          reads=["pwacc", "rms_sc"], writes=["rms_sc"])
    A("pe", lambda e: e.matmul(PS7[0:3, 0:1], qsc[:, 0:3], ones_cf[:, 0:1], start=True, stop=True),
      reads=["rms_sc", "ones_cf", "PS7"], writes=["PS7"])
    A("dve", lambda e: e.tensor_copy(coef_s[0:3, 0:1], PS7[0:3, 0:1]),
      reads=["PS7", "coef_s"], writes=["coef_s", "PS7"])
    A("act", lambda e: e.dma_start(out=row_dram[3:4, 0:3].rearrange("o (a u) -> (o a) u", a=3),
                                   in_=coef_s[0:3, 0:1]),
      reads=["coef_s"], writes=["rowch", "row_d4"], dma="Q_t")
    A("sp", lambda e: e.dma_start(out=qrow[0:1, 0:3], in_=row_dram[3:4, 0:3]),
      reads=["row_d4", "rms_sc"], writes=["rms_sc"], dma="Q_s")
    A("dve", lambda e: e.tensor_scalar(qrow[0:1, 0:4], qrow[0:1, 0:4], 1.0 / (DFF * D), 1e-5,
                                       ALU.mult, ALU.add),
      reads=["rms_sc"], writes=["rms_sc"])
    A("dve", lambda e: e.reciprocal(qrow[0:1, 4:8], qrow[0:1, 0:4]), reads=["rms_sc"], writes=["rms_sc"])
    A("pe", lambda e: e.matmul(PS7[0:128, 0:4], ones_rf[0:1, :], qrow[0:1, 4:8], start=True, stop=True),
      reads=["rms_sc", "ones_rf", "PS7"], writes=["PS7"])
    A("dve", lambda e: e.tensor_copy(qsc[:, 0:4], PS7[0:128, 0:4]),
      reads=["PS7", "rms_sc"], writes=["rms_sc", "PS7"])
    A("pe", lambda e: e.matmul(PS7[0:128, 4:7], ones_rf[0:1, :], qrow[0:1, 0:3], start=True, stop=True),
      reads=["rms_sc", "ones_rf", "PS7"], writes=["PS7"])
    A("dve", lambda e: e.tensor_copy(wsraw[:, 0:3], PS7[0:128, 4:7]),
      reads=["PS7", "rms_sc"], writes=["rms_sc", "PS7"])
    def q_gu(m, r):
        hbm = (wg_hbm, wu_hbm)[m]
        nat = (wqg_all, wqu_all)[m]
        kk = ("qg", "qu")[m]
        for t in range(DF8 // 128):
            row0 = (r * (DF8 // 128) + t) * 128
            if t % 2 == 0:
                zb = z_st[:, 0:D]
                zkeys = ["z_st", ("zs", 0), ("zs", 1), ("zs", 2), ("zs", 3)]
            else:
                zb = pwf[:, :].bitcast(BF16)
                zkeys = [("pwf", 0), ("pwf", 1)]
            for h in range(2):
                q = (t * 2 + h) % 4
                xq = x_st[:, q * 1024:(q + 1) * 1024]
                A("sp", lambda e, xq=xq, row0=row0, h=h, hbm=hbm: e.dma_start(
                    out=xq, in_=hbm[row0:row0 + 128, h * 1024:(h + 1) * 1024]),
                  reads=[("x_stq", q), ("x_st", q // 2)],
                  writes=[("x_stq", q), ("x_st", q // 2)], dma="Q_s")
                A("dve", lambda e, xq=xq, m=m: e.tensor_scalar(xq, xq, qsc[:, m:m + 1], CBIG,
                                                               ALU.mult, ALU.add),
                  reads=[("x_stq", q), "rms_sc"], writes=[("x_stq", q)])
                A("dve", lambda e, xq=xq: e.tensor_scalar(xq, xq, CBIG, None, ALU.subtract),
                  reads=[("x_stq", q)], writes=[("x_stq", q)])
                A("dve", lambda e, xq=xq, zb=zb, h=h: e.tensor_scalar(
                    zb[:, h * 1024:(h + 1) * 1024], xq, 1.0, -1.0, ALU.min, ALU.max),
                  reads=[("x_stq", q)] + zkeys, writes=zkeys)
            A("act", lambda e, nat=nat, row0=row0, zb=zb: e.dma_start(
                out=nat[row0:row0 + 128, :], in_=zb[:, 0:D]),
              reads=zkeys, writes=[kk + "ch", (kk, r, t)], dma="Q_t")

    def q_d(r):
        for t in range(D // 128):
            q = t % 4
            if t % 2 == 0:
                zb = z_st[:, 0:DD8]
                zkeys = ["z_st", ("zs", 0), ("zs", 1)]
            else:
                zb = pwf[:, :].bitcast(BF16)[:, 0:DD8]
                zkeys = [("pwf", 0), ("pwf", 1)]
            xq = x_st[:, q * 1024:(q + 1) * 1024]
            A("sp", lambda e, xq=xq, r=r, t=t: e.dma_start(
                out=xq, in_=wd_hbm[t * 128:(t + 1) * 128, r * DD8:(r + 1) * DD8]),
              reads=[("x_stq", q), ("x_st", q // 2)],
              writes=[("x_stq", q), ("x_st", q // 2)], dma="Q_s")
            A("dve", lambda e, xq=xq: e.tensor_scalar(xq, xq, qsc[:, 2:3], CBIG,
                                                      ALU.mult, ALU.add),
              reads=[("x_stq", q), "rms_sc"], writes=[("x_stq", q)])
            A("dve", lambda e, xq=xq: e.tensor_scalar(xq, xq, CBIG, None, ALU.subtract),
              reads=[("x_stq", q)], writes=[("x_stq", q)])
            A("dve", lambda e, xq=xq, zb=zb: e.tensor_scalar(zb, xq, 1.0, -1.0, ALU.min, ALU.max),
              reads=[("x_stq", q)] + zkeys, writes=zkeys)
            A("act", lambda e, r=r, t=t, zb=zb: e.dma_start(
                out=wqd_all[t * 128:(t + 1) * 128, r * DD8:(r + 1) * DD8], in_=zb),
              reads=zkeys, writes=["qdch", ("qd", r, t)], dma="Q_t")

    pw_units = ([lambda m=m, r=r: q_gu(m, r) for m in range(2) for r in range(DFF // DF8)]
                + [lambda r=r: q_d(r) for r in range(DFF // DD8)])
    PWL = (len(pw_units) + nch - 1) // nch

    ebg = AR[:, 25600:26624]
    gbg = AR[:, 26624:27648]
    A("dve", lambda e: e.engine_nop(), reads=[("AR", 25), ("AR", 26)],
      writes=[("AR", 25), ("AR", 26), ("bce", 0), ("bce", 1), ("bcg", 0), ("bcg", 1)])
    for c in range(nch):
        p = c % 2
        A("sp", lambda e, c=c, p=p: e.dma_start(out=xs_db[:, p * DI:(p + 1) * DI],
                                                in_=xs_dram[c * 128:(c + 1) * 128, :]),
          reads=[("xs_d", c), ("xs_db", p)], writes=[("xs_db", p)], dma="Q_s")
        A("act", lambda e, c=c: e.dma_start(out=aflat_h[:], in_=ahl_dram[c:c + 1, :]),
          reads=["ahl_d"] + K_AFH, writes=K_AFH, dma="Q_t")
        A("act", lambda e, c=c: e.dma_start(out=aflat_l[:], in_=ahl_dram[nch + c:nch + c + 1, :]),
          reads=["ahl_d2"] + K_AFL, writes=K_AFL, dma="Q_t")
        A("pe", lambda e, c=c: e.matmul(PS4[:, 0:128], b_fmb[0:NS, c * CH:(c + 1) * CH],
                                        c_fmb[0:NS, c * CH:(c + 1) * CH], start=True, stop=True),
          reads=[("bf", c), ("cf", c), "PS4"], writes=["PS4"])
        A("dve", lambda e: e.tensor_copy(s_bf[:], PS4[:, 0:128]),
          reads=["PS4", "s_bf"], writes=["PS4", "s_bf"])
        for q in range(4):
            for i in range(4):
                A("pe", lambda e, q=q, i=i: e.matmul(
                    PB[:, i * 512:(i + 1) * 512], ones_rb[:],
                    aflat_h[0:1, q * 2048 + i * 512:q * 2048 + (i + 1) * 512],
                    start=True, stop=False),
                  reads=K_AFH + ["ones_rb"], writes=[("PB", i)])
                A("pe", lambda e, q=q, i=i: e.matmul(
                    PB[:, i * 512:(i + 1) * 512], ones_rb[:],
                    aflat_l[0:1, q * 2048 + i * 512:q * 2048 + (i + 1) * 512],
                    start=False, stop=False),
                  reads=K_AFL + ["ones_rb"], writes=[])
                A("pe", lambda e, q=q, i=i: e.matmul(
                    PB[:, i * 512:(i + 1) * 512].rearrange("p (r n) -> p r n", r=4),
                    triu[:], maskr[:, :].unsqueeze(1).broadcast_to((128, 4, 128)),
                    start=False, stop=True),
                  reads=["triu", "maskr"], writes=[("PB", i)])
            for i in range(4):
                eb = ebg[:, (i % 2) * 512:(i % 2) * 512 + 512]
                gb = gbg[:, (i % 2) * 512:(i % 2) * 512 + 512]
                h0 = q * 16 + i * 4
                A("dve", lambda e, i=i, h0=h0, c=c: e.tensor_tensor(
                    PB[:, i * 512:(i + 1) * 512].rearrange("p (r n) -> p r n", r=4),
                    PB[:, i * 512:(i + 1) * 512].rearrange("p (r n) -> p r n", r=4),
                    ev_tm[:, c * H + h0:c * H + h0 + 4].unsqueeze(2)
                    .broadcast_to((128, 4, CH)), ALU.add),
                  reads=[("PB", i), ("evt", c)], writes=[("PB", i)])
                A("act", lambda e, i=i, eb=eb: e.activation(eb, PB[:, i * 512:(i + 1) * 512],
                                                            AF.Exp),
                  reads=[("PB", i), ("bce", i % 2), ("AR", 25)], writes=[("bce", i % 2)])
                A("dve", lambda e, eb=eb, gb=gb: e.tensor_tensor(
                    gb.rearrange("p (r n) -> p r n", r=4),
                    eb.rearrange("p (r n) -> p r n", r=4),
                    s_bf[:, :].unsqueeze(1).broadcast_to((128, 4, CH)), ALU.mult),
                  reads=["s_bf", ("bce", i % 2), ("bcg", i % 2), ("AR", 25), ("AR", 26)],
                  writes=[("bcg", i % 2)])
                for hl4 in range(4):
                    h = q * 16 + i * 4 + hl4
                    pg, pgk = (PS5, "PS5") if (h // 8) % 2 == 0 else (PS6, "PS6")
                    A("pe", lambda e, h=h, gb=gb, hl4=hl4, pg=pg, p=p: e.matmul(
                        pg[:, (h % 8) * 64:(h % 8 + 1) * 64], gb[:, hl4 * 128:(hl4 + 1) * 128],
                        xs_db[:, p * DI + h * 64:p * DI + (h + 1) * 64], start=True, stop=True),
                      reads=[("bcg", i % 2), ("AR", 26), ("xs_db", p), pgk],
                      writes=[pgk] if h % 8 == 7 else [])
                    if h % 8 != 7:
                        continue
                    gg = h // 8
                    A("dve", lambda e, c=c, gg=gg, p=p: e.tensor_tensor(
                        u2_bf[:, :].rearrange("p (r n) -> p r n", r=8),
                        xs_db[:, p * DI + gg * 512:p * DI + (gg + 1) * 512]
                        .rearrange("p (r n) -> p r n", r=8),
                        c0_tm[:, c * H + gg * 8:c * H + (gg + 1) * 8].unsqueeze(2)
                        .broadcast_to((128, 8, 64)), ALU.mult),
                      reads=[("xs_db", p), ("c0t", c), "scz"], writes=["scz"])
                    A("dve", lambda e, gg=gg, pg=pg, pgk=pgk: e.tensor_tensor(
                        y_st[:, gg * 512:(gg + 1) * 512], pg[:, :], u2_bf[:], ALU.add),
                      reads=["scz", pgk, "y_st"], writes=["y_st", pgk])
        A("act", lambda e, c=c: e.dma_start(out=y_dram[c * 128:(c + 1) * 128, :], in_=y_st[:, :]),
          reads=["y_st"], writes=["ych", ("y_d", c)], dma="Q_t")
        for u in pw_units[c * PWL:(c + 1) * PWL]:
            u()

    for c in range(nch):
        p = c % 2
        A("sp", lambda e, c=c, p=p: e.dma_start(out=xs_db[:, p * DI:(p + 1) * DI],
                                                in_=z_dram[c * 128:(c + 1) * 128, :]),
          reads=[("z_d", c), ("xs_db", p)], writes=[("xs_db", p)], dma="Q_s")
        A("sp", lambda e, c=c: e.dma_start(out=y_st[:, :], in_=y_dram[c * 128:(c + 1) * 128, :]),
          reads=[("y_d", c), "y_st"], writes=["y_st"], dma="Q_s")
        A("sp", lambda e, c=c: e.dma_start(out=prev_bf[:, :], in_=prev_dram[c * NS:(c + 1) * NS, :]),
          reads=[("prevd", c)] + K_PRV, writes=K_PRV, dma="Q_s")
        for gg in range(8):
            pg, pgk = (PS5, "PS5") if gg % 2 == 0 else (PS6, "PS6")
            A("pe", lambda e, c=c, gg=gg, pg=pg: e.matmul(
                pg[:, :], c_fmb[0:NS, c * CH:(c + 1) * CH],
                prev_bf[:, gg * 512:(gg + 1) * 512], start=True, stop=True),
              reads=K_PRV + [("cf", c), pgk], writes=[pgk])
            A("dve", lambda e, c=c, gg=gg, pg=pg, pgk=pgk: e.tensor_tensor(
                u2_bf[:, :].rearrange("p (r n) -> p r n", r=8),
                pg[:, :].rearrange("p (r n) -> p r n", r=8),
                qd_tm[:, c * H + gg * 8:c * H + (gg + 1) * 8].unsqueeze(2).broadcast_to((128, 8, 64)),
                ALU.mult),
              reads=[pgk, ("qdt", c), "scz"], writes=["scz", pgk])
            A("dve", lambda e, gg=gg: e.tensor_tensor(y_st[:, gg * 512:(gg + 1) * 512],
                                                      y_st[:, gg * 512:(gg + 1) * 512],
                                                      u2_bf[:], ALU.add),
              reads=["scz", "y_st"], writes=["y_st"])
        for gg in range(8):
            A("dve", lambda e, gg=gg: e.tensor_tensor(
                x_st[:, 0:512], y_st[:, gg * 512:(gg + 1) * 512], y_st[:, gg * 512:(gg + 1) * 512],
                ALU.mult),
              reads=["y_st", ("x_st", 0)], writes=[("x_st", 0)])
            A("dve", lambda e, gg=gg: e.tensor_reduce(
                (rw1[:, 0:1] if gg == 0 else rw2[:, 0:1]), x_st[:, 0:512], AX.X, ALU.add),
              reads=[("x_st", 0), "rw1", "rw2"], writes=[("rw1" if gg == 0 else "rw2")])
            if gg > 0:
                A("dve", lambda e: e.tensor_tensor(rw1[:], rw1[:], rw2[:], ALU.add),
                  reads=["rw1", "rw2"], writes=["rw1"])
        A("dve", lambda e: e.tensor_scalar(rw1[:], rw1[:], 1.0 / DI, 1e-6, ALU.mult, ALU.add),
          reads=["rw1"], writes=["rw1"])
        A("dve", lambda e: e.reciprocal(rw2[:], rw1[:]), reads=["rw1", "rw2"], writes=["rw2"])
        A("act", lambda e: e.activation(rw2[:], rw2[:], AF.Sqrt), reads=["rw2"], writes=["rw2"])
        A("act", lambda e, p=p: e.activation(xs_st[:, :], xs_db[:, p * DI:(p + 1) * DI], AF.Silu),
          reads=[("xs_db", p), "xs_st"], writes=["xs_st"])
        A("dve", lambda e: e.tensor_scalar(y_st[:, :], y_st[:, :], rw2[:], None, ALU.mult),
          reads=["y_st", "rw2"], writes=["y_st"])
        A("dve", lambda e: e.tensor_tensor(y_st[:, :], y_st[:, :], xs_st[:, :], ALU.mult),
          reads=["y_st", "xs_st"], writes=["y_st"])
        A("act", lambda e, c=c: e.dma_start(out=y_dram[c * 128:(c + 1) * 128, :], in_=y_st[:, :]),
          reads=["y_st"], writes=["ych", ("y_d", c)], dma="Q_t")

    def psg(t):
        return PB[:, (t % 4) * 512:(t % 4 + 1) * 512] if t < 4 else (PS4, PS5, PS6, PS7)[t - 4][:, :]

    def psk(t):
        return ("PB", t) if t < 4 else ("PS4", "PS5", "PS6", "PS7")[t - 4]

    for j in range(32):
        A("act", lambda e, j=j: e.dma_start_transpose(y_fm[:, j * t_loc:(j + 1) * t_loc],
                                                      y_dram[:, j * 128:(j + 1) * 128]),
          reads=[("y_d", c) for c in range(nch)] + K_YFM[j], writes=K_YFM[j], dma="Q_t")
    for dgr in range(4):
        for j in range(32):
            A("pool", lambda e, dgr=dgr, j=j: e.dma_start(
                out=wo_buf[:, (j % 4) * 512:(j % 4) * 512 + 512],
                in_=w_out[j * 128:(j + 1) * 128, dgr * 512:(dgr + 1) * 512]),
              reads=[("wos", j % 4)] + K_WO[j % 4],
              writes=[("wos", j % 4)] + K_WO[j % 4], dma="Q_x")
            for t in range(ntt):
                A("pe", lambda e, j=j, t=t: e.matmul(
                    psg(t), y_fm[:, j * t_loc + t * 128:j * t_loc + (t + 1) * 128],
                    wo_buf[:, (j % 4) * 512:(j % 4) * 512 + 512],
                    start=(j == 0), stop=(j == 31)),
                  reads=K_YFM[j] + [("wos", j % 4)] + K_WO[j % 4] + ([psk(t)] if j == 0 else []),
                  writes=[psk(t)] if j == 31 else [])
        for t in range(ntt):
            p = t % 2
            A("sp", lambda e, dgr=dgr, t=t, p=p: e.dma_start(
                out=x_st[:, p * D + dgr * 512:p * D + (dgr + 1) * 512],
                in_=x_sl[t * 128:(t + 1) * 128, dgr * 512:(dgr + 1) * 512]),
              reads=[("x_st", p)], writes=[("x_st", p)], dma="Q_s")
            A("dve", lambda e, dgr=dgr, t=t, p=p: e.tensor_tensor(
                x_st[:, p * D + dgr * 512:p * D + (dgr + 1) * 512],
                x_st[:, p * D + dgr * 512:p * D + (dgr + 1) * 512], psg(t), ALU.add),
              reads=[("x_st", p), psk(t)], writes=[("x_st", p), psk(t)])
            A("act", lambda e, dgr=dgr, t=t, p=p: e.dma_start(
                out=gu_dram[t * 128:(t + 1) * 128, dgr * 512:(dgr + 1) * 512],
                in_=x_st[:, p * D + dgr * 512:p * D + (dgr + 1) * 512]),
              reads=[("x_st", p)], writes=[("r1ch", t % 2), ("r1_d", t, dgr)], dma="Q_t")
    for c in range(ntt):
        p = c % 2
        A("sp", lambda e, c=c, p=p: e.dma_start(out=x_st[:, p * D:(p + 1) * D],
                                                in_=gu_dram[c * 128:(c + 1) * 128, 0:D]),
          reads=[("r1_d", c, dgr) for dgr in range(4)] + [("x_st", p)],
          writes=[("x_st", p)], dma="Q_s")
        for bg in range(4):
            A("dve", lambda e, p=p, bg=bg: e.bn_stats(
                sc_a[:, bg * 6:(bg + 1) * 6], x_st[:, p * D + bg * 512:p * D + (bg + 1) * 512]),
              reads=[("x_st", p), "scz"], writes=["scz"])
        A("dve", lambda e: e.bn_aggr(sc_b[:, 0:2], sc_a[:, 0:24]), reads=["scz"], writes=["scz"])
        A("dve", lambda e: e.tensor_scalar(rw1[:], sc_b[:, 1:2], 1.0, 1e-5, ALU.mult, ALU.add),
          reads=["scz", "rw1"], writes=["rw1"])
        A("dve", lambda e: e.reciprocal(rw2[:], rw1[:]), reads=["rw1", "rw2"], writes=["rw2"])
        A("act", lambda e: e.activation(rw2[:], rw2[:], AF.Sqrt), reads=["rw2"], writes=["rw2"])
        A("dve", lambda e, p=p: e.tensor_scalar(x_st[:, p * D:(p + 1) * D], x_st[:, p * D:(p + 1) * D],
                                                sc_b[:, 0:1], rw2[:], ALU.subtract, ALU.mult),
          reads=[("x_st", p), "scz", "rw2"], writes=[("x_st", p)])
        A("act", lambda e, c=c, p=p: e.dma_start(out=ln1_dram[c * 128:(c + 1) * 128, :],
                                                 in_=x_st[:, p * D:(p + 1) * D]),
          reads=[("x_st", p)], writes=["lnch", ("ln1_d", c)], dma="Q_t")
        A("dve", lambda e, p=p: e.tensor_reduce(rw1[:], x_st[:, p * D:(p + 1) * D], AX.X, ALU.max,
                                                apply_absolute_value=True),
          reads=[("x_st", p), "rw1"], writes=["rw1"])
        A("dve", lambda e: e.tensor_scalar(rw1[:], rw1[:], 1.0 / 127.0, 1e-5, ALU.mult, ALU.add),
          reads=["rw1"], writes=["rw1"])
        A("dve", lambda e, c=c: e.tensor_copy(xscal[:, c:c + 1], rw1[:]),
          reads=["rw1", "xscal"], writes=["xscal"])
        A("dve", lambda e: e.reciprocal(rw2[:], rw1[:]), reads=["rw1", "rw2"], writes=["rw2"])
        A("dve", lambda e, p=p: e.tensor_scalar(x_st[:, p * D:(p + 1) * D], x_st[:, p * D:(p + 1) * D],
                                                rw2[:], CBIG, ALU.mult, ALU.add),
          reads=[("x_st", p), "rw2"], writes=[("x_st", p)])
        A("dve", lambda e, p=p: e.tensor_scalar(z_st[:, 0:D], x_st[:, p * D:(p + 1) * D],
                                                CBIG, None, ALU.subtract),
          reads=[("x_st", p), "z_st", ("zs", 0), ("zs", 1), ("zs", 2), ("zs", 3)],
          writes=["z_st", ("zs", 0), ("zs", 1), ("zs", 2), ("zs", 3)])
        A("act", lambda e, c=c: e.dma_start(out=xbf_dram[c * 128:(c + 1) * 128, :], in_=z_st[:, 0:D]),
          reads=["z_st"], writes=["xqch", ("xq_d", c)], dma="Q_t")
    for j in range(nk):
        A("act", lambda e, j=j: e.dma_start_transpose(xq_fm[:, j * t_loc:(j + 1) * t_loc],
                                                      xbf_dram[:, j * 128:(j + 1) * 128]),
          reads=[("xq_d", c) for c in range(ntt)] + K_XQ[j], writes=K_XQ[j], dma="Q_t")

    def psg(t):
        return PB[:, (t % 4) * 512:(t % 4 + 1) * 512] if t < 4 else (PS4, PS5, PS6, PS7)[t - 4][:, :]

    def psk(t):
        return ("PB", t) if t < 4 else ("PS4", "PS5", "PS6", "PS7")[t - 4]

    KW4 = [ark(nk * t_loc + s * 1024, 512) for s in range(4)]
    A("dve", lambda e: e.memset(runmax[:], 0.0), reads=["rms_sc"], writes=["rms_sc"])
    for t in range(ntt):
        A("dve", lambda e, t=t: e.tensor_tensor(xscal[:, 8 + t:9 + t], xscal[:, t:t + 1],
                                                wsraw[:, 0:1], ALU.mult),
          reads=["xscal", "rms_sc"], writes=["xscal"])
        A("dve", lambda e, t=t: e.tensor_tensor(xscal[:, 16 + t:17 + t], xscal[:, t:t + 1],
                                                wsraw[:, 1:2], ALU.mult),
          reads=["xscal", "rms_sc"], writes=["xscal"])
    for ig in range(16):
        for mm, kk in ((0, "qg"), (1, "qu")):
            alln = wqg_all if mm == 0 else wqu_all
            for j in range(nk):
                A("act", lambda e, alln=alln, j=j, ig=ig: e.dma_start_transpose(
                    wff_buf[:, (j % 4) * 1024:(j % 4) * 1024 + 512],
                    alln[ig * 512:(ig + 1) * 512, j * 128:(j + 1) * 128]),
                  reads=[(kk, (ig * 512) // DF8, ((ig * 512) % DF8) // 128 + dtt)
                         for dtt in range(4)] + KW4[j % 4],
                  writes=KW4[j % 4] + [("wff_l", j)], dma="Q_x")
                for t in range(ntt):
                    A("pe", lambda e, j=j, t=t: e.matmul(
                        psg(t), xq_fm[:, j * t_loc + t * 128:j * t_loc + (t + 1) * 128],
                        wff_buf[:, (j % 4) * 1024:(j % 4) * 1024 + 512],
                        start=(j == 0), stop=(j == nk - 1)),
                      reads=K_XQ[j] + KW4[j % 4] + [("wff_l", j)] + ([psk(t)] if j == 0 else []),
                      writes=[psk(t)] if j == nk - 1 else [])
            if mm == 0:
                for t in range(ntt):
                    A("act", lambda e, t=t: e.activation(
                        xs_db[:, :].bitcast(F32)[:, t * 512:(t + 1) * 512], psg(t), AF.Silu,
                        scale=xscal[:, 8 + t:9 + t]),
                      reads=[psk(t), "xs_db_f", "xscal", ("xs_db", 0), ("xs_db", 1)],
                      writes=[psk(t), "xs_db_f", ("xs_db", 0), ("xs_db", 1)])
            else:
                for t in range(ntt):
                    A("dve", lambda e, t=t: e.scalar_tensor_tensor(
                        x_st[:, (t % 2) * D:(t % 2) * D + 512], psg(t), xscal[:, 16 + t:17 + t],
                        xs_db[:, :].bitcast(F32)[:, t * 512:(t + 1) * 512], ALU.mult, ALU.mult),
                      reads=[psk(t), "xs_db_f", "xscal", ("x_st", t % 2)],
                      writes=[("x_st", t % 2), psk(t)])
                    A("dve", lambda e, t=t: e.tensor_reduce(
                        rw1[:], x_st[:, (t % 2) * D:(t % 2) * D + 512], AX.X, ALU.max,
                        apply_absolute_value=True),
                      reads=[("x_st", t % 2), "rw1"], writes=["rw1"])
                    A("dve", lambda e, t=t: e.tensor_tensor(runmax[:, t:t + 1], runmax[:, t:t + 1],
                                                            rw1[:], ALU.max),
                      reads=["rw1", "rms_sc"], writes=["rms_sc"])
                    A("pool", lambda e, t=t, ig=ig: e.dma_start(
                        out=gu_dram[t * 128:(t + 1) * 128, ig * 512:(ig + 1) * 512],
                        in_=x_st[:, (t % 2) * D:(t % 2) * D + 512]),
                      reads=[("x_st", t % 2), ("r1_d2",)], writes=[("guch", t % 4), ("gu_d", ig, t)], dma="Q_t")

    A("dve", lambda e: e.tensor_scalar(runmax[:], runmax[:], 1.0 / 127.0, 1e-5, ALU.mult, ALU.add),
      reads=["rms_sc"], writes=["rms_sc"])
    for t in range(ntt):
        A("dve", lambda e, t=t: e.tensor_tensor(xscal[:, 24 + t:25 + t], runmax[:, t:t + 1],
                                                wsraw[:, 2:3], ALU.mult),
          reads=["xscal", "rms_sc"], writes=["xscal"])
    A("dve", lambda e: e.reciprocal(wqacc[:, 0:8], runmax[:]), reads=["rms_sc", "rms_sc"],
      writes=["rms_sc"])
    for t in range(ntt):
        for ig in range(4):
            p = ig % 2
            A("sp", lambda e, t=t, ig=ig, p=p: e.dma_start(
                out=x_st[:, p * D:(p + 1) * D],
                in_=gu_dram[t * 128:(t + 1) * 128, ig * D:(ig + 1) * D]),
              reads=[("gu_d", gg, t) for gg in range(ig * 4, ig * 4 + 4)] + [("x_st", p)],
              writes=[("x_st", p)], dma="Q_s")
            A("dve", lambda e, t=t, p=p: e.tensor_scalar(x_st[:, p * D:(p + 1) * D],
                                                         x_st[:, p * D:(p + 1) * D],
                                                         wqacc[:, t:t + 1], CBIG, ALU.mult, ALU.add),
              reads=[("x_st", p), "rms_sc"], writes=[("x_st", p)])
            A("dve", lambda e, p=p: e.tensor_scalar(z_st[:, 0:D], x_st[:, p * D:(p + 1) * D],
                                                    CBIG, None, ALU.subtract),
              reads=[("x_st", p), "z_st"], writes=["z_st"])
            A("act", lambda e, t=t, ig=ig: e.dma_start(
                out=guq_dram[t * 128:(t + 1) * 128, ig * D:(ig + 1) * D], in_=z_st[:, 0:D]),
              reads=["z_st"], writes=["guqch", ("guq_d", t, ig)], dma="Q_t")

    GUQD = [("guq_d", t, ig) for t in range(8) for ig in range(4)]
    for qq in range(NQ):
        for j in range(DFF // 128):
            A("act", lambda e, j=j, qq=qq: e.dma_start_transpose(
                guq_fm[:, j * TQ:(j + 1) * TQ],
                guq_dram[qq * TQ:(qq + 1) * TQ, j * 128:(j + 1) * 128]),
              reads=GUQD + K_GQ[j], writes=K_GQ[j], dma="Q_t")
        KWD4 = [ark(nk * t_loc + s * 2048, 2048) for s in range(4)]
        for j in range(DFF // 128):
            A("act", lambda e, j=j: e.dma_start_transpose(
                wff_buf[:, (j % 4) * 2048:(j % 4) * 2048 + 2048],
                wqd_all[0:D, j * 128:(j + 1) * 128]),
              reads=[("qd", (j * 128) // DD8, tt) for tt in range(D // 128)] + KWD4[j % 4],
              writes=KWD4[j % 4] + [("wfd_l", j)], dma="Q_x")
            for t in range(TQ // 128):
                for dgr in range(4):
                    bk = (t * 4 + dgr)
                    A("pe", lambda e, j=j, t=t, dgr=dgr, qq=qq, bk=bk: e.matmul(
                        psg(bk), guq_fm[:, j * TQ + t * 128:j * TQ + (t + 1) * 128],
                        wff_buf[:, (j % 4) * 2048 + dgr * 512:(j % 4) * 2048 + (dgr + 1) * 512],
                        start=(j == 0), stop=(j == DFF // 128 - 1)),
                      reads=K_GQ[j] + KWD4[j % 4] + [("wfd_l", j)] + ([psk(bk)] if j == 0 else []),
                      writes=[psk(bk)] if j == DFF // 128 - 1 else [])
        for t in range(TQ // 128):
            c = qq * (TQ // 128) + t
            p = t % 2
            A("sp", lambda e, c=c, p=p: e.dma_start(out=x_st[:, p * D:(p + 1) * D],
                                                    in_=ln1_dram[c * 128:(c + 1) * 128, :]),
              reads=[("ln1_d", c), ("x_st", p)], writes=[("x_st", p)], dma="Q_s")
            for dgr in range(4):
                bk = t * 4 + dgr
                A("dve", lambda e, t=t, dgr=dgr, p=p, bk=bk, c=c: e.scalar_tensor_tensor(
                    x_st[:, p * D + dgr * 512:p * D + (dgr + 1) * 512],
                    psg(bk), xscal[:, 24 + c:25 + c],
                    x_st[:, p * D + dgr * 512:p * D + (dgr + 1) * 512], ALU.mult, ALU.add),
                  reads=[("x_st", p), psk(bk), "xscal"], writes=[("x_st", p), psk(bk)])
            for bg in range(4):
                A("dve", lambda e, p=p, bg=bg: e.bn_stats(
                    sc_a[:, bg * 6:(bg + 1) * 6], x_st[:, p * D + bg * 512:p * D + (bg + 1) * 512]),
                  reads=[("x_st", p), "scz"], writes=["scz"])
            A("dve", lambda e: e.bn_aggr(sc_b[:, 0:2], sc_a[:, 0:24]),
              reads=["scz", "scz"], writes=["scz"])
            A("dve", lambda e: e.tensor_scalar(rw1[:], sc_b[:, 1:2], 1.0, 1e-5, ALU.mult, ALU.add),
              reads=["scz", "rw1"], writes=["rw1"])
            A("dve", lambda e: e.reciprocal(rw2[:], rw1[:]), reads=["rw1", "rw2"], writes=["rw2"])
            A("act", lambda e: e.activation(rw2[:], rw2[:], AF.Sqrt), reads=["rw2"], writes=["rw2"])
            A("dve", lambda e, p=p: e.tensor_scalar(x_st[:, p * D:(p + 1) * D],
                                                    x_st[:, p * D:(p + 1) * D],
                                                    sc_b[:, 0:1], rw2[:], ALU.subtract, ALU.mult),
              reads=[("x_st", p), "scz", "rw2"], writes=[("x_st", p)])
            A("pool", lambda e, c=c, p=p: e.dma_start(out=out_hbm[c * 128:(c + 1) * 128, :],
                                                      in_=x_st[:, p * D:(p + 1) * D]),
              reads=[("x_st", p)], writes=["outch", ("out_d", c)], dma="Q_o")

    def dmp(name, src, rkeys):
        if name in dbg_out:
            A("pool", lambda e, s=src: e.dma_start(out=dbg_out[name], in_=s), reads=rkeys, writes=["dbgch"], dma="Q_o")

    dmp("dtl", dtl_fm[:], ["dtl"])
    if "xfm0" in dbg_out:
        A("pool", lambda e: e.dma_start(out=dbg_out["xfm0"], in_=x_fm[:, 0:t_loc]),
          reads=K_XFM[0], writes=["dbgch"], dma="Q_o")
    dmp("dt", dt_loc[:], ["dt"])
    dmp("u", u_fm[:], ["u"])
    dmp("aloc", aloc_fm[:], ["aloc"])
    dmp("cum", cum_l[:], ["cum"])
    dmp("ev", ev_fm[:], ["ev"])
    if "xsilu" in dbg_out:
        for c in range(nch):
            A("pool", lambda e, c=c: e.dma_start(out=dbg_out["xsilu"][c * 128:(c + 1) * 128, :],
                                                 in_=xs_dram[c * 128:(c + 1) * 128, :].bitcast(BF16))
              if False else e.dma_start(out=dbg_out["xsilu"][c * 128:(c + 1) * 128, :],
                                        in_=xs_dram[c * 128:(c + 1) * 128, :]),
              reads=[("xs_d", c)], writes=["dbgch"], dma="Q_o")
    if "zdmp" in dbg_out:
        for c in range(nch):
            A("pool", lambda e, c=c: e.dma_start(out=dbg_out["zdmp"][c * 128:(c + 1) * 128, :],
                                                 in_=z_dram[c * 128:(c + 1) * 128, :]),
              reads=[("z_d", c)], writes=["dbgch"], dma="Q_o")
    if "guf" in dbg_out:
        for t in range(ntt):
            A("pool", lambda e, t=t: e.dma_start(out=dbg_out["guf"][t * 128:(t + 1) * 128, :],
                                                 in_=gu_dram[t * 128:(t + 1) * 128, :]),
              reads=[("gu_d", ig, t) for ig in range(16)], writes=["dbgch"], dma="Q_o")
    if "xqd" in dbg_out:
        for c in range(ntt):
            A("pool", lambda e, c=c: e.dma_start(out=dbg_out["xqd"][c * 128:(c + 1) * 128, :],
                                                 in_=xbf_dram[c * 128:(c + 1) * 128, :]),
              reads=[("xq_d", c)], writes=["dbgch"], dma="Q_o")
    if "qrowd" in dbg_out:
        A("pool", lambda e: e.dma_start(out=dbg_out["qrowd"], in_=qrow[0:1, 0:8]),
          reads=["rms_sc"], writes=["dbgch"], dma="Q_o")
    if "ymix" in dbg_out:
        for c in range(nch):
            A("pool", lambda e, c=c: e.dma_start(out=dbg_out["ymix"][c * 128:(c + 1) * 128, :],
                                                 in_=y_dram[c * 128:(c + 1) * 128, :]),
              reads=[("y_d", c)], writes=["dbgch"], dma="Q_o")
    if "ln1" in dbg_out:
        for c in range(ntt):
            A("pool", lambda e, c=c: e.dma_start(out=dbg_out["ln1"][c * 128:(c + 1) * 128, :],
                                                 in_=ln1_dram[c * 128:(c + 1) * 128, :]),
              reads=[("ln1_d", c)], writes=["dbgch"], dma="Q_o")

    sems = {}
    sem_ctx = []
    names = ["S_" + e for e in Emitter.ENGS] + sorted(em.dma_counts.keys())
    for nm in names:
        s = nc.semaphore(nm)
        sem_ctx.append(s)
        sems[nm] = s.__enter__()
    finals = {("S_" + e): c for e, c in em.counts.items() if c}
    finals.update({k: v for k, v in em.dma_counts.items() if v})
    per_eng = {e: [] for e in Emitter.ENGS}
    for (eng, fn, waits, sem, inc) in em.steps:
        per_eng[eng].append((fn, waits, sem, inc))
    blk = nc.Block()
    block = blk.__enter__()

    def runner(e):
        def run(obj):
            for (fn, waits, sem, inc) in per_eng[e]:
                for (s, cc) in waits:
                    obj.wait_ge(sems[s], cc)
                fn(obj).then_inc(sems[sem], inc)
            if e == "pool":
                for s, cc in sorted(finals.items()):
                    obj.wait_ge(sems[s], cc)
        return run

    block.tensor(runner("pe"))
    block.scalar(runner("act"))
    block.vector(runner("dve"))
    block.gpsimd(runner("pool"))
    block.sync(runner("sp"))
    blk.__exit__(None, None, None)
    for s in reversed(sem_ctx):
        s.__exit__(None, None, None)
    for t in reversed(ctx):
        t.__exit__(None, None, None)
    return nc


def make_core_inputs(core, full, t_loc, ncores):
    cps = ncores // 2
    b, slot = core // cps, core % cps
    x = np.asarray(full["L0_attn_x"])
    t0 = slot * t_loc
    d = {}
    d["x_sl"] = np.ascontiguousarray(x[b, t0:t0 + t_loc, :])
    npre = max(cps - 1, 1)
    xp = np.zeros((npre * t_loc, D), np.float32)
    if t0 > 0:
        xp[npre * t_loc - t0:] = x[b, 0:t0, :]
    d["x_pre"] = xp
    if t0 + t_loc < x.shape[1]:
        d["x_ex"] = np.ascontiguousarray(x[b, t0 + t_loc:t0 + t_loc + 1, :])
        bm = 1.0
    else:
        d["x_ex"] = np.zeros((1, D), np.float32)
        bm = 0.0
    d["w_in"] = np.asarray(full["L0_attn_W_in"])
    d["w_out"] = np.asarray(full["L0_attn_W_out"])
    d["wg_hbm"] = np.ascontiguousarray(np.asarray(full["L0_ffn_w_gate"]))
    d["wu_hbm"] = np.ascontiguousarray(np.asarray(full["L0_ffn_w_up"]))
    d["wd_hbm"] = np.ascontiguousarray(np.asarray(full["L0_ffn_w_down"]))
    d["a_log"] = np.asarray(full["L0_attn_A_log"]).reshape(H, 1).astype(np.float32)
    d["dt_bias"] = np.asarray(full["L0_attn_dt_bias"]).reshape(H, 1).astype(np.float32)
    d["bnb_in"] = np.asarray(full["L0_attn_B_norm_bias"]).reshape(1, NS).astype(np.float32)
    d["cnb_in"] = np.asarray(full["L0_attn_C_norm_bias"]).reshape(1, NS).astype(np.float32)
    d["identf_in"] = np.eye(128, dtype=np.float32)
    d["triu_in"] = np.triu(np.ones((128, 128), np.float32))
    d["maskr_in"] = (-1e4 * np.eye(128)).astype(np.float32)
    d["bmask_in"] = np.full((H, 1), bm, np.float32)
    return d


def kernel(**inputs):
    t_loc, ncores = 1024, NC
    nc = build_program(t_loc=t_loc, ncores=ncores)
    in_maps = [make_core_inputs(c, inputs, t_loc, ncores) for c in range(ncores)]
    res = run_bass_kernel_spmd(nc, in_maps, core_ids=list(range(ncores)))
    out = np.zeros((2, 4096, D), np.float32)
    cps = ncores // 2
    for c in range(ncores):
        b, slot = c // cps, c % cps
        out[b, slot * t_loc:(slot + 1) * t_loc, :] = res.results[c]["out"]
    return out


NC = 8
